# version 03
# baseline (speedup 1.0000x reference)
import contextlib
import numpy as np
import ml_dtypes
import concourse.bass as bass
import concourse.mybir as mybir
from concourse.bass_utils import run_bass_kernel_spmd
from concourse.alu_op_type import AluOpType

F32 = mybir.dt.float32
BF16 = mybir.dt.bfloat16
I8 = mybir.dt.int8
AF = mybir.ActivationFunctionType
AX = mybir.AxisListType

T = 2048
D = 2048
NKD = 16
EPS = 1e-5

ENGINES = ("tensor", "vector", "scalar", "gpsimd", "sync")


class Prog:

    def __init__(self, nc, es):
        self.nc = nc
        self.es = es
        self.q = {e: [] for e in ENGINES}
        self.cnt = {e: 0 for e in ENGINES}
        self.sems = {e: es.enter_context(nc.semaphore(f"sem_{e}")) for e in ENGINES}
        self.last_wait = {}
        self.streams = []

    def op(self, eng, fn, inc=0):
        if inc:
            self.cnt[eng] += inc
            sem = self.sems[eng]

            def wrapped(h, fn=fn, sem=sem, inc=inc):
                fn(h).then_inc(sem, inc)

            self.q[eng].append(wrapped)
            return (sem, self.cnt[eng], eng)
        self.q[eng].append(fn)
        return None

    def stream(self, name):
        s = DmaStream(self, name)
        self.streams.append(s)
        return s

    def wait(self, eng, tok):
        if tok is None:
            return
        sem, val, origin = tok
        if origin == eng:
            return
        key = (eng, id(sem))
        if self.last_wait.get(key, 0) >= val:
            return
        self.last_wait[key] = val
        self.q[eng].append(lambda h, sem=sem, val=val: h.wait_ge(sem, val))

    def waits(self, eng, toks):
        for t in toks:
            self.wait(eng, t)

    def milestone(self, eng):
        self.cnt[eng] += 1
        sem = self.sems[eng]
        prev = self.q[eng][-1]

        def wrapped(h, prev=prev, sem=sem):
            inst = prev(h)
            assert inst is not None, "milestone() must follow a real instruction"
            inst.then_inc(sem, 1)

        self.q[eng][-1] = wrapped
        return (sem, self.cnt[eng], eng)

    def finish_streams(self):
        for s in self.streams:
            if s.cnt:
                self.q[s.eng].append(
                    lambda h, sem=s.sem, v=s.cnt: h.wait_ge(sem, v))

    def finalize(self):
        with self.nc.Block() as blk:
            for ename in ENGINES:
                deco = getattr(blk, ename)

                def body(h, ename=ename):
                    for fn in self.q[ename]:
                        fn(h)

                deco(body)


class DmaStream:

    def __init__(self, p, name):
        self.p = p
        self.sem = p.es.enter_context(p.nc.semaphore(f"dma_{name}"))
        self.cnt = 0
        self.eng = None

    def dma(self, eng, out, in_, transpose=False):
        p = self.p
        self.eng = eng
        if self.cnt:
            p.q[eng].append(lambda h, sem=self.sem, v=self.cnt: h.wait_ge(sem, v))
        self.cnt += 16
        sem = self.sem

        def fn(h, out=out, in_=in_, sem=sem, transpose=transpose):
            h.dma_start(out, in_, transpose=transpose).then_inc(sem, 16)

        p.q[eng].append(fn)
        return (sem, self.cnt, None)


class Ctx:
    def __init__(self, nc, es):
        self.nc = nc
        self.es = es
        self.p = Prog(nc, es)
        self.dram = {}
        self.sb = {}

    def dram_in(self, name, shape, dtype):
        self.dram[name] = self.nc.dram_tensor(name, list(shape), dtype,
                                              kind="ExternalInput").ap()
        return self.dram[name]

    def dram_out(self, name, shape, dtype):
        self.dram[name] = self.nc.dram_tensor(name, list(shape), dtype,
                                              kind="ExternalOutput").ap()
        return self.dram[name]

    def dram_tmp(self, name, shape, dtype):
        self.dram[name] = self.nc.dram_tensor(name, list(shape), dtype,
                                              kind="Internal").ap()
        return self.dram[name]

    def sbuf(self, name, shape, dtype):
        t = self.es.enter_context(self.nc.sbuf_tensor(name, list(shape), dtype))
        self.sb[name] = t
        return t

    def psum(self, name, shape=(128, 512), dtype=F32):
        return self.es.enter_context(self.nc.psum_tensor(name, list(shape), dtype))


def emit_consts(c):
    p = c.p
    c.ones_m1 = c.sbuf("ones_m1", [128, 1], BF16)
    c.ones_k1 = c.sbuf("ones_k1", [1, 128], BF16)
    c.cpool = c.sbuf("cpool", [128, 4], F32)
    p.op("gpsimd", lambda h: h.memset(c.ones_m1[:], 1.0))
    p.op("gpsimd", lambda h: h.memset(c.ones_k1[:], 1.0))
    p.op("gpsimd", lambda h: h.memset(c.cpool[:, 0:1], EPS))
    p.op("gpsimd", lambda h: h.memset(c.cpool[:, 1:2], 1.0))
    p.op("gpsimd", lambda h: h.memset(c.cpool[:, 2:3], 0.0))
    p.op("gpsimd", lambda h: h.memset(c.cpool[:, 3:4], -12.0))
    c.const_tok = p.milestone("gpsimd")


def rsqrt_row(c, row_tok, var_row, out_row, n_part=1):
    p = c.p
    p.wait("scalar", row_tok)
    p.wait("scalar", c.const_tok)
    bias = c.cpool[0:n_part, 0:1]
    p.op("scalar", lambda h: h.activation(out_row, var_row, AF.Ln, bias=bias))
    p.op("scalar", lambda h: h.activation(out_row, out_row, AF.Exp, scale=-0.5))
    return p.milestone("scalar")


def bcast_rows(c, pairs, bank, row_tok, free_tok=None, out_free=None):
    p = c.p
    p.wait("tensor", row_tok)
    p.wait("tensor", c.const_tok)
    p.wait("tensor", free_tok)
    p.wait("scalar", out_free)
    ev = None
    for (row, out) in pairs:
        for g in range(T // 512):
            cs = slice(g * 512, (g + 1) * 512)
            p.op("tensor", lambda h, row=row, cs=cs: h.matmul(
                bank[:], c.ones_k1[:], row[0:1, cs], start=True, stop=True))
            mt = p.milestone("tensor")
            p.wait("scalar", mt)
            p.op("scalar", lambda h, out=out, cs=cs: h.copy(out[:, cs], bank[:]))
            ev = p.milestone("scalar")
            p.wait("tensor", ev)
    return ev



def emit_L1(c):
    p = c.p
    xin = c.dram["xin"]
    w_in = c.dram["w_in"]
    ln1w = c.dram["ln1w_col"]
    xres_o = c.dram["xres_o"]
    hscr = c.dram["hscr"]
    zscr = c.dram["zscr"]
    xbcscr = c.dram["xbcscr"]
    dtscr = c.dram["dtscr"]

    rmsh = c.sbuf("rmsh", [128, NKD * T], BF16)
    xsl = [c.sbuf(f"xsl{i}", [128, T], BF16) for i in range(2)]
    sqsl = [c.sbuf(f"sqsl{i}", [128, T], BF16) for i in range(2)]
    mu_row = c.sbuf("mu_row", [1, T], F32)
    var_row = c.sbuf("var_row", [1, T], F32)
    r_row = c.sbuf("r_row", [1, T], F32)
    row_bf = c.sbuf("row_bf", [1, T], BF16)
    mu_b = c.sbuf("mu_b", [128, T], BF16)
    r_b = c.sbuf("r_b", [128, T], BF16)
    rr_b = c.sbuf("rr_b", [128, T], BF16)
    ln1w_sb = c.sbuf("ln1w_sb", [128, NKD], F32)

    bS = [c.psum(f"bS{i}") for i in range(4)]
    bB = c.psum("bB")
    bM = [c.psum(f"bM{i}") for i in range(2)]

    s1t = [bS[0][0:1, :], bS[0][64:65, :], bS[1][0:1, :], bS[1][64:65, :]]
    s2t = [bS[2][0:1, :], bS[2][64:65, :], bS[3][0:1, :], bS[3][64:65, :]]

    s_x = [p.stream("x0"), p.stream("x1")]
    s_st = [p.stream("st0"), p.stream("st1")]
    s_h = [p.stream("h0"), p.stream("h1")]
    s_w = [p.stream("w0"), p.stream("w1")]
    s_stage = [p.stream("sg0"), p.stream("sg1")]
    s_misc = p.stream("misc")
    lw_tok = s_misc.dma("gpsimd", ln1w_sb[:], ln1w[:, :])

    ld = [None] * NKD
    st = [None] * NKD
    mmA = [None] * NKD
    sqv = [None] * NKD
    for k in range(NKD):
        sl = k % 2
        if k >= 2:
            p.wait("gpsimd", st[k - 2])
            p.wait("gpsimd", mmA[k - 2])
            p.wait("gpsimd", sqv[k - 2])
        ld[k] = s_x[sl].dma("gpsimd", xsl[sl][:], xin[k * 128:(k + 1) * 128, :])
        p.wait("sync", ld[k])
        st[k] = s_st[sl].dma("sync", xres_o[k * 128:(k + 1) * 128, :], xsl[sl][:])
        p.wait("vector", ld[k])
        if k >= 2:
            p.wait("vector", mmA[k - 2])
        p.op("vector", lambda h, sl=sl: h.tensor_tensor(
            sqsl[sl][:], xsl[sl][:], xsl[sl][:], AluOpType.mult))
        sqv[k] = p.milestone("vector")
        p.wait("tensor", ld[k])
        p.wait("tensor", sqv[k])
        if k == 0:
            p.wait("tensor", c.const_tok)
        for g in range(4):
            cs = slice(g * 512, (g + 1) * 512)
            p.op("tensor", lambda h, k=k, g=g, sl=sl, cs=cs: h.matmul(
                s1t[g], c.ones_m1[:], xsl[sl][:, cs],
                start=(k == 0), stop=(k == NKD - 1)))
            p.op("tensor", lambda h, k=k, g=g, sl=sl, cs=cs: h.matmul(
                s2t[g], c.ones_m1[:], sqsl[sl][:, cs],
                start=(k == 0), stop=(k == NKD - 1)))
        mmA[k] = p.milestone("tensor")

    p.wait("vector", mmA[NKD - 1])
    s1row = c.sbuf("s1row", [1, T], F32)
    s2row = c.sbuf("s2row", [1, T], F32)
    for g in range(4):
        cs = slice(g * 512, (g + 1) * 512)
        p.op("vector", lambda h, g=g, cs=cs: h.tensor_copy(s1row[0:1, cs], s1t[g]))
        p.op("vector", lambda h, g=g, cs=cs: h.tensor_copy(s2row[0:1, cs], s2t[g]))
    stats_free = p.milestone("vector")
    inv = 1.0 / D
    p.op("vector", lambda h: h.tensor_scalar(mu_row[:], s1row[:], inv, None, AluOpType.mult))
    p.op("vector", lambda h: h.tensor_scalar(var_row[:], s2row[:], inv, None, AluOpType.mult))
    p.op("vector", lambda h: h.tensor_tensor(s1row[:], mu_row[:], mu_row[:], AluOpType.mult))
    p.op("vector", lambda h: h.tensor_tensor(var_row[:], var_row[:], s1row[:], AluOpType.subtract))
    rowv = p.milestone("vector")
    rs = rsqrt_row(c, rowv, var_row[:], r_row[:])
    p.wait("vector", rs)
    p.op("vector", lambda h: h.tensor_copy(row_bf[:], mu_row[:]))
    t1 = p.milestone("vector")
    b1 = bcast_rows(c, [(row_bf[:], mu_b)], bB, t1)
    p.wait("vector", b1)
    p.op("vector", lambda h: h.tensor_copy(row_bf[:], r_row[:]))
    t2 = p.milestone("vector")
    b2 = bcast_rows(c, [(row_bf[:], r_b)], bB, t2, b1)

    mmB = [None] * NKD
    happ = [None] * NKD
    hst = [None] * NKD
    for k in range(NKD):
        sl = k % 2
        p.wait("gpsimd", st[NKD - 2 + sl])
        if k < 2:
            p.wait("gpsimd", mmA[NKD - 2 + k])
            p.wait("gpsimd", sqv[NKD - 2 + k])
        else:
            p.wait("gpsimd", mmB[k - 2])
            p.wait("gpsimd", hst[k - 2])
        ld2 = s_x[sl].dma("gpsimd", xsl[sl][:], xres_o[k * 128:(k + 1) * 128, :])
        p.wait("vector", ld2)
        p.wait("vector", b2)
        if k >= 2:
            p.wait("vector", mmB[k - 2])
        hv = sqsl[sl]
        if k >= 2:
            p.wait("vector", hst[k - 2])
        p.op("vector", lambda h, sl=sl, hv=hv: h.tensor_tensor(
            hv[:], xsl[sl][:], mu_b[:], AluOpType.subtract))
        p.op("vector", lambda h, hv=hv: h.tensor_tensor(
            hv[:], hv[:], r_b[:], AluOpType.mult))
        happ[k] = p.milestone("vector")
        p.wait("sync", happ[k])
        hst[k] = s_h[sl].dma("sync", hscr[k * 128:(k + 1) * 128, :], hv[:])
        p.op("vector", lambda h, sl=sl, hv=hv: h.tensor_tensor(
            xsl[sl][:], hv[:], hv[:], AluOpType.mult))
        sq2 = p.milestone("vector")
        p.wait("tensor", sq2)
        if k == 0:
            p.wait("tensor", stats_free)
        for g in range(4):
            cs = slice(g * 512, (g + 1) * 512)
            p.op("tensor", lambda h, k=k, g=g, sl=sl, cs=cs: h.matmul(
                s1t[g], c.ones_m1[:], xsl[sl][:, cs],
                start=(k == 0), stop=(k == NKD - 1)))
        mmB[k] = p.milestone("tensor")

    p.wait("vector", mmB[NKD - 1])
    for g in range(4):
        cs = slice(g * 512, (g + 1) * 512)
        p.op("vector", lambda h, g=g, cs=cs: h.tensor_copy(s2row[0:1, cs], s1t[g]))
    p.op("vector", lambda h: h.tensor_scalar(var_row[:], s2row[:], inv, None, AluOpType.mult))
    rowv2 = p.milestone("vector")
    rs2 = rsqrt_row(c, rowv2, var_row[:], r_row[:])
    p.wait("vector", rs2)
    p.op("vector", lambda h: h.tensor_copy(row_bf[:], r_row[:]))
    t3 = p.milestone("vector")
    b3 = bcast_rows(c, [(row_bf[:], rr_b)], bB, t3, b2)

    rmsh_done = [None] * NKD
    for k in range(NKD):
        sl = k % 2
        if k >= 2:
            p.wait("gpsimd", rmsh_done[k - 2])
        else:
            p.wait("gpsimd", mmB[NKD - 2 + k])
        p.wait("gpsimd", hst[k])
        ld3 = s_x[sl].dma("gpsimd", xsl[sl][:], hscr[k * 128:(k + 1) * 128, :])
        if k < 2:
            p.wait("vector", hst[NKD - 2 + k])
        p.waits("vector", [ld3, b3, lw_tok])
        rv = rmsh[:, k * T:(k + 1) * T]
        p.op("vector", lambda h, sl=sl, k=k: h.tensor_scalar(
            sqsl[sl][:], xsl[sl][:], ln1w_sb[:, k:k + 1], None, AluOpType.mult))
        p.op("vector", lambda h, sl=sl, rv=rv: h.tensor_tensor(
            rv, sqsl[sl][:], rr_b[:], AluOpType.mult))
        rmsh_done[k] = p.milestone("vector")

    wbuf = [c.sbuf(f"wbuf{i}", [128, NKD * 512], BF16) for i in range(2)]
    stage = c.sbuf("stage", [128, 16 * 512], BF16)
    wtok = [None, None]
    wfree = [None, None]
    pb_free = [None, None]
    stage_free = [None, None]
    rmsh_all = rmsh_done[NKD - 1]

    def load_wblk(b):
        sl = b % 2
        p.wait("gpsimd", wfree[sl])
        src = w_in[:, b * 512:(b + 1) * 512].rearrange("(k p) m -> p k m", p=128)
        dst = wbuf[sl][:, :].rearrange("p (k m) -> p k m", k=NKD)
        wtok[sl] = s_w[sl].dma("gpsimd", dst, src)

    load_wblk(0)
    load_wblk(1)
    for b in range(18):
        sl = b % 2
        p.wait("tensor", wtok[sl])
        p.wait("tensor", rmsh_all)
        if b < 8:
            for half in range(2):
                soff = half * 4096
                p.wait("scalar", stage_free[half])
                for mi in range(8):
                    m = half * 8 + mi
                    pb = bM[m % 2]
                    p.wait("tensor", pb_free[m % 2])
                    for k in range(NKD):
                        p.op("tensor", lambda h, m=m, k=k, sl=sl, pb=pb: h.matmul(
                            pb[:], rmsh[:, k * T + m * 128: k * T + (m + 1) * 128],
                            wbuf[sl][:, k * 512:(k + 1) * 512],
                            start=(k == 0), stop=(k == NKD - 1)))
                    mt = p.milestone("tensor")
                    p.wait("scalar", mt)
                    p.op("scalar", lambda h, mi=mi, pb=pb, soff=soff: h.copy(
                        stage[:, soff + mi * 512:soff + (mi + 1) * 512], pb[:]))
                    pb_free[m % 2] = p.milestone("scalar")
                p.wait("sync", pb_free[0])
                p.wait("sync", pb_free[1])
                dst = zscr[half * 1024:(half + 1) * 1024,
                           b * 512:(b + 1) * 512].rearrange("(m p) z -> p m z", p=128)
                ssrc = stage[:, soff:soff + 4096].rearrange("p (m z) -> p m z", m=8)
                stok = s_stage[half].dma("sync", dst, ssrc)
                stage_free[half] = stok
        else:
            xb = b - 8
            for mt_i in range(4):
                half = mt_i % 2
                soff = half * 4096
                p.wait("scalar", stage_free[half])
                for g in range(4):
                    idx = mt_i * 4 + g
                    pb = bM[idx % 2]
                    p.wait("tensor", pb_free[idx % 2])
                    for k in range(NKD):
                        p.op("tensor", lambda h, mt_i=mt_i, g=g, k=k, sl=sl, pb=pb:
                             h.matmul(pb[:],
                                      wbuf[sl][:, k * 512 + mt_i * 128:
                                               k * 512 + (mt_i + 1) * 128],
                                      rmsh[:, k * T + g * 512: k * T + (g + 1) * 512],
                                      start=(k == 0), stop=(k == NKD - 1)))
                    mt = p.milestone("tensor")
                    p.wait("scalar", mt)
                    p.op("scalar", lambda h, g=g, pb=pb, soff=soff: h.copy(
                        stage[:, soff + g * 512:soff + (g + 1) * 512], pb[:]))
                    pb_free[idx % 2] = p.milestone("scalar")
                p.wait("sync", pb_free[0])
                p.wait("sync", pb_free[1])
                r0 = xb * 512 + mt_i * 128
                dst = xbcscr[r0:r0 + 128, 3:3 + T]
                stok = s_stage[half].dma("sync", dst, stage[:, soff:soff + T])
                stage_free[half] = stok
        wfree[sl] = mt
        if b + 2 < 18:
            load_wblk(b + 2)

    wdt = c.sbuf("wdt", [128, NKD * 64], BF16)
    dtst = c.sbuf("dtst", [128, 16 * 64], F32)
    src = w_in[:, 9216:9280].rearrange("(k p) m -> p k m", p=128)
    wdst = wdt[:, :].rearrange("p (k m) -> p k m", k=NKD)
    dt_ld = s_misc.dma("gpsimd", wdst, src)
    p.wait("tensor", dt_ld)
    for m in range(16):
        pb = bM[m % 2]
        p.wait("tensor", pb_free[m % 2])
        for k in range(NKD):
            p.op("tensor", lambda h, m=m, k=k, pb=pb: h.matmul(
                pb[:, 0:64], rmsh[:, k * T + m * 128: k * T + (m + 1) * 128],
                wdt[:, k * 64:(k + 1) * 64],
                start=(k == 0), stop=(k == NKD - 1)))
        mt = p.milestone("tensor")
        p.wait("scalar", mt)
        p.op("scalar", lambda h, m=m, pb=pb: h.copy(
            dtst[:, m * 64:(m + 1) * 64], pb[:, 0:64]))
        pb_free[m % 2] = p.milestone("scalar")
    p.wait("sync", pb_free[0])
    p.wait("sync", pb_free[1])
    p.stream("dtout").dma("sync", dtscr.rearrange("(m p) hh -> p m hh", p=128),
          dtst[:, :].rearrange("p (m hh) -> p m hh", m=16))

    zhalo = c.sbuf("zhalo", [128, 40 * 3], BF16)
    htok = p.op("gpsimd", lambda h: h.memset(zhalo[:], 0.0), inc=1)
    p.wait("sync", htok)
    p.stream("halo").dma("sync", xbcscr[:, 0:3].rearrange("(ct p) t -> p ct t", p=128),
          zhalo[:, :].rearrange("p (ct t) -> p ct t", ct=40))



def emit_L2(c):
    p = c.p
    xbcscr = c.dram["xbcscr"]
    xbcp = c.dram["xbcp"]
    zscr = c.dram["zscr"]
    zsil = c.dram["zsil"]
    convw = c.dram["convw_sb"]
    convb = c.dram["convb_sb"]

    cw = c.sbuf("cw", [128, 160], F32)
    cb = c.sbuf("cb", [128, 40], F32)
    insl = [c.sbuf(f"cin{i}", [128, T + 3], BF16) for i in range(2)]
    outsl = [c.sbuf(f"cout{i}", [128, T], BF16) for i in range(2)]
    accs = [c.sbuf(f"cacc{i}", [128, T], F32) for i in range(2)]
    ctmp = c.sbuf("ctmp", [128, T], F32)
    zslb = [c.sbuf(f"zsl{i}", [128, 4096], BF16) for i in range(2)]

    s_cw = p.stream("cw")
    s_in = [p.stream("ci0"), p.stream("ci1")]
    s_out = [p.stream("co0"), p.stream("co1")]
    s_zi = [p.stream("zi0"), p.stream("zi1")]
    s_zo = [p.stream("zo0"), p.stream("zo1")]

    cw_tok = s_cw.dma("gpsimd", cw[:], convw[:, :])
    cb_tok = s_cw.dma("gpsimd", cb[:], convb[:, :])

    vtok = [None, None]
    stok = [None, None]
    sctok = [None, None]
    for ct in range(40):
        sl = ct % 2
        p.wait("gpsimd", stok[sl])
        p.wait("gpsimd", vtok[sl])
        ld = s_in[sl].dma("gpsimd", insl[sl][:],
                          xbcscr[ct * 128:(ct + 1) * 128, :])
        p.waits("vector", [ld, cw_tok, cb_tok])
        p.wait("vector", sctok[sl])
        acc = accs[sl]
        p.op("vector", lambda h, sl=sl, ct=ct, acc=acc: h.tensor_scalar(
            acc[:], insl[sl][:, 0:T], cw[:, ct * 4:ct * 4 + 1], None, AluOpType.mult))
        for j in range(1, 4):
            p.op("vector", lambda h, sl=sl, ct=ct, j=j: h.tensor_scalar(
                ctmp[:], insl[sl][:, j:j + T], cw[:, ct * 4 + j:ct * 4 + j + 1],
                None, AluOpType.mult))
            p.op("vector", lambda h, acc=acc: h.tensor_tensor(
                acc[:], acc[:], ctmp[:], AluOpType.add))
        vtok[sl] = p.milestone("vector")
        p.wait("scalar", vtok[sl])
        p.wait("scalar", stok[sl])
        p.op("scalar", lambda h, sl=sl, ct=ct, acc=acc: h.activation(
            outsl[sl][:], acc[:], AF.Silu, bias=cb[:, ct:ct + 1]))
        sctok[sl] = p.milestone("scalar")
        p.wait("sync", sctok[sl])
        stok[sl] = s_out[sl].dma("sync", xbcp[ct * 128:(ct + 1) * 128, :], outsl[sl][:])

    zvt = [None, None]
    zst = [None, None]
    for m in range(16):
        sl = m % 2
        p.wait("gpsimd", zst[sl])
        p.wait("gpsimd", zvt[sl])
        ld = s_zi[sl].dma("gpsimd", zslb[sl][:], zscr[m * 128:(m + 1) * 128, :])
        p.wait("scalar", ld)
        p.op("scalar", lambda h, sl=sl: h.activation(
            zslb[sl][:], zslb[sl][:], AF.Silu))
        zvt[sl] = p.milestone("scalar")
        p.wait("sync", zvt[sl])
        zst[sl] = s_zo[sl].dma("sync", zsil[m * 128:(m + 1) * 128, :], zslb[sl][:])



CH = 128
NC_ = T // CH
NH = 64
HD = 64


def emit_L3(c):
    p = c.p
    xbcp = c.dram["xbcp"]
    zsil = c.dram["zsil"]
    dtscr = c.dram["dtscr"]
    vscr = c.dram["vscr"]
    rstdscr = c.dram["rstdscr"]
    tmask = c.dram["tmask_f"]
    masklT = c.dram["masklT"]
    ident_bf = c.dram["ident_bf"]
    ident_f = c.dram["ident_f32"]
    rows = c.dram["hrows"]

    tmsk = c.sbuf("tmsk", [128, 128], F32)
    mskl = c.sbuf("mskl", [128, 128], F32)
    idb = c.sbuf("idb", [128, 128], BF16)
    idf = c.sbuf("idf", [128, 128], F32)
    hrows = c.sbuf("hrows_sb", [1, 192], F32)
    dtr = c.sbuf("dtr", [128, NC_ * 64], F32)
    dtsp = c.sbuf("dtsp", [128, NC_ * 64], F32)
    t1 = c.sbuf("t1g", [128, NC_ * 64], F32)
    t2 = c.sbuf("t2g", [128, NC_ * 64], F32)
    acum = c.sbuf("acum", [128, NC_ * 64], F32)
    nacum = c.sbuf("nacum", [128, NC_ * 64], F32)
    dste = c.sbuf("dste", [128, NC_ * 64], F32)
    outd = c.sbuf("outd", [128, NC_ * 64], F32)
    comb = c.sbuf("comb", [128, NC_ * 64], F32)
    ace = c.sbuf("ace", [1, NC_ * 64], F32)
    acer = c.sbuf("acer", [128, NC_ * 64], F32)
    cdh = c.sbuf("cdh", [128, NC_ * 64], F32)
    dtb_rep = c.sbuf("dtb_rep", [128, 64], F32)
    negA = c.sbuf("negA", [128, 64], F32)
    d_rep = c.sbuf("d_rep", [128, 64], F32)
    prev = c.sbuf("prev", [64, 4096], F32)
    prevbf = c.sbuf("prevbf", [64, 4096], BF16)
    rstdt = c.sbuf("rstdt", [128, NC_], F32)
    vsq = c.sbuf("vsq", [128, 1], F32)
    vpad = c.sbuf("vpad", [128, 1], F32)

    sxbc = [c.sbuf(f"sxbc{i}", [128, 4 * 128], BF16) for i in range(2)]
    sB = [c.sbuf(f"sB{i}", [64, 8 * 128], BF16) for i in range(2)]
    sC = [c.sbuf(f"sC{i}", [64, 64 * 128], BF16) for i in range(2)]
    szs = [c.sbuf(f"szs{i}", [128, 4096], BF16) for i in range(2)]
    vout = [c.sbuf(f"vout{i}", [128, 4096], BF16) for i in range(2)]
    xtm = c.sbuf("xtm", [128, 512], BF16)
    btm = c.sbuf("btm", [128, 512], BF16)
    xd8 = c.sbuf("xd8", [128, 512], BF16)
    dxd8 = c.sbuf("dxd8", [128, 512], BF16)
    dx8 = c.sbuf("dx8", [128, 512], BF16)
    yo8 = c.sbuf("yo8", [128, 512], F32)
    mrg = c.sbuf("mrg", [128, 512], F32)
    sq8 = c.sbuf("sq8", [128, 4096], BF16)
    lts = [c.sbuf(f"lts{i}", [128, 128], F32) for i in range(2)]
    ltb = [c.sbuf(f"ltb{i}", [128, 128], BF16) for i in range(2)]
    gl = c.sbuf("gl", [128, 8 * 128], BF16)

    b0 = c.psum("s_b0")
    b1 = c.psum("s_b1")
    bt = c.psum("s_bt", (128, 1024), BF16)
    bg = [c.psum("s_bg0"), c.psum("s_bg1")]
    bl = [c.psum("s_bl0"), c.psum("s_bl1")]
    bs = c.psum("s_bs")

    s_ld = p.stream("l3w")
    s_x = [p.stream("l3x0"), p.stream("l3x1")]
    s_z = [p.stream("l3z0"), p.stream("l3z1")]
    s_v = [p.stream("l3v0"), p.stream("l3v1")]

    for (dst, src) in ((tmsk, tmask), (mskl, masklT), (idb, ident_bf),
                       (idf, ident_f), (hrows, rows)):
        ldt = s_ld.dma("gpsimd", dst[:], src[:, :])
    ld_dt = s_ld.dma("gpsimd",
                     dtr[:, :].rearrange("p (m hh) -> p m hh", m=NC_),
                     dtscr.rearrange("(m p) hh -> p m hh", p=128))
    p.op("gpsimd", lambda h: h.memset(prev[:], 0.0))
    pz_tok = p.milestone("gpsimd")

    p.waits("tensor", [ldt, c.const_tok])
    for (row_i, dst) in ((0, dtb_rep), (1, negA), (2, d_rep)):
        p.op("tensor", lambda h, row_i=row_i: h.matmul(
            b0[:, 0:64], c.ones_k1f[:], hrows[0:1, row_i * 64:(row_i + 1) * 64],
            start=True, stop=True))
        mt = p.milestone("tensor")
        p.wait("scalar", mt)
        p.op("scalar", lambda h, dst=dst: h.copy(dst[:], b0[:, 0:64]))
        ev = p.milestone("scalar")
        p.wait("tensor", ev)
    p.op("scalar", lambda h: h.activation(negA[:], negA[:], AF.Exp))
    sc0 = p.milestone("scalar")
    p.wait("vector", sc0)
    p.op("vector", lambda h: h.tensor_scalar(negA[:], negA[:], -1.0, None, AluOpType.mult))
    p.wait("vector", ld_dt)
    dtbB = dtb_rep[:, :].unsqueeze(1).broadcast_to([128, NC_, 64])
    p.op("vector", lambda h: h.tensor_tensor(
        t1[:, :].rearrange("p (m hh) -> p m hh", m=NC_),
        dtr[:, :].rearrange("p (m hh) -> p m hh", m=NC_), dtbB, AluOpType.add))
    vt1 = p.milestone("vector")
    p.wait("scalar", vt1)
    p.op("scalar", lambda h: h.activation(t2[:], t1[:], AF.Abs))
    p.op("scalar", lambda h: h.activation(t2[:], t2[:], AF.Exp, scale=-1.0))
    p.op("scalar", lambda h: h.activation(
        t2[:], t2[:], AF.Ln, bias=c.cpool[0:128, 1:2]))
    sc1 = p.milestone("scalar")
    p.wait("vector", sc1)
    p.op("vector", lambda h: h.tensor_scalar(
        dtsp[:], t1[:], 0.0, None, AluOpType.max))
    p.op("vector", lambda h: h.tensor_tensor(dtsp[:], dtsp[:], t2[:], AluOpType.add))
    negAB = negA[:, :].unsqueeze(1).broadcast_to([128, NC_, 64])
    p.op("vector", lambda h: h.tensor_tensor(
        t1[:, :].rearrange("p (m hh) -> p m hh", m=NC_),
        dtsp[:, :].rearrange("p (m hh) -> p m hh", m=NC_), negAB, AluOpType.mult))
    da_tok = p.milestone("vector")
    acum_toks = []
    p.wait("tensor", da_tok)
    for cc in range(NC_):
        csl = slice(cc * 64, (cc + 1) * 64)
        p.op("tensor", lambda h, csl=csl: h.matmul(
            b0[:, 0:64], tmsk[:], t1[:, csl], start=True, stop=True))
        p.op("tensor", lambda h, csl=csl: h.matmul(
            b1[0:1, 0:64], c.ones_m1f[:], t1[:, csl], start=True, stop=True))
        mt = p.milestone("tensor")
        p.wait("scalar", mt)
        ta = p.op("scalar", lambda h, csl=csl: h.copy(acum[:, csl], b0[:, 0:64]), inc=1)
        ev = p.op("scalar", lambda h, csl=csl: h.copy(ace[0:1, csl], b1[0:1, 0:64]), inc=1)
        p.wait("tensor", ev)
        p.wait("gpsimd", ta)
        acum_toks.append(ta)
    evl = None
    for g2 in range(2):
        cs = slice(g2 * 512, (g2 + 1) * 512)
        p.op("tensor", lambda h, cs=cs: h.matmul(
            b0[:], c.ones_k1f[:], ace[0:1, cs], start=True, stop=True))
        mt = p.milestone("tensor")
        p.wait("scalar", mt)
        p.op("scalar", lambda h, cs=cs: h.copy(acer[:, cs], b0[:]))
        evl = p.milestone("scalar")
        p.wait("tensor", evl)
    p.wait("vector", evl)
    t_nacum = p.op("vector", lambda h: h.tensor_scalar(
        nacum[:], acum[:], -1.0, None, AluOpType.mult), inc=1)
    t_dsub = p.op("vector", lambda h: h.tensor_tensor(
        dste[:], acer[:], acum[:], AluOpType.subtract), inc=1)
    p.wait("scalar", t_dsub)
    t_dste = p.op("scalar", lambda h: h.activation(dste[:], dste[:], AF.Exp), inc=1)
    t_outd = p.op("scalar", lambda h: h.activation(outd[:], acum[:], AF.Exp), inc=1)
    t_cdh = p.op("scalar", lambda h: h.activation(cdh[:], acer[:], AF.Exp), inc=1)
    p.wait("vector", t_dste)
    pro_tok = p.op("vector", lambda h: h.tensor_tensor(
        comb[:], dtsp[:], dste[:], AluOpType.mult), inc=1)
    p.wait("gpsimd", t_nacum)
    p.wait("gpsimd", t_outd)
    p.wait("gpsimd", t_cdh)

    xtok = [None, None]
    ztok = [None, None]
    vtok = [None, None]
    lts_free = [None, None]
    ltb_free = [None, None]
    pY_free = None
    pYo_free = None
    bt_free = None
    bs_free = None
    bg_free = [None, None]
    bl_free = [None, None]
    for cc in range(NC_):
        sl = cc % 2
        p.wait("gpsimd", xtok[sl])
        ldx = s_x[sl].dma("gpsimd", sxbc[sl][:, :].rearrange(
            "p (ct t) -> p ct t", ct=4),
            xbcp[0:512, cc * 128:(cc + 1) * 128].rearrange("(ct p) t -> p ct t", p=128))
        ldb = s_x[sl].dma("gpsimd", sB[sl][:, :].rearrange("n (g t) -> n g t", g=8),
            xbcp[512:1024, cc * 128:(cc + 1) * 128].rearrange("(g n) t -> n g t", n=64))
        ldc = s_x[sl].dma("gpsimd", sC[sl][:, :].rearrange("n (hd t) -> n hd t", hd=64),
            xbcp[1024:5120, cc * 128:(cc + 1) * 128].rearrange("(hd n) t -> n hd t", n=64))
        p.wait("gpsimd", ztok[sl])
        ldz = s_z[sl].dma("gpsimd", szs[sl][:], zsil[cc * 128:(cc + 1) * 128, :])
        p.waits("tensor", [ldx, ldb, ldc, pz_tok, pro_tok, bt_free])
        p.waits("vector", [t_outd, t_cdh, pro_tok])
        p.waits("scalar", [t_nacum])
        for i in range(4):
            p.op("tensor", lambda h, sl=sl, i=i: h.transpose(
                bt[:, i * 128:(i + 1) * 128],
                sxbc[sl][:, i * 128:(i + 1) * 128], idb[:]))
        for g in range(8):
            p.op("tensor", lambda h, sl=sl, g=g: h.transpose(
                bt[:, 512 + g * 64:512 + (g + 1) * 64],
                sB[sl][:, g * 128:(g + 1) * 128], idb[0:64, 0:64]))
        tpt = p.milestone("tensor")
        p.wait("vector", tpt)
        for i in range(4):
            p.op("vector", lambda h, i=i: h.tensor_copy(
                xtm[:, i * 128:(i + 1) * 128], bt[:, i * 128:(i + 1) * 128]))
        btm_tok = p.op("vector", lambda h: h.tensor_copy(btm[:], bt[:, 512:1024]),
                       inc=1)
        bt_free = btm_tok
        p.op("vector", lambda h: h.tensor_copy(prevbf[:], prev[:]))
        pbf_tok = p.milestone("vector")

        vo = vout[sl]
        p.wait("vector", vtok[sl])
        p.wait("vector", pz_tok)
        for g in range(8):
            hs0 = cc * 64 + g * 8
            xg = xtm[:, g * 64:(g + 1) * 64]
            xgB = xg.unsqueeze(1).broadcast_to([128, 8, 64])
            dtB = dtsp[:, hs0:hs0 + 8].unsqueeze(2).broadcast_to([128, 8, 64])
            cbB = comb[:, hs0:hs0 + 8].unsqueeze(2).broadcast_to([128, 8, 64])
            ddB = d_rep[:, g * 8:g * 8 + 8].unsqueeze(2).broadcast_to([128, 8, 64])
            odB = outd[:, hs0:hs0 + 8].unsqueeze(2).broadcast_to([128, 8, 64])
            cdB64 = cdh[0:64, hs0:hs0 + 8].unsqueeze(2).broadcast_to([64, 8, 64])
            r3 = lambda ap: ap.rearrange("p (r i) -> p r i", r=8)
            xd8_tok = p.op("vector", lambda h, xgB=xgB, dtB=dtB: h.tensor_tensor(
                r3(xd8[:, :]), xgB, dtB, AluOpType.mult), inc=1)
            dxd_tok = p.op("vector", lambda h, xgB=xgB, cbB=cbB: h.tensor_tensor(
                r3(dxd8[:, :]), xgB, cbB, AluOpType.mult), inc=1)
            xd_tok = p.op("vector", lambda h, xgB=xgB, ddB=ddB: h.tensor_tensor(
                r3(dx8[:, :]), xgB, ddB, AluOpType.mult), inc=1)
            for hh in range(8):
                hd = g * 8 + hh
                bgs = bg[hh % 2]
                p.wait("tensor", bg_free[hh % 2])
                p.op("tensor", lambda h, sl=sl, g=g, hd=hd, bgs=bgs: h.matmul(
                    bgs[:, 0:128],
                    sB[sl][:, g * 128:(g + 1) * 128],
                    sC[sl][:, hd * 128:(hd + 1) * 128],
                    start=True, stop=True))
                gmt = p.milestone("tensor")
                lsl = hh % 2
                p.wait("gpsimd", lts_free[lsl])
                p.wait("gpsimd", pro_tok)
                col = cc * 64 + hd
                p.op("gpsimd", lambda h, lsl=lsl, col=col: h.tensor_scalar(
                    lts[lsl][:], mskl[:], acum[:, col:col + 1], 1.0,
                    AluOpType.add, AluOpType.mult))
                lt1 = p.milestone("gpsimd")
                p.waits("tensor", [lt1, bl_free[lsl]])
                p.op("tensor", lambda h, lsl=lsl: h.transpose(
                    bl[lsl][:, 0:128], lts[lsl][:], idf[:]))
                ltt = p.milestone("tensor")
                lts_free[lsl] = ltt
                p.wait("scalar", ltt)
                p.wait("scalar", ltb_free[lsl])
                p.op("scalar", lambda h, lsl=lsl, col=col: h.activation(
                    ltb[lsl][:], bl[lsl][:, 0:128], AF.Exp,
                    bias=nacum[:, col:col + 1]))
                lte = p.milestone("scalar")
                bl_free[lsl] = lte
                p.waits("vector", [gmt, lte])
                p.op("vector", lambda h, hh=hh, bgs=bgs, lsl=lsl: h.tensor_tensor(
                    gl[:, hh * 128:(hh + 1) * 128], bgs[:, 0:128], ltb[lsl][:],
                    AluOpType.mult))
                gltok = p.milestone("vector")
                bg_free[hh % 2] = gltok
                ltb_free[lsl] = gltok
            p.waits("tensor", [xd8_tok, dxd_tok, bs_free, btm_tok])
            p.op("tensor", lambda h, g=g: h.matmul(
                bs[0:64, :], btm[:, g * 64:(g + 1) * 64], dxd8[:],
                start=True, stop=True))
            smt = p.milestone("tensor")
            p.waits("tensor", [pY_free, pYo_free, pbf_tok, gltok, xd8_tok])
            for hh in range(8):
                hd = g * 8 + hh
                p.op("tensor", lambda h, hh=hh: h.matmul(
                    b0[:, hh * 64:(hh + 1) * 64], gl[:, hh * 128:(hh + 1) * 128],
                    xd8[:, hh * 64:(hh + 1) * 64], start=True, stop=True))
                p.op("tensor", lambda h, sl=sl, hd=hd, hh=hh: h.matmul(
                    b1[:, hh * 64:(hh + 1) * 64],
                    sC[sl][:, hd * 128:(hd + 1) * 128],
                    prevbf[0:64, hd * 64:(hd + 1) * 64], start=True, stop=True))
            ymt = p.milestone("tensor")
            p.waits("vector", [ymt, ldz])
            p.op("vector", lambda h, odB=odB: h.tensor_tensor(
                r3(yo8[:, :]), b1[:, :].rearrange("p (r i) -> p r i", r=8), odB,
                AluOpType.mult))
            p.op("vector", lambda h: h.tensor_tensor(
                mrg[:], b0[:], yo8[:], AluOpType.add))
            pY_free = p.milestone("vector")
            pYo_free = pY_free
            p.op("vector", lambda h: h.tensor_tensor(
                mrg[:], mrg[:], dx8[:], AluOpType.add))
            p.op("vector", lambda h, sl=sl, g=g, vo=vo: h.tensor_tensor(
                vo[:, g * 512:(g + 1) * 512], mrg[:],
                szs[sl][:, g * 512:(g + 1) * 512], AluOpType.mult))
            vmg = p.milestone("vector")
            p.op("vector", lambda h, g=g, cdB64=cdB64: h.tensor_tensor(
                prev[0:64, g * 512:(g + 1) * 512].rearrange("p (r i) -> p r i", r=8),
                prev[0:64, g * 512:(g + 1) * 512].rearrange("p (r i) -> p r i", r=8),
                cdB64, AluOpType.mult))
            p.wait("vector", smt)
            p.op("vector", lambda h, g=g: h.tensor_tensor(
                prev[:, g * 512:(g + 1) * 512], prev[:, g * 512:(g + 1) * 512],
                bs[0:64, :], AluOpType.add))
            bs_free = p.milestone("vector")
        p.op("vector", lambda h, vo=vo: h.tensor_tensor(
            sq8[:], vo[:], vo[:], AluOpType.mult))
        p.op("vector", lambda h: h.tensor_reduce(
            vsq[:], sq8[:], axis=AX.X, op=AluOpType.add))
        p.op("vector", lambda h: h.tensor_copy(vpad[:], vsq[:]))
        rt = p.milestone("vector")
        p.wait("scalar", rt)
        p.op("scalar", lambda h, cc=cc: h.activation(
            rstdt[:, cc:cc + 1], vsq[:], AF.Ln, bias=c.cpool[0:128, 0:1],
            scale=1.0 / 4096.0))
        p.op("scalar", lambda h, cc=cc: h.activation(
            rstdt[:, cc:cc + 1], rstdt[:, cc:cc + 1], AF.Exp, scale=-0.5))
        p.wait("sync", vmg)
        vtok[sl] = s_v[sl].dma("sync", vscr[cc * 128:(cc + 1) * 128, :], vo[:])
        xtok[sl] = ymt
        ztok[sl] = vmg
    rfin = p.milestone("scalar")
    p.waits("tensor", [rfin, bl_free[0]])
    p.op("tensor", lambda h: h.transpose(bl[0][0:16, 0:128], rstdt[:], idf[:]))
    rtt = p.milestone("tensor")
    rrow = c.sbuf("rrow", [16, 128], F32)
    p.wait("scalar", rtt)
    p.op("scalar", lambda h: h.copy(rrow[:], bl[0][0:16, 0:128]))
    rev = p.milestone("scalar")
    p.wait("sync", rev)
    p.stream("rstd").dma("sync", rstdscr.rearrange("(c p) -> c p", p=128), rrow[:])



def colstats(c, nk, get_tile, banks, sq_slots, want_mean, s1row, s2row, free_tok=None):
    p = c.p
    s1t = [banks[0][0:1, :], banks[0][64:65, :], banks[1][0:1, :], banks[1][64:65, :]]
    s2t = [banks[2][0:1, :], banks[2][64:65, :], banks[3][0:1, :], banks[3][64:65, :]]
    sq_tok = [None, None]
    mm_tok = None
    p.wait("tensor", free_tok)
    for k in range(nk):
        src, toks = get_tile(k)
        sl = k % 2
        sq = sq_slots[sl]
        p.waits("vector", toks)
        p.wait("vector", sq_tok[sl])
        p.op("vector", lambda h, sq=sq, src=src: h.tensor_tensor(
            sq[:], src, src, AluOpType.mult))
        vtok = p.milestone("vector")
        p.waits("tensor", toks)
        p.wait("tensor", vtok)
        for g in range(4):
            cs = slice(g * 512, (g + 1) * 512)
            if want_mean:
                p.op("tensor", lambda h, k=k, g=g, src=src, cs=cs: h.matmul(
                    s1t[g], c.ones_m1[:], src[:, cs],
                    start=(k == 0), stop=(k == nk - 1)))
            p.op("tensor", lambda h, k=k, g=g, sq=sq, cs=cs: h.matmul(
                s2t[g], c.ones_m1[:], sq[:, cs],
                start=(k == 0), stop=(k == nk - 1)))
        sq_tok[sl] = p.milestone("tensor")
    mm_tok = sq_tok[(nk - 1) % 2]
    p.wait("vector", mm_tok)
    for g in range(4):
        cs = slice(g * 512, (g + 1) * 512)
        if want_mean:
            p.op("vector", lambda h, g=g, cs=cs: h.tensor_copy(s1row[0:1, cs], s1t[g]))
        p.op("vector", lambda h, g=g, cs=cs: h.tensor_copy(s2row[0:1, cs], s2t[g]))
    ev = p.milestone("vector")
    return mm_tok, ev


def ln_mu(c, R1, row_bf, dd, ev_tok):
    p = c.p
    p.wait("vector", ev_tok)
    p.op("vector", lambda h: h.tensor_scalar(R1[:], R1[:], 1.0 / dd, None, AluOpType.mult))
    p.op("vector", lambda h: h.tensor_copy(row_bf[:], R1[:]))
    return p.milestone("vector")


def ln_r(c, R1, R2, row_bf, dd, mu_done):
    p = c.p
    p.wait("vector", mu_done)
    p.op("vector", lambda h: h.tensor_scalar(R2[:], R2[:], 1.0 / dd, None, AluOpType.mult))
    p.op("vector", lambda h: h.tensor_tensor(R1[:], R1[:], R1[:], AluOpType.mult))
    p.op("vector", lambda h: h.tensor_tensor(R2[:], R2[:], R1[:], AluOpType.subtract))
    rowv = p.milestone("vector")
    rs = rsqrt_row(c, rowv, R2[:], R2[:])
    p.wait("vector", rs)
    p.op("vector", lambda h: h.tensor_copy(row_bf[:], R2[:]))
    return p.milestone("vector")


def rms_r(c, R2, row_bf, dd, ev_tok):
    p = c.p
    p.wait("vector", ev_tok)
    p.op("vector", lambda h: h.tensor_scalar(R2[:], R2[:], 1.0 / dd, None, AluOpType.mult))
    rowv = p.milestone("vector")
    rs = rsqrt_row(c, rowv, R2[:], R2[:])
    p.wait("vector", rs)
    p.op("vector", lambda h: h.tensor_copy(row_bf[:], R2[:]))
    return p.milestone("vector")



def emit_L4(c):
    p = c.p
    vscr = c.dram["vscr"]
    rstdscr = c.dram["rstdscr"]
    hscr = c.dram["hscr"]
    xres_o = c.dram["xres_o"]
    w_out = c.dram["w_out"]
    normw = c.dram["normw_col"]
    ln2w = c.dram["ln2w_col"]
    gate_w = c.dram["gate_w"]
    up_w = c.dram["up_w"]
    down_w = c.dram["down_w"]
    h2scr = c.dram["h2scr"]
    xres1 = c.dram["xres1"]

    bS = [c.psum(f"l4S{i}") for i in range(4)]
    bB = c.psum("l4B")
    bM = [c.psum(f"l4M{i}") for i in range(2)]

    row_bf = c.sbuf("row_bf", [1, T], BF16)
    s1row = c.sbuf("s1row", [1, T], F32)
    s2row = c.sbuf("s2row", [1, T], F32)
    nw = c.sbuf("nw_sb", [128, 32], F32)
    l2w = c.sbuf("l2w_sb", [128, 16], F32)
    arena = c.sbuf("l4arena", [128, 65536], BF16)
    rb1 = arena[:, 61440:63488]
    mub = arena[:, 63488:65536]
    slots = [c.sbuf(f"l4s{i}", [128, T], BF16) for i in range(4)]
    wb = [c.sbuf(f"l4wb{i}", [128, 8192], BF16) for i in range(2)]
    tmpf = c.sbuf("l4tmp", [128, 512], F32)

    s_v = p.stream("l4v")
    s_w = [p.stream("l4w0"), p.stream("l4w1")]
    s_h = [p.stream("l4h0"), p.stream("l4h1")]
    s_o = [p.stream("l4o0"), p.stream("l4o1")]
    s_g = [p.stream("l4g0"), p.stream("l4g1")]
    s_d = [p.stream("l4d0"), p.stream("l4d1")]
    s_m = p.stream("l4misc")
    s_x5 = p.stream("l4x5")

    mt_ = s_m.dma("gpsimd", nw[:], normw[:, :])
    lt_ = s_m.dma("gpsimd", l2w[:], ln2w[:, :])
    rt_ = s_m.dma("gpsimd", s1row[0:1, :], rstdscr.rearrange("(a t) -> a t", a=1))
    p.wait("vector", rt_)
    p.op("vector", lambda h: h.tensor_copy(row_bf[:], s1row[0:1, :]))
    rbt = p.milestone("vector")
    b1 = bcast_rows(c, [(row_bf[:], rb1)], bB, rbt)

    vfm = arena[:, 0:32 * 1024]
    wtok = [None, None]
    wfree = [None, None]
    wnorm = [None, None]

    def load_wout(mt, half):
        sl = mt % 2
        p.wait("gpsimd", wfree[sl])
        src = w_out[:, mt * 128:(mt + 1) * 128].rearrange("(k p) m -> p k m", p=128)
        dst = wb[sl][:, 0:4096].rearrange("p (k m) -> p k m", k=32)
        tok = s_w[sl].dma("gpsimd", dst, src)
        p.waits("vector", [tok, mt_])
        p.wait("vector", wnorm[sl])
        for k in range(32):
            p.op("vector", lambda h, sl=sl, k=k: h.tensor_scalar(
                wb[sl][:, k * 128:(k + 1) * 128], wb[sl][:, k * 128:(k + 1) * 128],
                nw[:, k:k + 1], None, AluOpType.mult))
        wnorm[sl] = p.milestone("vector")
        wtok[sl] = wnorm[sl]

    pb_free = [None, None]
    hload = [None, None]
    ost = [None, None]
    hsl = slots[0:2]
    osl = slots[2:4]
    part1_mm = None
    vld_prev = None
    for half in range(2):
        toff = half * 1024
        vld = None
        p.wait("scalar", part1_mm)
        for kd in range(32):
            vld = s_v.dma("scalar", vfm[:, kd * 1024:(kd + 1) * 1024],
                          vscr[toff:toff + 1024, kd * 128:(kd + 1) * 128],
                          transpose=True)
        load_wout(0, half)
        load_wout(1, half)
        for mt in range(16):
            sl = mt % 2
            p.wait("gpsimd", hload[sl])
            hld = s_h[sl].dma("gpsimd", hsl[sl][:, 0:1024],
                              hscr[mt * 128:(mt + 1) * 128, toff:toff + 1024])
            p.waits("tensor", [wtok[sl], vld, b1])
            for g in range(2):
                pb = bM[g % 2]
                p.wait("tensor", pb_free[g % 2])
                for k in range(32):
                    p.op("tensor", lambda h, sl=sl, g=g, k=k, pb=pb: h.matmul(
                        pb[:], wb[sl][:, k * 128:(k + 1) * 128],
                        vfm[:, k * 1024 + g * 512: k * 1024 + (g + 1) * 512],
                        start=(k == 0), stop=(k == 31)))
                mmt = p.milestone("tensor")
                cs = slice(g * 512, (g + 1) * 512)
                p.waits("vector", [mmt, hld, ost[sl]])
                p.op("vector", lambda h, pb=pb, cs=cs, toff=toff: h.tensor_tensor(
                    tmpf[:], pb[:], rb1[:, toff + cs.start:toff + cs.stop],
                    AluOpType.mult))
                pb_free[g % 2] = p.milestone("vector")
                p.op("vector", lambda h, sl=sl, cs=cs: h.tensor_tensor(
                    hsl[sl][:, cs], tmpf[:], hsl[sl][:, cs], AluOpType.add))
            ovt = p.milestone("vector")
            hload[sl] = ovt
            p.wait("sync", ovt)
            ost[sl] = s_o[sl].dma("sync",
                                  h2scr[mt * 128:(mt + 1) * 128, toff:toff + 1024],
                                  hsl[sl][:, 0:1024])
            wfree[sl] = mmt
            part1_mm = mmt
            if mt + 2 < 16:
                load_wout(mt + 2, half)
    part1_ovt = hload[1]

    xs = slots[0:2]
    sqs = slots[2:4]
    ld_toks = {}
    p.wait("gpsimd", ost[0])
    p.wait("gpsimd", ost[1])

    def get_h2(k):
        sl = k % 2
        p.wait("gpsimd", ld_toks.get(k - 2))
        t = s_h[sl].dma("gpsimd", xs[sl][:], h2scr[k * 128:(k + 1) * 128, :])
        ld_toks[k] = t
        return xs[sl][:], [t, ost[sl]]

    mmq, evq = colstats(c, NKD, get_h2, bS, sqs, False, None, s2row)
    t3 = rms_r(c, s2row, row_bf, D, evq)
    b2 = bcast_rows(c, [(row_bf[:], rb1)], bB, t3, b1, out_free=part1_ovt)

    rmsh2 = arena[:, 0:16 * 1024]
    inter = arena[:, 16384:16384 + 45056]
    hacc = slots[0:2]
    silu_toks = {}
    down_mmt = None
    gtok = [None, None]
    gfree = [None, None]
    h2ld = [None, None]
    h2st = [None, None]
    apply_tok = None
    for half in range(2):
        toff = half * 1024
        ap_toks = {}
        for k in range(NKD):
            sl = k % 2
            p.wait("gpsimd", ap_toks.get(k - 2))
            if half == 0 and k < 2:
                p.wait("gpsimd", mmq)
            if half == 1 and k < 2:
                p.wait("gpsimd", h2st[0])
                p.wait("gpsimd", h2st[1])
            ldh = s_h[sl].dma("gpsimd", xs[sl][:, 0:1024],
                              h2scr[k * 128:(k + 1) * 128, toff:toff + 1024])
            p.waits("vector", [ldh, b2, lt_, part1_mm, down_mmt])
            p.op("vector", lambda h, sl=sl, k=k: h.tensor_scalar(
                sqs[sl][:, 0:1024], xs[sl][:, 0:1024], l2w[:, k:k + 1],
                None, AluOpType.mult))
            p.op("vector", lambda h, sl=sl, k=k, toff=toff: h.tensor_tensor(
                rmsh2[:, k * 1024:(k + 1) * 1024], sqs[sl][:, 0:1024],
                rb1[:, toff:toff + 1024], AluOpType.mult))
            ap_toks[k] = p.milestone("vector")
        rmsh2_all = ap_toks[NKD - 1]
        p.wait("scalar", down_mmt)
        for ib in range(11):
            for which in range(2):
                sl = (ib * 2 + which) % 2
                w = gate_w if which == 0 else up_w
                p.wait("gpsimd", gfree[sl])
                src = w[:, ib * 512:(ib + 1) * 512].rearrange("(k p) m -> p k m", p=128)
                dst = wb[sl][:, :].rearrange("p (k m) -> p k m", k=16)
                gtok[sl] = s_g[sl].dma("gpsimd", dst, src)
                p.waits("tensor", [gtok[sl], rmsh2_all])
                last_mmt = None
                for mt4 in range(4):
                    im = ib * 4 + mt4
                    for tg in range(2):
                        pb = bM[tg % 2]
                        p.wait("tensor", pb_free[tg % 2])
                        for k in range(NKD):
                            p.op("tensor", lambda h, sl=sl, mt4=mt4, tg=tg, k=k,
                                 pb=pb: h.matmul(
                                     pb[:], wb[sl][:, k * 512 + mt4 * 128:
                                                   k * 512 + (mt4 + 1) * 128],
                                     rmsh2[:, k * 1024 + tg * 512:
                                           k * 1024 + (tg + 1) * 512],
                                     start=(k == 0), stop=(k == NKD - 1)))
                        mmt = p.milestone("tensor")
                        last_mmt = mmt
                        ics = slice(16384 + im * 1024 + tg * 512,
                                    16384 + im * 1024 + (tg + 1) * 512)
                        if which == 0:
                            p.wait("scalar", mmt)
                            p.op("scalar", lambda h, pb=pb, ics=ics: h.activation(
                                arena[:, ics], pb[:], AF.Silu))
                            tk = p.milestone("scalar")
                            silu_toks[(im, tg)] = tk
                            pb_free[tg % 2] = tk
                        else:
                            p.waits("vector", [mmt, silu_toks[(im, tg)]])
                            p.op("vector", lambda h, pb=pb, ics=ics: h.tensor_tensor(
                                arena[:, ics], arena[:, ics], pb[:], AluOpType.mult))
                            pb_free[tg % 2] = p.milestone("vector")
                gfree[sl] = last_mmt
        inter_tok = pb_free[0]
        inter_tok2 = pb_free[1]
        for mt in range(16):
            sl = mt % 2
            p.wait("gpsimd", gfree[sl])
            src = down_w[:, mt * 128:(mt + 1) * 128].rearrange("(k p) m -> p k m", p=128)
            dstw = wb[sl][:, 0:44 * 128].rearrange("p (k m) -> p k m", k=44)
            dtok = s_d[sl].dma("gpsimd", dstw, src)
            p.wait("gpsimd", h2ld[sl])
            p.wait("gpsimd", rmsh2_all)
            hld = s_h[sl].dma("gpsimd", hacc[sl][:, 0:1024],
                              h2scr[mt * 128:(mt + 1) * 128, toff:toff + 1024])
            p.waits("tensor", [dtok, inter_tok, inter_tok2])
            for tg in range(2):
                pb = bM[tg % 2]
                p.wait("tensor", pb_free[tg % 2])
                for k in range(44):
                    p.op("tensor", lambda h, sl=sl, tg=tg, k=k, pb=pb: h.matmul(
                        pb[:], wb[sl][:, k * 128:(k + 1) * 128],
                        inter[:, k * 1024 + tg * 512: k * 1024 + (tg + 1) * 512],
                        start=(k == 0), stop=(k == 43)))
                mmt = p.milestone("tensor")
                p.waits("vector", [mmt, hld, h2st[sl]])
                cs = slice(tg * 512, (tg + 1) * 512)
                p.op("vector", lambda h, sl=sl, pb=pb, cs=cs: h.tensor_tensor(
                    hacc[sl][:, cs], hacc[sl][:, cs], pb[:], AluOpType.add))
                pb_free[tg % 2] = p.milestone("vector")
            hvt = pb_free[1]
            p.wait("sync", pb_free[0])
            p.wait("sync", pb_free[1])
            h2st[sl] = s_o[sl].dma(
                "sync", h2scr[mt * 128:(mt + 1) * 128, toff:toff + 1024],
                hacc[sl][:, 0:1024])
            h2ld[sl] = hvt
            gfree[sl] = mmt
            down_mmt = mmt

    p.wait("gpsimd", h2st[0])
    p.wait("gpsimd", h2st[1])
    ld5 = {}

    def get_h2b(k):
        sl = k % 2
        p.wait("gpsimd", ld5.get(k - 2, down_mmt))
        p.wait("gpsimd", h2ld[sl])
        t = s_h[sl].dma("gpsimd", xs[sl][:], h2scr[k * 128:(k + 1) * 128, :])
        ld5[k] = t
        return xs[sl][:], [t]

    mm5, ev5 = colstats(c, NKD, get_h2b, bS, sqs, True, s1row, s2row)
    t5 = ln_mu(c, s1row, row_bf, D, ev5)
    b5 = bcast_rows(c, [(row_bf[:], mub)], bB, t5, b2, out_free=down_mmt)
    t6 = ln_r(c, s1row, s2row, row_bf, D, b5)
    b6 = bcast_rows(c, [(row_bf[:], rb1)], bB, t6, b5, out_free=rmsh2_all)
    x1done = {}
    for k in range(NKD):
        sl = k % 2
        p.wait("gpsimd", x1done.get(k - 2, mm5))
        ldh = s_h[sl].dma("gpsimd", xs[sl][:], h2scr[k * 128:(k + 1) * 128, :])
        ldx = s_x5.dma("gpsimd", sqs[sl][:], xres_o[k * 128:(k + 1) * 128, :])
        p.waits("vector", [ldh, ldx, b6])
        p.op("vector", lambda h, sl=sl: h.tensor_tensor(
            xs[sl][:], xs[sl][:], mub[:], AluOpType.subtract))
        p.op("vector", lambda h, sl=sl: h.tensor_tensor(
            xs[sl][:], xs[sl][:], rb1[:], AluOpType.mult))
        p.op("vector", lambda h, sl=sl: h.tensor_tensor(
            xs[sl][:], xs[sl][:], sqs[sl][:], AluOpType.add))
        vt = p.milestone("vector")
        p.wait("sync", vt)
        x1done[k] = s_o[sl].dma("sync", xres1[k * 128:(k + 1) * 128, :], xs[sl][:])


def finish(c):
    c.p.finish_streams()


def build_L4():
    nc = make_nc()
    with contextlib.ExitStack() as es:
        c = Ctx(nc, es)
        c.dram_in("vscr", [T, 4096], BF16)
        c.dram_in("rstdscr", [T], F32)
        c.dram_in("hscr", [D, T], BF16)
        c.dram_in("xres_o", [D, T], BF16)
        c.dram_in("w_out", [4096, 2048], F32)
        c.dram_in("normw_col", [128, 32], F32)
        c.dram_in("ln2w_col", [128, 16], F32)
        c.dram_in("gate_w", [2048, 5632], F32)
        c.dram_in("up_w", [2048, 5632], F32)
        c.dram_in("down_w", [5632, 2048], F32)
        c.dram_out("h2scr", [D, T], BF16)
        c.dram_out("xres1", [D, T], BF16)
        emit_consts(c)
        emit_L4(c)
        finish(c)
        c.p.finalize()
    return nc




def emit_L5(c):
    p = c.p
    xres1 = c.dram["xres1"]
    w_gate = c.dram["w_gate"]
    w_up = c.dram["w_up"]
    w_down = c.dram["w_down"]
    sg = c.dram["sgate_col"]
    su = c.dram["sup_col"]
    sd = c.dram["sdown_col"]
    f2scr = c.dram["f2scr"]
    xres2 = c.dram["xres2"]

    bS = [c.psum(f"l5S{i}") for i in range(4)]
    bB = c.psum("l5B")
    bM = [c.psum(f"l5M{i}") for i in range(2)]

    row_bf = c.sbuf("row_bf", [1, T], BF16)
    s1row = c.sbuf("s1row", [1, T], F32)
    s2row = c.sbuf("s2row", [1, T], F32)
    mub = c.sbuf("mub", [128, T], BF16)
    rb = c.sbuf("rb", [128, T], BF16)
    sgc = c.sbuf("sgc", [128, 64], F32)
    suc = c.sbuf("suc", [128, 64], F32)
    sdc = c.sbuf("sdc", [128, 16], F32)
    lnx = c.sbuf("lnx", [128, NKD * T], BF16)
    interq = c.sbuf("interq", [128, 64 * 512], BF16)
    wbuf = [c.sbuf(f"l5w{i}", [128, 16 * 512], BF16) for i in range(2)]
    slots = [c.sbuf(f"l5s{i}", [128, T], BF16) for i in range(4)]
    upt = c.sbuf("l5up", [128, 512], BF16)

    s_x = [p.stream("l5x0"), p.stream("l5x1")]
    s_w = [p.stream("l5w0"), p.stream("l5w1")]
    s_o = [p.stream("l5o0"), p.stream("l5o1")]
    s_m = p.stream("l5m")

    t1 = s_m.dma("gpsimd", sgc[:], sg[:, :])
    t2 = s_m.dma("gpsimd", suc[:], su[:, :])
    t3 = s_m.dma("gpsimd", sdc[:], sd[:, :])

    xs = slots[0:2]
    sqs = slots[2:4]
    ldk = {}

    def get_x1(k):
        sl = k % 2
        p.wait("gpsimd", ldk.get(k - 2))
        t = s_x[sl].dma("gpsimd", xs[sl][:], xres1[k * 128:(k + 1) * 128, :])
        ldk[k] = t
        return xs[sl][:], [t]

    mm1, ev1 = colstats(c, NKD, get_x1, bS, sqs, True, s1row, s2row)
    tmu = ln_mu(c, s1row, row_bf, D, ev1)
    b1 = bcast_rows(c, [(row_bf[:], mub)], bB, tmu)
    tr = ln_r(c, s1row, s2row, row_bf, D, b1)
    b2 = bcast_rows(c, [(row_bf[:], rb)], bB, tr, b1)
    lna = {}
    for k in range(NKD):
        sl = k % 2
        p.wait("gpsimd", lna.get(k - 2, mm1))
        ld = s_x[sl].dma("gpsimd", xs[sl][:], xres1[k * 128:(k + 1) * 128, :])
        p.waits("vector", [ld, b2])
        p.op("vector", lambda h, sl=sl: h.tensor_tensor(
            sqs[sl][:], xs[sl][:], mub[:], AluOpType.subtract))
        p.op("vector", lambda h, sl=sl, k=k: h.tensor_tensor(
            lnx[:, k * T:(k + 1) * T], sqs[sl][:], rb[:], AluOpType.mult))
        lna[k] = p.milestone("vector")
    lnx_all = lna[NKD - 1]

    pb_free = [None, None]
    wtok = [None, None]
    wfree = [None, None]
    fst = [None, None]
    down_mmt = None
    silu_toks = {}
    for q in range(4):
        toff = q * 512
        p.wait("scalar", down_mmt)
        p.wait("vector", down_mmt)
        for ib in range(16):
            for which in range(2):
                sl = (ib * 2 + which) % 2
                w = w_gate if which == 0 else w_up
                p.wait("gpsimd", wfree[sl])
                src = w[:, ib * 512:(ib + 1) * 512].rearrange("(k p) m -> p k m", p=128)
                dst = wbuf[sl][:, :].rearrange("p (k m) -> p k m", k=16)
                wtok[sl] = s_w[sl].dma("gpsimd", dst, src)
                p.waits("tensor", [wtok[sl], lnx_all])
                last_mmt = None
                for mt4 in range(4):
                    im = ib * 4 + mt4
                    pb = bM[mt4 % 2]
                    p.wait("tensor", pb_free[mt4 % 2])
                    for k in range(NKD):
                        p.op("tensor", lambda h, sl=sl, mt4=mt4, k=k, pb=pb,
                             toff=toff: h.matmul(
                                 pb[:], wbuf[sl][:, k * 512 + mt4 * 128:
                                                 k * 512 + (mt4 + 1) * 128],
                                 lnx[:, k * T + toff: k * T + toff + 512],
                                 start=(k == 0), stop=(k == NKD - 1)))
                    mmt = p.milestone("tensor")
                    last_mmt = mmt
                    ics = slice(im * 512, (im + 1) * 512)
                    if which == 0:
                        p.waits("scalar", [mmt, t1])
                        p.op("scalar", lambda h, pb=pb, ics=ics, im=im: h.activation(
                            interq[:, ics], pb[:], AF.Silu,
                            scale=sgc[:, im:im + 1]))
                        tk = p.milestone("scalar")
                        silu_toks[im] = tk
                        pb_free[mt4 % 2] = tk
                    else:
                        p.waits("scalar", [mmt, t2])
                        p.op("scalar", lambda h, pb=pb, im=im: h.activation(
                            upt[:], pb[:], AF.Identity, scale=suc[:, im:im + 1]))
                        sct = p.milestone("scalar")
                        p.waits("vector", [sct, silu_toks[im]])
                        p.op("vector", lambda h, ics=ics: h.tensor_tensor(
                            interq[:, ics], interq[:, ics], upt[:], AluOpType.mult))
                        vt = p.milestone("vector")
                        pb_free[mt4 % 2] = vt
                        p.wait("scalar", vt)
                wfree[sl] = last_mmt
        inter_tok = pb_free[0]
        inter_tok2 = pb_free[1]
        for mt in range(16):
            sl = mt % 2
            p.wait("gpsimd", wfree[sl])
            src = w_down[:, mt * 128:(mt + 1) * 128].rearrange("(k p) m -> p k m", p=128)
            dst = wbuf[sl][:, 0:64 * 128].rearrange("p (k m) -> p k m", k=64)
            dtok = s_w[sl].dma("gpsimd", dst, src)
            p.waits("tensor", [dtok, inter_tok, inter_tok2])
            pb = bM[mt % 2]
            p.wait("tensor", pb_free[mt % 2])
            for k in range(64):
                p.op("tensor", lambda h, sl=sl, k=k, pb=pb: h.matmul(
                    pb[:], wbuf[sl][:, k * 128:(k + 1) * 128],
                    interq[:, k * 512:(k + 1) * 512],
                    start=(k == 0), stop=(k == 63)))
            mmt = p.milestone("tensor")
            p.waits("scalar", [mmt, t3, fst[sl]])
            p.op("scalar", lambda h, sl=sl, mt=mt, pb=pb: h.activation(
                xs[sl][:, 0:512], pb[:], AF.Identity, scale=sdc[:, mt:mt + 1]))
            sct = p.milestone("scalar")
            pb_free[mt % 2] = sct
            p.wait("sync", sct)
            fst[sl] = s_o[sl].dma("sync", f2scr[mt * 128:(mt + 1) * 128,
                                                toff:toff + 512], xs[sl][:, 0:512])
            wfree[sl] = mmt
            down_mmt = mmt

    p.wait("gpsimd", fst[0])
    p.wait("gpsimd", fst[1])
    ld5 = {}

    def get_f2(k):
        sl = k % 2
        p.wait("gpsimd", ld5.get(k - 2, down_mmt))
        t = s_x[sl].dma("gpsimd", xs[sl][:], f2scr[k * 128:(k + 1) * 128, :])
        ld5[k] = t
        return xs[sl][:], [t]

    mm5, ev5 = colstats(c, NKD, get_f2, bS, sqs, True, s1row, s2row)
    t5 = ln_mu(c, s1row, row_bf, D, ev5)
    b5 = bcast_rows(c, [(row_bf[:], mub)], bB, t5, b2, out_free=lnx_all)
    t6 = ln_r(c, s1row, s2row, row_bf, D, b5)
    b6 = bcast_rows(c, [(row_bf[:], rb)], bB, t6, b5, out_free=lnx_all)
    x2done = {}
    s_x2 = p.stream("l5x2")
    for k in range(NKD):
        sl = k % 2
        p.wait("gpsimd", x2done.get(k - 2, mm5))
        ldh = s_x[sl].dma("gpsimd", xs[sl][:], f2scr[k * 128:(k + 1) * 128, :])
        ldx = s_x2.dma("gpsimd", sqs[sl][:], xres1[k * 128:(k + 1) * 128, :])
        p.waits("vector", [ldh, ldx, b6])
        p.op("vector", lambda h, sl=sl: h.tensor_tensor(
            xs[sl][:], xs[sl][:], mub[:], AluOpType.subtract))
        p.op("vector", lambda h, sl=sl: h.tensor_tensor(
            xs[sl][:], xs[sl][:], rb[:], AluOpType.mult))
        p.op("vector", lambda h, sl=sl: h.tensor_tensor(
            xs[sl][:], xs[sl][:], sqs[sl][:], AluOpType.add))
        vt = p.milestone("vector")
        p.wait("sync", vt)
        x2done[k] = s_o[sl].dma("sync", xres2[k * 128:(k + 1) * 128, :], xs[sl][:])


def build_L5():
    nc = make_nc()
    with contextlib.ExitStack() as es:
        c = Ctx(nc, es)
        c.dram_in("xres1", [D, T], BF16)
        c.dram_in("w_gate", [2048, 8192], I8)
        c.dram_in("w_up", [2048, 8192], I8)
        c.dram_in("w_down", [8192, 2048], I8)
        c.dram_in("sgate_col", [128, 64], F32)
        c.dram_in("sup_col", [128, 64], F32)
        c.dram_in("sdown_col", [128, 16], F32)
        c.dram_out("f2scr", [D, T], BF16)
        c.dram_out("xres2", [D, T], BF16)
        emit_consts(c)
        emit_L5(c)
        finish(c)
        c.p.finalize()
    return nc




def emit_L6(c):
    p = c.p
    xres2 = c.dram["xres2"]
    wqT = c.dram["wqT"]
    wkT = c.dram["wkT"]
    wvT = c.dram["wvT"]
    woT = c.dram["woT"]
    lnxscr = c.dram["lnxscr"]
    qscr = c.dram["qscr"]
    kscr = c.dram["kscr"]
    vscr2 = c.dram["vscr2"]
    zascr = c.dram["zascr"]
    aoscr = c.dram["aoscr"]
    xres3 = c.dram["xres3"]
    caus = c.dram["causmask"]

    bS = [c.psum(f"l6S{i}") for i in range(4)]
    bB = c.psum("l6B")
    bM = [c.psum(f"l6M{i}") for i in range(2)]

    row_bf = c.sbuf("row_bf", [1, T], BF16)
    s1row = c.sbuf("s1row", [1, T], F32)
    s2row = c.sbuf("s2row", [1, T], F32)
    mub = c.sbuf("mub", [128, T], BF16)
    rb = c.sbuf("rb", [128, T], BF16)
    slots = [c.sbuf(f"l6s{i}", [128, T], BF16) for i in range(4)]
    arena = c.sbuf("l6arena", [128, 32768], BF16)
    qt = [c.sbuf(f"qt{i}", [128, 512], BF16) for i in range(2)]
    zh = c.sbuf("zh", [128, T], BF16)
    rrow = c.sbuf("rrow", [1, 512], F32)
    rbt = c.sbuf("rbt", [128, 512], BF16)
    cmask = c.sbuf("cmask", [128, 128], F32)
    stage = [c.sbuf(f"l6stage{i}", [128, 512], BF16) for i in range(2)]

    s_x = [p.stream("l6x0"), p.stream("l6x1")]
    s_w = p.stream("l6w")
    s_o = [p.stream("l6o0"), p.stream("l6o1")]
    s_q = [p.stream("l6q0"), p.stream("l6q1")]
    s_k = [p.stream("l6k0"), p.stream("l6k1")]
    s_v = [p.stream("l6v0"), p.stream("l6v1")]
    s_z = p.stream("l6z")
    s_m = p.stream("l6m")

    cm_tok = s_m.dma("gpsimd", cmask[:], caus[:, :])

    xs = slots[0:2]
    sqs = slots[2:4]
    ldk = {}

    def get_x2(k):
        sl = k % 2
        p.wait("gpsimd", ldk.get(k - 2))
        t = s_x[sl].dma("gpsimd", xs[sl][:], xres2[k * 128:(k + 1) * 128, :])
        ldk[k] = t
        return xs[sl][:], [t]

    mm1, ev1 = colstats(c, NKD, get_x2, bS, sqs, True, s1row, s2row)
    tmu = ln_mu(c, s1row, row_bf, D, ev1)
    b1 = bcast_rows(c, [(row_bf[:], mub)], bB, tmu)
    tr = ln_r(c, s1row, s2row, row_bf, D, b1)
    b2 = bcast_rows(c, [(row_bf[:], rb)], bB, tr, b1)
    lna = {}
    for k in range(NKD):
        sl = k % 2
        p.wait("gpsimd", lna.get(k - 2, mm1))
        ld = s_x[sl].dma("gpsimd", xs[sl][:], xres2[k * 128:(k + 1) * 128, :])
        p.waits("vector", [ld, b2])
        p.op("vector", lambda h, sl=sl: h.tensor_tensor(
            sqs[sl][:], xs[sl][:], mub[:], AluOpType.subtract))
        p.op("vector", lambda h, sl=sl: h.tensor_tensor(
            sqs[sl][:], sqs[sl][:], rb[:], AluOpType.mult))
        vt = p.milestone("vector")
        p.wait("sync", vt)
        lna[k] = s_o[sl].dma("sync", lnxscr[k * 128:(k + 1) * 128, :], sqs[sl][:])
    lnx_st = [lna[NKD - 2], lna[NKD - 1]]

    wbig = arena[:, 0:16384]
    lnb = [arena[:, 16384:24576], arena[:, 24576:32768]]
    pbf = [None, None]
    qst = [None, None]
    arena_user = None

    def proj_fm(wsrc, dst_hbm, prev_tok):
        last = prev_tok
        wtok = None
        for hhalf in range(2):
            p.wait("gpsimd", last)
            wtok = s_w.dma("gpsimd", wbig[:, :],
                           wsrc[:, hhalf * 16384:(hhalf + 1) * 16384])
            ln_tok = [last, last]
            for tg in range(4):
                sl = tg % 2
                p.wait("gpsimd", ln_tok[sl])
                p.waits("gpsimd", lnx_st)
                lt = s_q[sl].dma("gpsimd",
                                 lnb[sl].rearrange("p (k t) -> p k t", k=16),
                                 lnxscr[:, tg * 512:(tg + 1) * 512].rearrange(
                                     "(k p) t -> p k t", p=128))
                p.waits("tensor", [lt, wtok, prev_tok])
                for hh8 in range(8):
                    hh = hhalf * 8 + hh8
                    pb = bM[hh8 % 2]
                    p.wait("tensor", pbf[hh8 % 2])
                    for k in range(NKD):
                        p.op("tensor", lambda h, hh8=hh8, k=k, sl=sl, pb=pb: h.matmul(
                            pb[:], wbig[:, (hh8 * 16 + k) * 128:(hh8 * 16 + k + 1) * 128],
                            lnb[sl][:, k * 512:(k + 1) * 512],
                            start=(k == 0), stop=(k == NKD - 1)))
                    mmt = p.milestone("tensor")
                    ln_tok[sl] = mmt
                    last = mmt
                    p.waits("scalar", [mmt, qst[hh8 % 2]])
                    p.op("scalar", lambda h, pb=pb, hh8=hh8: h.copy(
                        stage[hh8 % 2][:], pb[:]))
                    sct = p.milestone("scalar")
                    pbf[hh8 % 2] = sct
                    p.wait("sync", sct)
                    qst[hh8 % 2] = s_o[hh8 % 2].dma(
                        "sync", dst_hbm[hh * 128:(hh + 1) * 128,
                                        tg * 512:(tg + 1) * 512], stage[hh8 % 2][:])
        return last, qst[0], qst[1]

    kdone, kst0, kst1 = proj_fm(wkT, kscr, None)
    qdone, qst0, qst1 = proj_fm(wqT, qscr, kdone)

    vlast = qdone
    for hhalf in range(2):
        p.wait("gpsimd", vlast)
        wtok = s_w.dma("gpsimd", wbig[:, :],
                       wvT[:, hhalf * 16384:(hhalf + 1) * 16384])
        lcb = [vlast, vlast]
        for tm in range(16):
            sl = tm % 2
            p.wait("gpsimd", lcb[sl])
            lt = s_q[sl].dma("gpsimd",
                             lnb[sl][:, 0:2048].rearrange("p (k t) -> p k t", k=16),
                             lnxscr[:, tm * 128:(tm + 1) * 128].rearrange(
                                 "(k p) t -> p k t", p=128))
            p.waits("tensor", [lt, wtok, qdone])
            for eg in range(2):
                pb = bM[eg % 2]
                p.wait("tensor", pbf[eg % 2])
                for k in range(NKD):
                    p.op("tensor", lambda h, eg=eg, k=k, sl=sl, pb=pb: h.matmul(
                        pb[:], lnb[sl][:, k * 128:(k + 1) * 128],
                        wbig[:, k * 1024 + eg * 512: k * 1024 + (eg + 1) * 512],
                        start=(k == 0), stop=(k == NKD - 1)))
                mmt = p.milestone("tensor")
                lcb[sl] = mmt
                vlast = mmt
                p.waits("scalar", [mmt, qst[eg % 2]])
                p.op("scalar", lambda h, pb=pb, eg=eg: h.copy(
                    stage[eg % 2][:], pb[:]))
                sct = p.milestone("scalar")
                pbf[eg % 2] = sct
                p.wait("sync", sct)
                qst[eg % 2] = s_o[eg % 2].dma(
                    "sync", vscr2[tm * 128:(tm + 1) * 128,
                                  hhalf * 1024 + eg * 512:
                                  hhalf * 1024 + (eg + 1) * 512], stage[eg % 2][:])
    vst = [qst[0], qst[1]]
    if getattr(c, "stop_after_v", False):
        return

    ptb = arena[:, 0:8192]
    kst = [arena[:, 8192:10240], arena[:, 10240:12288]]
    vstp = [arena[:, 12288:14336], arena[:, 14336:16384]]
    csb = bB
    bz = bS[0]
    brb = bS[1]
    qld = [None, None]
    kld = [vlast, vlast]
    vld = [vlast, vlast]
    ptb_free = {}
    zev = None
    zst = None
    att_last = None
    prev_rsc = None
    for hh in range(16):
        for tg in range(4):
            ntk = 4 * (tg + 1)
            sl = tg % 2
            p.wait("gpsimd", qld[sl])
            qtok = s_q[sl].dma("gpsimd", qt[sl][:],
                               qscr[hh * 128:(hh + 1) * 128, tg * 512:(tg + 1) * 512])
            p.wait("gpsimd", kld[sl])
            ktok = s_k[sl].dma(
                "gpsimd", kst[sl][:, 0:ntk * 128],
                kscr[hh * 128:(hh + 1) * 128, 0:ntk * 128])
            p.wait("gpsimd", vld[sl])
            vtok = s_v[sl].dma(
                "gpsimd", vstp[sl][:, 0:ntk * 128].rearrange("p (tk e) -> p tk e", tk=ntk),
                vscr2[0:ntk * 128, hh * 128:(hh + 1) * 128].rearrange(
                    "(tk p) e -> p tk e", p=128))
            p.waits("tensor", [qtok, ktok, vtok, qdone, kst0, kst1, vst[0], vst[1]])
            p.waits("tensor", [prev_rsc, zev])
            for tkT in range(ntk):
                pb = bM[tkT % 2]
                p.wait("tensor", pbf[tkT % 2])
                p.op("tensor", lambda h, tkT=tkT, sl=sl, pb=pb: h.matmul(
                    pb[:], kst[sl][:, tkT * 128:(tkT + 1) * 128],
                    qt[sl][:], start=True, stop=True))
                mmt = p.milestone("tensor")
                kld[sl] = mmt
                j = tkT - 4 * tg
                if 0 <= j < 4:
                    p.waits("vector", [mmt, cm_tok])
                    p.op("vector", lambda h, j=j, pb=pb: h.tensor_tensor(
                        pb[:, j * 128:(j + 1) * 128], pb[:, j * 128:(j + 1) * 128],
                        cmask[:], AluOpType.add))
                    mmt = p.milestone("vector")
                p.waits("scalar", [mmt, ptb_free.get(tkT)])
                p.op("scalar", lambda h, tkT=tkT, pb=pb: h.activation(
                    ptb[:, tkT * 512:(tkT + 1) * 512], pb[:], AF.Exp,
                    bias=c.cpool[0:128, 3:4], scale=0.08838834764831845))
                et = p.milestone("scalar")
                pbf[tkT % 2] = et
                if 0 < j < 4:
                    p.wait("vector", et)
                    p.op("vector", lambda h, tkT=tkT, j=j: h.memset(
                        ptb[:, tkT * 512:tkT * 512 + j * 128], 0.0))
                    et = p.milestone("vector")
                    pbf[tkT % 2] = et
                p.wait("tensor", et)
                p.op("tensor", lambda h, tkT=tkT, ntk=ntk: h.matmul(
                    csb[0:1, :], c.ones_m1[:], ptb[:, tkT * 512:(tkT + 1) * 512],
                    start=(tkT == 0), stop=(tkT == ntk - 1)))
                p.op("tensor", lambda h, sl=sl, tkT=tkT, ntk=ntk: h.matmul(
                    bz[:], vstp[sl][:, tkT * 128:(tkT + 1) * 128],
                    ptb[:, tkT * 512:(tkT + 1) * 512],
                    start=(tkT == 0), stop=(tkT == ntk - 1)))
                tmk = p.milestone("tensor")
                ptb_free[tkT] = tmk
                vld[sl] = tmk
            cs_mm = ptb_free[ntk - 1]
            att_last = cs_mm
            qld[sl] = kld[sl]
            p.wait("scalar", cs_mm)
            p.op("scalar", lambda h: h.activation(rrow[0:1, :], csb[0:1, :], AF.Ln))
            p.op("scalar", lambda h: h.activation(rrow[0:1, :], rrow[0:1, :],
                                                  AF.Exp, scale=-1.0))
            p.op("scalar", lambda h: h.copy(row_bf[0:1, 0:512], rrow[0:1, :]))
            rsc = p.milestone("scalar")
            prev_rsc = rsc
            p.waits("tensor", [rsc, zev])
            p.op("tensor", lambda h: h.matmul(
                brb[:], c.ones_k1[:], row_bf[0:1, 0:512], start=True, stop=True))
            rbm = p.milestone("tensor")
            p.wait("scalar", rbm)
            p.op("scalar", lambda h: h.copy(rbt[:], brb[:]))
            rbe = p.milestone("scalar")
            p.waits("vector", [rbe, zst])
            p.op("vector", lambda h, tg=tg: h.tensor_tensor(
                zh[:, tg * 512:(tg + 1) * 512], bz[:], rbt[:], AluOpType.mult))
            zev = p.milestone("vector")
        p.wait("sync", zev)
        zst = s_z.dma("sync", zascr[hh * 128:(hh + 1) * 128, :], zh[:])

    zfull = arena[:, 0:32768]
    zldt = {}
    for k in range(NKD):
        sl = k % 2
        p.wait("gpsimd", att_last)
        p.wait("gpsimd", zst)
        zldt[k] = s_q[sl].dma("gpsimd", zfull[:, k * T:(k + 1) * T],
                              zascr[k * 128:(k + 1) * 128, :])
    zall = zldt[NKD - 1]
    wot = c.sbuf("wot", [128, 2048], BF16)
    aost = [None, None]
    wo_free = None
    s_wo = p.stream("l6wo")
    for mt in range(16):
        sl = mt % 2
        p.wait("gpsimd", wo_free)
        wt = s_wo.dma("gpsimd",
                      wot[:, :].rearrange("p (k m) -> p k m", k=16),
                      woT[:, mt * 128:(mt + 1) * 128].rearrange("(k p) m -> p k m", p=128))
        p.waits("tensor", [wt, zall])
        for tg in range(4):
            pb = bM[tg % 2]
            p.wait("tensor", pbf[tg % 2])
            for k in range(NKD):
                p.op("tensor", lambda h, tg=tg, k=k, pb=pb: h.matmul(
                    pb[:], wot[:, k * 128:(k + 1) * 128],
                    zfull[:, k * T + tg * 512: k * T + (tg + 1) * 512],
                    start=(k == 0), stop=(k == NKD - 1)))
            mmt = p.milestone("tensor")
            wo_free = mmt
            p.waits("scalar", [mmt, aost[sl]])
            p.op("scalar", lambda h, sl=sl, tg=tg, pb=pb: h.copy(
                xs[sl][:, tg * 512:(tg + 1) * 512], pb[:]))
            pbf[tg % 2] = p.milestone("scalar")
        p.wait("sync", pbf[0])
        p.wait("sync", pbf[1])
        aost[sl] = s_o[sl].dma("sync", aoscr[mt * 128:(mt + 1) * 128, :], xs[sl][:])

    p.wait("gpsimd", aost[0])
    p.wait("gpsimd", aost[1])
    ld5 = {}

    def get_ao(k):
        sl = k % 2
        p.wait("gpsimd", ld5.get(k - 2, wo_free))
        t = s_x[sl].dma("gpsimd", xs[sl][:], aoscr[k * 128:(k + 1) * 128, :])
        ld5[k] = t
        return xs[sl][:], [t, aost[sl]]

    mm5, ev5 = colstats(c, NKD, get_ao, bS, sqs, True, s1row, s2row)
    t5 = ln_mu(c, s1row, row_bf, D, ev5)
    b5 = bcast_rows(c, [(row_bf[:], mub)], bB, t5, b2)
    t6 = ln_r(c, s1row, s2row, row_bf, D, b5)
    b6 = bcast_rows(c, [(row_bf[:], rb)], bB, t6, b5)
    x3done = {}
    s_x3 = p.stream("l6x3")
    for k in range(NKD):
        sl = k % 2
        p.wait("gpsimd", x3done.get(k - 2, mm5))
        ldh = s_x[sl].dma("gpsimd", xs[sl][:], aoscr[k * 128:(k + 1) * 128, :])
        ldx = s_x3.dma("gpsimd", sqs[sl][:], xres2[k * 128:(k + 1) * 128, :])
        p.waits("vector", [ldh, ldx, b6])
        p.op("vector", lambda h, sl=sl: h.tensor_tensor(
            xs[sl][:], xs[sl][:], mub[:], AluOpType.subtract))
        p.op("vector", lambda h, sl=sl: h.tensor_tensor(
            xs[sl][:], xs[sl][:], rb[:], AluOpType.mult))
        p.op("vector", lambda h, sl=sl: h.tensor_tensor(
            xs[sl][:], xs[sl][:], sqs[sl][:], AluOpType.add))
        vt = p.milestone("vector")
        p.wait("sync", vt)
        x3done[k] = s_o[sl].dma("sync", xres3[k * 128:(k + 1) * 128, :], xs[sl][:])


def build_L6(debug=False, stop_after_v=False):
    nc = make_nc()
    with contextlib.ExitStack() as es:
        c = Ctx(nc, es)
        c.stop_after_v = stop_after_v
        mk = c.dram_out if debug else c.dram_tmp
        c.dram_in("xres2", [D, T], BF16)
        c.dram_in("wqT", [128, 32768], F32)
        c.dram_in("wkT", [128, 32768], F32)
        c.dram_in("wvT", [128, 32768], F32)
        c.dram_in("woT", [2048, 2048], F32)
        c.dram_in("causmask", [128, 128], F32)
        mk("lnxscr", [D, T], BF16)
        mk("qscr", [2048, T], BF16)
        mk("kscr", [2048, T], BF16)
        mk("vscr2", [T, 2048], BF16)
        mk("zascr", [2048, T], BF16)
        c.dram_out("aoscr", [D, T], BF16)
        c.dram_out("xres3", [D, T], BF16)
        emit_consts(c)
        emit_L6(c)
        finish(c)
        c.p.finalize()
    return nc




def emit_L7(c):
    p = c.p
    xres3 = c.dram["xres3"]
    routerT = c.dram["routerT"]
    gate_w = c.dram["moe_gate"]
    up_w = c.dram["moe_up"]
    down_w = c.dram["moe_down"]
    sgw = c.dram["sgw"]
    suw = c.dram["suw"]
    sdw = c.dram["sdw"]
    sgb = c.dram["sgb_col"]
    sub = c.dram["sub_col"]
    sdb = c.dram["sdb_col"]
    sel = c.dram["sel8"]
    lnxscr = c.dram["lnx7scr"]
    xout = c.dram["xout"]

    bS = [c.psum(f"l7S{i}") for i in range(4)]
    bB = c.psum("l7B")
    bM = [c.psum(f"l7M{i}") for i in range(2)]

    row_bf = c.sbuf("row_bf", [1, T], BF16)
    s1row = c.sbuf("s1row", [1, T], F32)
    s2row = c.sbuf("s2row", [1, T], F32)
    mub = c.sbuf("mub", [128, T], BF16)
    rb = c.sbuf("rb", [128, T], BF16)
    slots = [c.sbuf(f"l7s{i}", [128, T], BF16) for i in range(4)]
    rwt = c.sbuf("rwt", [128, 16 * 8], BF16)
    logit = c.sbuf("logit", [128, 16 * 8], F32)
    comb = c.sbuf("comb", [128, 16 * 8], F32)
    v16a = c.sbuf("l7va", [128, 16], F32)
    v16b = c.sbuf("l7vb", [128, 16], F32)
    v16p = c.sbuf("l7vp", [128, 16], F32)
    vtmp8a = c.sbuf("l7ta", [128, 128], F32)
    vtmp8b = c.sbuf("l7tb", [128, 128], F32)
    sg1 = c.sbuf("l7sg1", [128, 128], F32)
    sg2 = c.sbuf("l7sg2", [128, 128], F32)
    px = c.sbuf("l7px", [128, 128], F32)
    combTb = c.sbuf("combTb", [8, T], BF16)
    selb = c.sbuf("selb", [8, 1024], BF16)
    ceb = c.sbuf("ceb", [128, 8 * 1024], BF16)
    idf7 = c.sbuf("idf7", [128, 128], F32)
    acc = c.sbuf("l7acc", [128, 16 * 1024], F32)
    inter = c.sbuf("l7inter", [128, 11 * 1024], BF16)
    h1t = c.sbuf("l7h1", [128, 1024], BF16)
    lnq = [c.sbuf(f"l7ln{i}", [128, 16 * 1024], BF16) for i in range(1)]
    wbuf = [c.sbuf(f"l7w{i}", [128, 22 * 128], BF16) for i in range(2)]
    sgbc = c.sbuf("sgbc", [128, 22], F32)
    subc = c.sbuf("subc", [128, 22], F32)
    sdbc = c.sbuf("sdbc", [128, 16], F32)

    s_x = [p.stream("l7x0"), p.stream("l7x1")]
    s_w = [p.stream("l7w0"), p.stream("l7w1")]
    s_o = [p.stream("l7o0"), p.stream("l7o1")]
    s_m = p.stream("l7m")
    s_ln = p.stream("l7ln")

    m1 = s_m.dma("gpsimd", rwt[:, :].rearrange("p (k e) -> p k e", k=16),
                 routerT.rearrange("(k p) e -> p k e", p=128))
    m2 = s_m.dma("gpsimd", selb[:], sel[:, :])
    m3 = s_m.dma("gpsimd", idf7[:], c.dram["ident_f32"][:, :])
    m4 = s_m.dma("gpsimd", sgbc[:], sgb[:, :])
    m5 = s_m.dma("gpsimd", subc[:], sub[:, :])
    m6 = s_m.dma("gpsimd", sdbc[:], sdb[:, :])

    xs = slots[0:2]
    sqs = slots[2:4]
    ldk = {}

    def get_x3(k):
        sl = k % 2
        p.wait("gpsimd", ldk.get(k - 2))
        t = s_x[sl].dma("gpsimd", xs[sl][:], xres3[k * 128:(k + 1) * 128, :])
        ldk[k] = t
        return xs[sl][:], [t]

    mm1, ev1 = colstats(c, NKD, get_x3, bS, sqs, True, s1row, s2row)
    tmu = ln_mu(c, s1row, row_bf, D, ev1)
    b1 = bcast_rows(c, [(row_bf[:], mub)], bB, tmu)
    tr = ln_r(c, s1row, s2row, row_bf, D, b1)
    b2 = bcast_rows(c, [(row_bf[:], rb)], bB, tr, b1)
    lna = {}
    for k in range(NKD):
        sl = k % 2
        p.wait("gpsimd", lna.get(k - 2, mm1))
        ld = s_x[sl].dma("gpsimd", xs[sl][:], xres3[k * 128:(k + 1) * 128, :])
        p.waits("vector", [ld, b2])
        p.op("vector", lambda h, sl=sl: h.tensor_tensor(
            sqs[sl][:], xs[sl][:], mub[:], AluOpType.subtract))
        p.op("vector", lambda h, sl=sl: h.tensor_tensor(
            sqs[sl][:], sqs[sl][:], rb[:], AluOpType.mult))
        vt = p.milestone("vector")
        p.wait("sync", vt)
        lna[k] = s_o[sl].dma("sync", lnxscr[k * 128:(k + 1) * 128, :], sqs[sl][:])
    lnx_st = [lna[NKD - 2], lna[NKD - 1]]

    lnb = wbuf[0]
    lnld = None
    cps = []
    for tm in range(16):
        p.waits("gpsimd", lnx_st)
        p.wait("gpsimd", cps[-1] if cps else None)
        lt = s_ln.dma("gpsimd", lnb[:, 0:2048].rearrange("p (k t) -> p k t", k=16),
                      lnxscr[:, tm * 128:(tm + 1) * 128].rearrange("(k p) t -> p k t", p=128))
        p.waits("tensor", [lt, m1])
        pb = bB
        for k in range(NKD):
            p.op("tensor", lambda h, tm=tm, k=k, pb=pb: h.matmul(
                pb[:, 0:8], lnb[:, k * 128:(k + 1) * 128],
                rwt[:, k * 8:(k + 1) * 8], start=(k == 0), stop=(k == NKD - 1)))
        mmt = p.milestone("tensor")
        p.wait("vector", mmt)
        p.op("vector", lambda h, tm=tm, pb=pb: h.tensor_copy(
            logit[:, tm * 8:(tm + 1) * 8], pb[:, 0:8]))
        cp = p.milestone("vector")
        cps.append(cp)
        p.wait("tensor", cp)
    negb = c.sbuf("l7negb", [128, 128], F32)
    p.op("gpsimd", lambda h: h.memset(negb[:], -1e9))
    nb_tok = p.milestone("gpsimd")
    NT16 = 16
    d1a = vtmp8a
    l2a = vtmp8b
    p.wait("vector", nb_tok)
    dts = []
    for tm in range(NT16):
        lg = logit[:, tm * 8:(tm + 1) * 8]
        p.op("vector", lambda h, tm=tm, lg=lg: h.tensor_reduce(
            v16a[:, tm:tm + 1], lg, axis=AX.X, op=AluOpType.max))
        p.op("vector", lambda h, tm=tm: h.tensor_copy(
            v16p[:, tm:tm + 1], v16a[:, tm:tm + 1]))
    for tm in range(NT16):
        dts.append(p.op("vector", lambda h, tm=tm: h.tensor_scalar(
            d1a[:, tm * 8:(tm + 1) * 8], logit[:, tm * 8:(tm + 1) * 8],
            v16a[:, tm:tm + 1], None, AluOpType.subtract), inc=1))
    p.waits("scalar", dts)
    t_s1 = p.op("scalar", lambda h: h.activation(sg1[:], d1a[:], AF.Sign), inc=1)
    t_px = p.op("scalar", lambda h: h.activation(px[:], d1a[:], AF.Exp), inc=1)
    p.wait("vector", t_s1)
    p.op("vector", lambda h: h.tensor_tensor(l2a[:], sg1[:], negb[:], AluOpType.mult))
    p.op("vector", lambda h: h.tensor_tensor(l2a[:], l2a[:], negb[:], AluOpType.add))
    p.op("vector", lambda h: h.tensor_tensor(l2a[:], l2a[:], logit[:, :], AluOpType.add))
    for tm in range(NT16):
        p.op("vector", lambda h, tm=tm: h.tensor_reduce(
            v16b[:, tm:tm + 1], l2a[:, tm * 8:(tm + 1) * 8], axis=AX.X,
            op=AluOpType.max))
        p.op("vector", lambda h, tm=tm: h.tensor_copy(
            v16p[:, tm:tm + 1], v16b[:, tm:tm + 1]))
    d2ts = []
    for tm in range(NT16):
        d2ts.append(p.op("vector", lambda h, tm=tm: h.tensor_scalar(
            l2a[:, tm * 8:(tm + 1) * 8], l2a[:, tm * 8:(tm + 1) * 8],
            v16b[:, tm:tm + 1], None, AluOpType.subtract), inc=1))
    t_mm = p.op("vector", lambda h: h.tensor_tensor(
        v16b[:], v16b[:], v16a[:], AluOpType.subtract), inc=1)
    p.waits("scalar", d2ts)
    p.wait("scalar", t_mm)
    t_s2 = p.op("scalar", lambda h: h.activation(sg2[:], l2a[:], AF.Sign), inc=1)
    p.op("scalar", lambda h: h.activation(v16b[:], v16b[:], AF.Exp))
    p.op("scalar", lambda h: h.add(v16b[:], v16b[:], 1.0))
    p.op("scalar", lambda h: h.activation(v16b[:], v16b[:], AF.Ln))
    t_rd = p.op("scalar", lambda h: h.activation(
        v16a[:], v16b[:], AF.Exp, scale=-1.0), inc=1)
    p.waits("vector", [t_s2, t_px])
    p.op("vector", lambda h: h.tensor_tensor(comb[:, :], sg1[:], sg2[:], AluOpType.add))
    p.op("vector", lambda h: h.tensor_scalar(
        comb[:, :], comb[:, :], 2.0, None, AluOpType.add))
    t_ip = p.op("vector", lambda h: h.tensor_tensor(
        comb[:, :], comb[:, :], px[:], AluOpType.mult), inc=1)
    p.waits("scalar", [t_ip, t_rd])
    tp = None
    for tm in range(NT16):
        tp = p.op("scalar", lambda h, tm=tm: h.activation(
            comb[:, tm * 8:(tm + 1) * 8], comb[:, tm * 8:(tm + 1) * 8],
            AF.Identity, scale=v16a[:, tm:tm + 1]), inc=1)
    if getattr(c, "dbg", False):
        p.wait("sync", tp)
        p.stream("dbgc").dma("sync", c.dram["dbg_comb"], comb[:, :])
        p.stream("dbgl").dma("sync", c.dram["dbg_logit"], logit[:, :])
        p.stream("dbgv").dma("sync", c.dram["dbg_v16"], v16a[:, :])
        p.stream("dbgs").dma("sync", c.dram["dbg_sg1"], sg2[:, :])
    p.waits("tensor", [tp, m3])
    for tm in range(16):
        p.op("tensor", lambda h, tm=tm: h.transpose(
            bM[tm % 2][0:8, 0:128], comb[:, tm * 8:(tm + 1) * 8], idf7[:]))
        tt = p.milestone("tensor")
        p.wait("vector", tt)
        p.op("vector", lambda h, tm=tm: h.tensor_copy(
            combTb[0:8, tm * 128:(tm + 1) * 128], bM[tm % 2][0:8, 0:128]))
        cv = p.milestone("vector")
        p.wait("tensor", cv)
    cbt = cv

    lnx_half = lnq[0]
    pbf = [None, None]
    wld = [None, None]
    acc_tok = None
    accbf_tok = None
    close_done = None
    xo_st = {}
    for half in range(2):
        toff = half * 1024
        p.wait("gpsimd", acc_tok)
        lnh = s_ln.dma("gpsimd",
                       lnx_half[:, :].rearrange("p (k t) -> p k t", k=16),
                       lnxscr[:, toff:toff + 1024].rearrange("(k p) t -> p k t", p=128))
        p.waits("tensor", [cbt, m2])
        cebt = None
        for e in range(8):
            for g in range(2):
                cs = slice(toff + g * 512, toff + (g + 1) * 512)
                p.wait("tensor", accbf_tok)
                p.op("tensor", lambda h, e=e, cs=cs: h.matmul(
                    bB[:], selb[:, e * 128:(e + 1) * 128], combTb[0:8, cs],
                    start=True, stop=True))
                mt = p.milestone("tensor")
                p.wait("scalar", mt)
                p.op("scalar", lambda h, e=e, g=g: h.copy(
                    ceb[:, e * 1024 + g * 512:e * 1024 + (g + 1) * 512], bB[:]))
                cebt = p.milestone("scalar")
                p.wait("tensor", cebt)
        p.wait("gpsimd", close_done)
        zz = p.op("gpsimd", lambda h: h.memset(acc[:], 0.0), inc=1)
        dn_last = None
        for e in getattr(c, "dbg_elist", range(10)):
            sp = e - 8
            nim = 11
            p.wait("scalar", dn_last)
            p.wait("vector", dn_last)
            for im in range(nim):
                for which in range(2):
                    sl = (im * 2 + which) % 2
                    if e < 8:
                        wsrc = (gate_w if which == 0 else up_w)[e][:, im * 128:(im + 1) * 128]
                    else:
                        imo = sp * 11 + im
                        wsrc = (sgw if which == 0 else suw)[:, imo * 128:(imo + 1) * 128]
                    p.wait("gpsimd", wld[sl])
                    wt = s_w[sl].dma("gpsimd",
                                     wbuf[sl][:, 0:2048].rearrange("p (k m) -> p k m", k=16),
                                     wsrc.rearrange("(k p) m -> p k m", p=128))
                    p.waits("tensor", [wt, lnh])
                    pb = bM[which]
                    for tg in range(2):
                        p.wait("tensor", pbf[which])
                        for k in range(NKD):
                            p.op("tensor", lambda h, k=k, sl=sl, tg=tg, pb=pb: h.matmul(
                                pb[:], wbuf[sl][:, k * 128:(k + 1) * 128],
                                lnx_half[:, k * 1024 + tg * 512: k * 1024 + (tg + 1) * 512],
                                start=(k == 0), stop=(k == NKD - 1)))
                        mmt = p.milestone("tensor")
                        wld[sl] = mmt
                        ics = slice(im * 1024 + tg * 512, im * 1024 + (tg + 1) * 512)
                        hcs = slice(tg * 512, (tg + 1) * 512)
                        if which == 0:
                            p.waits("scalar", [mmt, m4, pbf[0], pbf[1]])
                            if e < 8:
                                p.op("scalar", lambda h, pb=pb, hcs=hcs: h.activation(
                                    h1t[:, hcs], pb[:], AF.Silu))
                            else:
                                imo = sp * 11 + im
                                p.op("scalar", lambda h, pb=pb, hcs=hcs, imo=imo:
                                     h.activation(h1t[:, hcs], pb[:], AF.Silu,
                                                  bias=sgbc[:, imo:imo + 1]))
                            pbf[0] = p.milestone("scalar")
                        else:
                            p.waits("vector", [mmt, pbf[0], m5, cebt])
                            if e < 8:
                                p.op("vector", lambda h, pb=pb, ics=ics, hcs=hcs: h.tensor_tensor(
                                    inter[:, ics], h1t[:, hcs], pb[:], AluOpType.mult))
                                p.op("vector", lambda h, e=e, ics=ics, hcs=hcs:
                                     h.tensor_tensor(
                                         inter[:, ics], inter[:, ics],
                                         ceb[:, e * 1024 + hcs.start:
                                             e * 1024 + hcs.stop], AluOpType.mult))
                            else:
                                imo = sp * 11 + im
                                p.op("vector", lambda h, pb=pb, ics=ics, imo=imo:
                                     h.tensor_scalar(
                                         inter[:, ics], pb[:], subc[:, imo:imo + 1],
                                         None, AluOpType.add))
                                p.op("vector", lambda h, ics=ics, hcs=hcs: h.tensor_tensor(
                                    inter[:, ics], inter[:, ics], h1t[:, hcs],
                                    AluOpType.mult))
                            pbf[1] = p.milestone("vector")
            inter_tok = pbf[1]
            inter_tok0 = pbf[0]
            if getattr(c, "dbg", False) and half == 0 and e == 0:
                p.waits("sync", [inter_tok, inter_tok0])
                p.stream("dbgi").dma("sync", c.dram["dbg_inter"], inter[:, :])
            for mt in range(16):
                sl = mt % 2
                nk = 11
                if e < 8:
                    wsrc = down_w[e][:, mt * 128:(mt + 1) * 128]
                else:
                    wsrc = sdw[sp * 1408:(sp + 1) * 1408, mt * 128:(mt + 1) * 128]
                p.wait("gpsimd", wld[sl])
                wt = s_w[sl].dma("gpsimd",
                                 wbuf[sl][:, 0:nk * 128].rearrange("p (k m) -> p k m", k=nk),
                                 wsrc.rearrange("(k p) m -> p k m", p=128))
                p.waits("tensor", [wt, inter_tok, inter_tok0, zz])
                for tg in range(2):
                    pb = bM[tg]
                    p.wait("tensor", pbf[tg])
                    for k in range(nk):
                        p.op("tensor", lambda h, k=k, sl=sl, pb=pb, tg=tg, nk=nk:
                             h.matmul(
                                 pb[:], wbuf[sl][:, k * 128:(k + 1) * 128],
                                 inter[:, k * 1024 + tg * 512: k * 1024 + (tg + 1) * 512],
                                 start=(k == 0), stop=(k == nk - 1)))
                    mmt = p.milestone("tensor")
                    wld[sl] = mmt
                    dn_last = mmt
                    acs = slice(mt * 1024 + tg * 512, mt * 1024 + (tg + 1) * 512)
                    p.waits("vector", [mmt, zz, m6, accbf_tok])
                    if e != 8:
                        p.op("vector", lambda h, pb=pb, acs=acs: h.tensor_tensor(
                            acc[:, acs], acc[:, acs], pb[:], AluOpType.add))
                    else:
                        p.op("vector", lambda h, pb=pb, acs=acs, mt=mt: h.tensor_scalar(
                            h1t[:, 0:512], pb[:], sdbc[:, mt:mt + 1], None, AluOpType.add))
                        p.op("vector", lambda h, acs=acs: h.tensor_tensor(
                            acc[:, acs], acc[:, acs], h1t[:, 0:512], AluOpType.add))
                    pbf[tg] = p.milestone("vector")
        acc_done = pbf[1]
        acc_done0 = pbf[0]
        accbf_tok = acc_done
        acc_tok = acc_done
        if getattr(c, "dbg", False) and half == 0:
            p.waits("sync", [acc_done, acc_done0])
            p.stream("dbga").dma("sync", c.dram["dbg_acc"], acc[:, :])
        sq_tok = None
        s1h = s1row[0:1, 0:1024]
        s2h = s2row[0:1, 0:1024]
        s1t = [bS[0][0:1, :], bS[0][64:65, :]]
        s2t = [bS[2][0:1, :], bS[2][64:65, :]]
        cast_tok = {}
        for kk in range(16):
            sl = kk % 2
            p.waits("vector", [acc_done, acc_done0])
            p.wait("vector", sq_tok)
            p.op("vector", lambda h, kk=kk, sl=sl: h.tensor_copy(
                xs[sl][:, 0:1024], acc[:, kk * 1024:(kk + 1) * 1024]))
            p.op("vector", lambda h, kk=kk, sl=sl: h.tensor_tensor(
                sqs[sl][:, 0:1024], xs[sl][:, 0:1024], xs[sl][:, 0:1024],
                AluOpType.mult))
            vsq = p.milestone("vector")
            p.wait("tensor", vsq)
            for g in range(2):
                cs = slice(g * 512, (g + 1) * 512)
                p.op("tensor", lambda h, sl=sl, kk=kk, g=g, cs=cs: h.matmul(
                    s1t[g], c.ones_m1[:], xs[sl][:, cs],
                    start=(kk == 0), stop=(kk == 15)))
                p.op("tensor", lambda h, kk=kk, g=g, sl=sl, cs=cs: h.matmul(
                    s2t[g], c.ones_m1[:], sqs[sl][:, cs],
                    start=(kk == 0), stop=(kk == 15)))
            sq_tok = p.milestone("tensor")
        p.wait("vector", sq_tok)
        for g in range(2):
            cs = slice(g * 512, (g + 1) * 512)
            p.op("vector", lambda h, g=g, cs=cs: h.tensor_copy(s1h[0:1, cs], s1t[g]))
            p.op("vector", lambda h, g=g, cs=cs: h.tensor_copy(s2h[0:1, cs], s2t[g]))
        evh = p.milestone("vector")
        tmu2 = ln_mu(c, s1h, row_bf[0:1, 0:1024], D, evh)
        bmu = bcast_rows2(c, row_bf[0:1, 0:1024], mub[:, 0:1024], bB, tmu2)
        trh = ln_r(c, s1h, s2h, row_bf[0:1, 0:1024], D, bmu)
        brh = bcast_rows2(c, row_bf[0:1, 0:1024], rb[:, 0:1024], bB, trh, bmu)
        fo = c.sb.get("l7fo")
        if fo is None:
            fo = c.sbuf("l7fo", [128, 1024], F32)
        for kk in range(16):
            sl = kk % 2
            p.wait("gpsimd", xo_st.get((half, kk - 2), sq_tok))
            ldx = s_x[sl].dma("gpsimd", xs[sl][:, 0:1024],
                              xres3[kk * 128:(kk + 1) * 128, toff:toff + 1024])
            p.waits("vector", [ldx, brh])
            p.op("vector", lambda h, kk=kk, sl=sl: h.tensor_tensor(
                sqs[sl][:, 0:1024], acc[:, kk * 1024:(kk + 1) * 1024],
                mub[:, 0:1024], AluOpType.subtract))
            p.op("vector", lambda h, sl=sl: h.tensor_tensor(
                sqs[sl][:, 0:1024], sqs[sl][:, 0:1024], rb[:, 0:1024], AluOpType.mult))
            p.wait("vector", xo_st.get((half, kk - 1)))
            p.op("vector", lambda h, sl=sl, fo=fo: h.tensor_tensor(
                fo[:], sqs[sl][:, 0:1024], xs[sl][:, 0:1024], AluOpType.add))
            vt = p.milestone("vector")
            close_done = vt
            p.wait("sync", vt)
            xo_st[(half, kk)] = s_o[sl].dma(
                "sync", xout[kk * 128:(kk + 1) * 128, toff:toff + 1024], fo[:])


def bcast_rows2(c, row, out, bank, row_tok, free_tok=None):
    p = c.p
    W = row.shape[-1] if hasattr(row, 'shape') else 1024
    p.wait("tensor", row_tok)
    p.wait("tensor", free_tok)
    ev = None
    for g in range(2):
        cs = slice(g * 512, (g + 1) * 512)
        p.op("tensor", lambda h, cs=cs: h.matmul(
            bank[:], c.ones_k1[:], row[0:1, cs], start=True, stop=True))
        mt = p.milestone("tensor")
        p.wait("scalar", mt)
        p.op("scalar", lambda h, cs=cs: h.copy(out[:, cs], bank[:]))
        ev = p.milestone("scalar")
        p.wait("tensor", ev)
    return ev


def build_L7(debug=False, elist=None):
    nc = make_nc()
    with contextlib.ExitStack() as es:
        c = Ctx(nc, es)
        c.dbg = debug
        if elist is not None:
            c.dbg_elist = elist
        c.dram_in("xres3", [D, T], BF16)
        c.dram_in("routerT", [2048, 8], F32)
        c.dram_in("moe_gate", [8, 2048, 1408], F32)
        c.dram_in("moe_up", [8, 2048, 1408], F32)
        c.dram_in("moe_down", [8, 1408, 2048], F32)
        c.dram_in("sgw", [2048, 2816], F32)
        c.dram_in("suw", [2048, 2816], F32)
        c.dram_in("sdw", [2816, 2048], F32)
        c.dram_in("sgb_col", [128, 22], F32)
        c.dram_in("sub_col", [128, 22], F32)
        c.dram_in("sdb_col", [128, 16], F32)
        c.dram_in("sel8", [8, 1024], BF16)
        c.dram_in("ident_f32", [128, 128], F32)
        (c.dram_out if debug else c.dram_tmp)("lnx7scr", [D, T], BF16)
        if debug:
            c.dram_out("dbg_comb", [128, 128], F32)
            c.dram_out("dbg_logit", [128, 128], F32)
            c.dram_out("dbg_v16", [128, 16], F32)
            c.dram_out("dbg_sg1", [128, 128], F32)
            c.dram_out("dbg_acc", [128, 16 * 1024], F32)
            c.dram_out("dbg_inter", [128, 11 * 1024], BF16)
        c.dram_out("xout", [D, T], F32)
        emit_consts(c)
        emit_L7(c)
        finish(c)
        c.p.finalize()
    return nc


def build_L2():
    nc = make_nc()
    with contextlib.ExitStack() as es:
        c = Ctx(nc, es)
        c.dram_in("xbcscr", [5120, T + 3], BF16)
        c.dram_in("zscr", [T, 4096], BF16)
        c.dram_in("convw_sb", [128, 160], F32)
        c.dram_in("convb_sb", [128, 40], F32)
        c.dram_out("xbcp", [5120, T], BF16)
        c.dram_out("zsil", [T, 4096], BF16)
        emit_consts(c)
        emit_L2(c)
        finish(c)
        c.p.finalize()
    return nc


def build_L3():
    nc = make_nc()
    with contextlib.ExitStack() as es:
        c = Ctx(nc, es)
        c.dram_in("xbcp", [5120, T], BF16)
        c.dram_in("zsil", [T, 4096], BF16)
        c.dram_in("dtscr", [T, 64], F32)
        c.dram_in("tmask_f", [128, 128], F32)
        c.dram_in("masklT", [128, 128], F32)
        c.dram_in("ident_bf", [128, 128], BF16)
        c.dram_in("ident_f32", [128, 128], F32)
        c.dram_in("hrows", [1, 192], F32)
        c.dram_out("vscr", [T, 4096], BF16)
        c.dram_out("rstdscr", [T], F32)
        emit_consts(c)
        c.ones_m1f = c.sbuf("ones_m1f", [128, 1], F32)
        c.ones_k1f = c.sbuf("ones_k1f", [1, 128], F32)
        c.p.op("gpsimd", lambda h: h.memset(c.ones_m1f[:], 1.0))
        c.p.op("gpsimd", lambda h: h.memset(c.ones_k1f[:], 1.0))
        c.const_tok = c.p.milestone("gpsimd")
        emit_L3(c)
        finish(c)
        c.p.finalize()
    return nc


def make_nc():
    return bass.Bass("TRN2", target_bir_lowering=False,
                     detect_race_conditions=False)


def build_L1():
    nc = make_nc()
    with contextlib.ExitStack() as es:
        c = Ctx(nc, es)
        c.dram_in("xin", [D, T], F32)
        c.dram_in("w_in", [2048, 9280], F32)
        c.dram_in("ln1w_col", [128, 16], F32)
        c.dram_out("xres_o", [D, T], BF16)
        c.dram_out("hscr", [D, T], BF16)
        c.dram_out("zscr", [T, 4096], BF16)
        c.dram_out("xbcscr", [5120, T + 3], BF16)
        c.dram_out("dtscr", [T, 64], F32)
        emit_consts(c)
        emit_L1(c)
        finish(c)
        c.p.finalize()
    return nc


def _prep_shared(inputs):
    f32 = np.float32
    A = lambda x: np.ascontiguousarray(np.asarray(x, f32))
    d = {}
    d["w_in"] = A(inputs["L0_attn_w_in"])
    d["ln1w_col"] = A(np.asarray(inputs["L0_attn_ln1_w"], f32).reshape(16, 128).T)
    cw = np.asarray(inputs["L0_attn_conv_w"], f32)
    d["convw_sb"] = A(cw.reshape(40, 128, 4).transpose(1, 0, 2).reshape(128, 160))
    d["convb_sb"] = A(np.asarray(inputs["L0_attn_conv_b"], f32).reshape(40, 128).T)
    d["tmask_f"] = A(np.triu(np.ones((128, 128))))
    d["masklT"] = A(np.where(np.tril(np.ones((128, 128))) > 0, 0.0, -1e9))
    d["ident_bf"] = np.eye(128).astype(ml_dtypes.bfloat16)
    d["ident_f32"] = A(np.eye(128))
    d["w_gate"] = np.ascontiguousarray(np.asarray(inputs["L0_ffn_w_gate"], np.int8))
    d["w_up"] = np.ascontiguousarray(np.asarray(inputs["L0_ffn_w_up"], np.int8))
    d["w_down"] = np.ascontiguousarray(np.asarray(inputs["L0_ffn_w_down"], np.int8))
    d["sgate_col"] = A(np.asarray(inputs["L0_ffn_s_gate"], f32).reshape(64, 128).T)
    d["sup_col"] = A(np.asarray(inputs["L0_ffn_s_up"], f32).reshape(64, 128).T)
    d["sdown_col"] = A(np.asarray(inputs["L0_ffn_s_down"], f32).reshape(16, 128).T)
    def _qk_prep(w):
        wt = np.asarray(w, f32).transpose(0, 2, 1).reshape(16, 16, 128, 128)
        return A(wt.transpose(2, 0, 1, 3).reshape(128, 32768))

    def _v_prep(w):
        wt = np.asarray(w, f32).transpose(0, 2, 1).reshape(16, 16, 128, 128)
        wp = wt.transpose(2, 1, 0, 3).reshape(128, 16, 2, 8, 128)
        return A(wp.transpose(0, 2, 1, 3, 4).reshape(128, 32768))

    d["wqT"] = _qk_prep(inputs["L1_attn_W_Q"])
    d["wkT"] = _qk_prep(inputs["L1_attn_W_K"])
    d["wvT"] = _v_prep(inputs["L1_attn_W_V"])
    d["woT"] = A(np.asarray(inputs["L1_attn_W_O"], f32).transpose(0, 2, 1).reshape(2048, 2048))
    ii, jj = np.meshgrid(np.arange(128), np.arange(128), indexing="ij")
    d["causmask"] = A(np.where(ii <= jj, 0.0, -1e9))
    d["routerT"] = A(np.asarray(inputs["L1_ffn_router_w"], f32).T)
    d["moe_gate"] = A(inputs["L1_ffn_gate_w"])
    d["moe_up"] = A(inputs["L1_ffn_up_w"])
    d["moe_down"] = A(inputs["L1_ffn_down_w"])
    d["sgw"] = A(inputs["L1_ffn_s_gate_w"])
    d["suw"] = A(inputs["L1_ffn_s_up_w"])
    d["sdw"] = A(inputs["L1_ffn_s_down_w"])
    d["sgb_col"] = A(np.asarray(inputs["L1_ffn_s_gate_b"], f32).reshape(22, 128).T)
    d["sub_col"] = A(np.asarray(inputs["L1_ffn_s_up_b"], f32).reshape(22, 128).T)
    d["sdb_col"] = A(np.asarray(inputs["L1_ffn_s_down_b"], f32).reshape(16, 128).T)
    selm = np.zeros((8, 1024), np.float32)
    for e in range(8):
        selm[e, e * 128:(e + 1) * 128] = 1.0
    d["sel8"] = selm.astype(ml_dtypes.bfloat16)
    d["normw_col"] = A(np.asarray(inputs["L0_attn_norm_w"], f32).reshape(32, 128).T)
    d["ln2w_col"] = A(np.asarray(inputs["L0_attn_ln2_w"], f32).reshape(16, 128).T)
    d["w_out"] = A(inputs["L0_attn_w_out"])
    d["gate_w"] = A(inputs["L0_attn_gate_w"])
    d["up_w"] = A(inputs["L0_attn_up_w"])
    d["down_w"] = A(inputs["L0_attn_down_w"])
    d["hrows"] = A(np.concatenate([np.asarray(inputs["L0_attn_dt_bias"], f32),
                                   np.asarray(inputs["L0_attn_A_log"], f32),
                                   np.asarray(inputs["L0_attn_D"], f32)])[None, :])
    return d


def _launch(nc, shared, per_core, out_names):
    in_maps = []
    for cid in range(8):
        m = dict(shared)
        for k, v in per_core.items():
            m[k] = v[cid]
        in_maps.append(m)
    res = run_bass_kernel_spmd(nc, in_maps, core_ids=list(range(8)))
    return {k: [res.results[c][k] for c in range(8)] for k in out_names}


def kernel(**inputs):
    x = np.asarray(inputs["L0_attn_hidden_states"])
    shared = _prep_shared(inputs)
    xin = [np.ascontiguousarray(x[cid].T) for cid in range(8)]
    r1 = _launch(build_L1(), shared, {"xin": xin},
                 ["xres_o", "hscr", "zscr", "xbcscr", "dtscr"])
    r2 = _launch(build_L2(), shared,
                 {"xbcscr": r1["xbcscr"], "zscr": r1["zscr"]},
                 ["xbcp", "zsil"])
    r3 = _launch(build_L3(), shared,
                 {"xbcp": r2["xbcp"], "zsil": r2["zsil"], "dtscr": r1["dtscr"]},
                 ["vscr", "rstdscr"])
    r4 = _launch(build_L4(), shared,
                 {"vscr": r3["vscr"], "rstdscr": r3["rstdscr"],
                  "hscr": r1["hscr"], "xres_o": r1["xres_o"]},
                 ["h2scr", "xres1"])
    r5 = _launch(build_L5(), shared, {"xres1": r4["xres1"]},
                 ["f2scr", "xres2"])
    r6 = _launch(build_L6(), shared, {"xres2": r5["xres2"]},
                 ["aoscr", "xres3"])
    r7 = _launch(build_L7(), shared, {"xres3": r6["xres3"]}, ["xout"])
    out = np.stack([np.ascontiguousarray(r7["xout"][cid].T) for cid in range(8)])
    return out.astype(np.float32)


# version 04 (final)
# speedup vs baseline: 1.0107x; 1.0107x over previous
import contextlib
import numpy as np
import ml_dtypes
import concourse.bass as bass
import concourse.mybir as mybir
from concourse.bass_utils import run_bass_kernel_spmd
from concourse.alu_op_type import AluOpType

F32 = mybir.dt.float32
BF16 = mybir.dt.bfloat16
I8 = mybir.dt.int8
AF = mybir.ActivationFunctionType
AX = mybir.AxisListType

T = 2048
D = 2048
NKD = 16
EPS = 1e-5

ENGINES = ("tensor", "vector", "scalar", "gpsimd", "sync")


class Prog:

    def __init__(self, nc, es):
        self.nc = nc
        self.es = es
        self.q = {e: [] for e in ENGINES}
        self.cnt = {e: 0 for e in ENGINES}
        self.sems = {e: es.enter_context(nc.semaphore(f"sem_{e}")) for e in ENGINES}
        self.last_wait = {}
        self.streams = []
        self.last_uninc = {e: None for e in ENGINES}

    def op(self, eng, fn, inc=0):
        if inc:
            self.cnt[eng] += inc
            sem = self.sems[eng]

            def wrapped(h, fn=fn, sem=sem, inc=inc):
                fn(h).then_inc(sem, inc)

            self.q[eng].append(wrapped)
            self.last_uninc[eng] = None
            return (sem, self.cnt[eng], eng)
        self.q[eng].append(fn)
        self.last_uninc[eng] = len(self.q[eng]) - 1
        return None

    def stream(self, name):
        s = DmaStream(self, name)
        self.streams.append(s)
        return s

    def wait(self, eng, tok):
        if tok is None:
            return
        sem, val, origin = tok
        if origin == eng:
            return
        key = (eng, id(sem))
        if self.last_wait.get(key, 0) >= val:
            return
        self.last_wait[key] = val
        self.q[eng].append(lambda h, sem=sem, val=val: h.wait_ge(sem, val))

    def waits(self, eng, toks):
        for t in toks:
            self.wait(eng, t)

    def milestone(self, eng):
        self.cnt[eng] += 1
        sem = self.sems[eng]
        prev = self.q[eng][-1]

        def wrapped(h, prev=prev, sem=sem):
            inst = prev(h)
            assert inst is not None, "milestone() must follow a real instruction"
            inst.then_inc(sem, 1)

        self.q[eng][-1] = wrapped
        if self.last_uninc[eng] == len(self.q[eng]) - 1:
            self.last_uninc[eng] = None
        return (sem, self.cnt[eng], eng)

    def finish_streams(self):
        for s in self.streams:
            if s.cnt:
                self.q[s.eng].append(
                    lambda h, sem=s.sem, v=s.cnt: h.wait_ge(sem, v))

    def barrier(self, mode="full"):
        toks = []
        if mode in ("full", "eng"):
            for e in ENGINES:
                idx = self.last_uninc[e]
                if idx is not None:
                    self.cnt[e] += 1
                    sem = self.sems[e]
                    prev = self.q[e][idx]

                    def wrapped(h, prev=prev, sem=sem):
                        inst = prev(h)
                        assert inst is not None, "barrier: not a real op"
                        inst.then_inc(sem, 1)

                    self.q[e][idx] = wrapped
                    self.last_uninc[e] = None
                    toks.append((sem, self.cnt[e], e))
                elif self.cnt[e]:
                    toks.append((self.sems[e], self.cnt[e], e))
        for e in ENGINES:
            if mode in ("full", "stream"):
                for s in self.streams:
                    if s.cnt:
                        key = (e, id(s.sem))
                        if self.last_wait.get(key, 0) < s.cnt:
                            self.last_wait[key] = s.cnt
                            self.q[e].append(
                                lambda h, sem=s.sem, v=s.cnt: h.wait_ge(sem, v))
            for (sem, v, oe) in toks:
                if oe != e:
                    self.last_wait[(e, id(sem))] = v
                    self.q[e].append(
                        lambda h, sem=sem, v=v: h.wait_ge(sem, v))

    def finalize(self):
        with self.nc.Block() as blk:
            for ename in ENGINES:
                deco = getattr(blk, ename)

                def body(h, ename=ename):
                    for fn in self.q[ename]:
                        fn(h)

                deco(body)


class DmaStream:

    def __init__(self, p, name):
        self.p = p
        self.sem = p.es.enter_context(p.nc.semaphore(f"dma_{name}"))
        self.cnt = 0
        self.eng = None

    def dma(self, eng, out, in_, transpose=False):
        p = self.p
        self.eng = eng
        if self.cnt:
            p.q[eng].append(lambda h, sem=self.sem, v=self.cnt: h.wait_ge(sem, v))
        self.cnt += 16
        sem = self.sem

        def fn(h, out=out, in_=in_, sem=sem, transpose=transpose):
            h.dma_start(out, in_, transpose=transpose).then_inc(sem, 16)

        p.q[eng].append(fn)
        return (sem, self.cnt, None)


class Ctx:
    def __init__(self, nc, es):
        self.nc = nc
        self.es = es
        self.alloc_es = es
        self.p = Prog(nc, es)
        self.dram = {}
        self.sb = {}

    def dram_in(self, name, shape, dtype):
        self.dram[name] = self.nc.dram_tensor(name, list(shape), dtype,
                                              kind="ExternalInput").ap()
        return self.dram[name]

    def dram_out(self, name, shape, dtype):
        self.dram[name] = self.nc.dram_tensor(name, list(shape), dtype,
                                              kind="ExternalOutput").ap()
        return self.dram[name]

    def dram_tmp(self, name, shape, dtype):
        self.dram[name] = self.nc.dram_tensor(name, list(shape), dtype,
                                              kind="Internal").ap()
        return self.dram[name]

    def sbuf(self, name, shape, dtype):
        t = self.alloc_es.enter_context(self.nc.sbuf_tensor(
            getattr(self, "name_prefix", "") + name, list(shape), dtype))
        self.sb[name] = t
        return t

    def psum(self, name, shape=(128, 512), dtype=F32):
        return self.alloc_es.enter_context(self.nc.psum_tensor(
            getattr(self, "name_prefix", "") + name, list(shape), dtype))


def emit_consts(c):
    p = c.p
    c.ones_m1 = c.sbuf("ones_m1", [128, 1], BF16)
    c.ones_k1 = c.sbuf("ones_k1", [1, 128], BF16)
    c.cpool = c.sbuf("cpool", [128, 4], F32)
    p.op("gpsimd", lambda h: h.memset(c.ones_m1[:], 1.0))
    p.op("gpsimd", lambda h: h.memset(c.ones_k1[:], 1.0))
    p.op("gpsimd", lambda h: h.memset(c.cpool[:, 0:1], EPS))
    p.op("gpsimd", lambda h: h.memset(c.cpool[:, 1:2], 1.0))
    p.op("gpsimd", lambda h: h.memset(c.cpool[:, 2:3], 0.0))
    p.op("gpsimd", lambda h: h.memset(c.cpool[:, 3:4], -12.0))
    c.const_tok = p.milestone("gpsimd")


def rsqrt_row(c, row_tok, var_row, out_row, n_part=1):
    p = c.p
    p.wait("scalar", row_tok)
    p.wait("scalar", c.const_tok)
    bias = c.cpool[0:n_part, 0:1]
    p.op("scalar", lambda h: h.activation(out_row, var_row, AF.Ln, bias=bias))
    p.op("scalar", lambda h: h.activation(out_row, out_row, AF.Exp, scale=-0.5))
    return p.milestone("scalar")


def bcast_rows(c, pairs, bank, row_tok, free_tok=None, out_free=None):
    p = c.p
    p.wait("tensor", row_tok)
    p.wait("tensor", c.const_tok)
    p.wait("tensor", free_tok)
    p.wait("scalar", out_free)
    ev = None
    for (row, out) in pairs:
        for g in range(T // 512):
            cs = slice(g * 512, (g + 1) * 512)
            p.op("tensor", lambda h, row=row, cs=cs: h.matmul(
                bank[:], c.ones_k1[:], row[0:1, cs], start=True, stop=True))
            mt = p.milestone("tensor")
            p.wait("scalar", mt)
            p.op("scalar", lambda h, out=out, cs=cs: h.copy(out[:, cs], bank[:]))
            ev = p.milestone("scalar")
            p.wait("tensor", ev)
    return ev



def emit_L1(c):
    p = c.p
    xin = c.dram["xin"]
    w_in = c.dram["w_in"]
    ln1w = c.dram["ln1w_col"]
    xres_o = c.dram["xres_o"]
    hscr = c.dram["hscr"]
    zscr = c.dram["zscr"]
    xbcscr = c.dram["xbcscr"]
    dtscr = c.dram["dtscr"]

    rmsh = c.sbuf("rmsh", [128, NKD * T], BF16)
    xsl = [c.sbuf(f"xsl{i}", [128, T], BF16) for i in range(2)]
    sqsl = [c.sbuf(f"sqsl{i}", [128, T], BF16) for i in range(2)]
    mu_row = c.sbuf("mu_row", [1, T], F32)
    var_row = c.sbuf("var_row", [1, T], F32)
    r_row = c.sbuf("r_row", [1, T], F32)
    row_bf = c.sbuf("row_bf", [1, T], BF16)
    mu_b = c.sbuf("mu_b", [128, T], BF16)
    r_b = c.sbuf("r_b", [128, T], BF16)
    rr_b = c.sbuf("rr_b", [128, T], BF16)
    ln1w_sb = c.sbuf("ln1w_sb", [128, NKD], F32)

    bS = [c.psum(f"bS{i}") for i in range(4)]
    bB = c.psum("bB")
    bM = [c.psum(f"bM{i}") for i in range(2)]

    s1t = [bS[0][0:1, :], bS[0][64:65, :], bS[1][0:1, :], bS[1][64:65, :]]
    s2t = [bS[2][0:1, :], bS[2][64:65, :], bS[3][0:1, :], bS[3][64:65, :]]

    s_x = [p.stream("x0"), p.stream("x1")]
    s_st = [p.stream("st0"), p.stream("st1")]
    s_h = [p.stream("h0"), p.stream("h1")]
    s_w = [p.stream("w0"), p.stream("w1")]
    s_stage = [p.stream("sg0"), p.stream("sg1")]
    s_misc = p.stream("misc")
    lw_tok = s_misc.dma("gpsimd", ln1w_sb[:], ln1w[:, :])

    ld = [None] * NKD
    st = [None] * NKD
    mmA = [None] * NKD
    sqv = [None] * NKD
    for k in range(NKD):
        sl = k % 2
        if k >= 2:
            p.wait("gpsimd", st[k - 2])
            p.wait("gpsimd", mmA[k - 2])
            p.wait("gpsimd", sqv[k - 2])
        ld[k] = s_x[sl].dma("gpsimd", xsl[sl][:], xin[k * 128:(k + 1) * 128, :])
        p.wait("sync", ld[k])
        st[k] = s_st[sl].dma("sync", xres_o[k * 128:(k + 1) * 128, :], xsl[sl][:])
        p.wait("vector", ld[k])
        if k >= 2:
            p.wait("vector", mmA[k - 2])
        p.op("vector", lambda h, sl=sl: h.tensor_tensor(
            sqsl[sl][:], xsl[sl][:], xsl[sl][:], AluOpType.mult))
        sqv[k] = p.milestone("vector")
        p.wait("tensor", ld[k])
        p.wait("tensor", sqv[k])
        if k == 0:
            p.wait("tensor", c.const_tok)
        for g in range(4):
            cs = slice(g * 512, (g + 1) * 512)
            p.op("tensor", lambda h, k=k, g=g, sl=sl, cs=cs: h.matmul(
                s1t[g], c.ones_m1[:], xsl[sl][:, cs],
                start=(k == 0), stop=(k == NKD - 1)))
            p.op("tensor", lambda h, k=k, g=g, sl=sl, cs=cs: h.matmul(
                s2t[g], c.ones_m1[:], sqsl[sl][:, cs],
                start=(k == 0), stop=(k == NKD - 1)))
        mmA[k] = p.milestone("tensor")

    p.wait("vector", mmA[NKD - 1])
    s1row = c.sbuf("s1row", [1, T], F32)
    s2row = c.sbuf("s2row", [1, T], F32)
    for g in range(4):
        cs = slice(g * 512, (g + 1) * 512)
        p.op("vector", lambda h, g=g, cs=cs: h.tensor_copy(s1row[0:1, cs], s1t[g]))
        p.op("vector", lambda h, g=g, cs=cs: h.tensor_copy(s2row[0:1, cs], s2t[g]))
    stats_free = p.milestone("vector")
    inv = 1.0 / D
    p.op("vector", lambda h: h.tensor_scalar(mu_row[:], s1row[:], inv, None, AluOpType.mult))
    p.op("vector", lambda h: h.tensor_scalar(var_row[:], s2row[:], inv, None, AluOpType.mult))
    p.op("vector", lambda h: h.tensor_tensor(s1row[:], mu_row[:], mu_row[:], AluOpType.mult))
    p.op("vector", lambda h: h.tensor_tensor(var_row[:], var_row[:], s1row[:], AluOpType.subtract))
    rowv = p.milestone("vector")
    rs = rsqrt_row(c, rowv, var_row[:], r_row[:])
    p.wait("vector", rs)
    p.op("vector", lambda h: h.tensor_copy(row_bf[:], mu_row[:]))
    t1 = p.milestone("vector")
    b1 = bcast_rows(c, [(row_bf[:], mu_b)], bB, t1)
    p.wait("vector", b1)
    p.op("vector", lambda h: h.tensor_copy(row_bf[:], r_row[:]))
    t2 = p.milestone("vector")
    b2 = bcast_rows(c, [(row_bf[:], r_b)], bB, t2, b1)

    mmB = [None] * NKD
    happ = [None] * NKD
    hst = [None] * NKD
    for k in range(NKD):
        sl = k % 2
        p.wait("gpsimd", st[NKD - 2 + sl])
        if k < 2:
            p.wait("gpsimd", mmA[NKD - 2 + k])
            p.wait("gpsimd", sqv[NKD - 2 + k])
        else:
            p.wait("gpsimd", mmB[k - 2])
            p.wait("gpsimd", hst[k - 2])
        ld2 = s_x[sl].dma("gpsimd", xsl[sl][:], xres_o[k * 128:(k + 1) * 128, :])
        p.wait("vector", ld2)
        p.wait("vector", b2)
        if k >= 2:
            p.wait("vector", mmB[k - 2])
        hv = sqsl[sl]
        if k >= 2:
            p.wait("vector", hst[k - 2])
        p.op("vector", lambda h, sl=sl, hv=hv: h.tensor_tensor(
            hv[:], xsl[sl][:], mu_b[:], AluOpType.subtract))
        p.op("vector", lambda h, hv=hv: h.tensor_tensor(
            hv[:], hv[:], r_b[:], AluOpType.mult))
        happ[k] = p.milestone("vector")
        p.wait("sync", happ[k])
        hst[k] = s_h[sl].dma("sync", hscr[k * 128:(k + 1) * 128, :], hv[:])
        p.op("vector", lambda h, sl=sl, hv=hv: h.tensor_tensor(
            xsl[sl][:], hv[:], hv[:], AluOpType.mult))
        sq2 = p.milestone("vector")
        p.wait("tensor", sq2)
        if k == 0:
            p.wait("tensor", stats_free)
        for g in range(4):
            cs = slice(g * 512, (g + 1) * 512)
            p.op("tensor", lambda h, k=k, g=g, sl=sl, cs=cs: h.matmul(
                s1t[g], c.ones_m1[:], xsl[sl][:, cs],
                start=(k == 0), stop=(k == NKD - 1)))
        mmB[k] = p.milestone("tensor")

    p.wait("vector", mmB[NKD - 1])
    for g in range(4):
        cs = slice(g * 512, (g + 1) * 512)
        p.op("vector", lambda h, g=g, cs=cs: h.tensor_copy(s2row[0:1, cs], s1t[g]))
    p.op("vector", lambda h: h.tensor_scalar(var_row[:], s2row[:], inv, None, AluOpType.mult))
    rowv2 = p.milestone("vector")
    rs2 = rsqrt_row(c, rowv2, var_row[:], r_row[:])
    p.wait("vector", rs2)
    p.op("vector", lambda h: h.tensor_copy(row_bf[:], r_row[:]))
    t3 = p.milestone("vector")
    b3 = bcast_rows(c, [(row_bf[:], rr_b)], bB, t3, b2)

    rmsh_done = [None] * NKD
    for k in range(NKD):
        sl = k % 2
        if k >= 2:
            p.wait("gpsimd", rmsh_done[k - 2])
        else:
            p.wait("gpsimd", mmB[NKD - 2 + k])
        p.wait("gpsimd", hst[k])
        ld3 = s_x[sl].dma("gpsimd", xsl[sl][:], hscr[k * 128:(k + 1) * 128, :])
        if k < 2:
            p.wait("vector", hst[NKD - 2 + k])
        p.waits("vector", [ld3, b3, lw_tok])
        rv = rmsh[:, k * T:(k + 1) * T]
        p.op("vector", lambda h, sl=sl, k=k: h.tensor_scalar(
            sqsl[sl][:], xsl[sl][:], ln1w_sb[:, k:k + 1], None, AluOpType.mult))
        p.op("vector", lambda h, sl=sl, rv=rv: h.tensor_tensor(
            rv, sqsl[sl][:], rr_b[:], AluOpType.mult))
        rmsh_done[k] = p.milestone("vector")

    wbuf = [c.sbuf(f"wbuf{i}", [128, NKD * 512], BF16) for i in range(2)]
    stage = c.sbuf("stage", [128, 16 * 512], BF16)
    wtok = [None, None]
    wfree = [None, None]
    pb_free = [None, None]
    stage_free = [None, None]
    rmsh_all = rmsh_done[NKD - 1]

    def load_wblk(b):
        sl = b % 2
        p.wait("gpsimd", wfree[sl])
        src = w_in[:, b * 512:(b + 1) * 512].rearrange("(k p) m -> p k m", p=128)
        dst = wbuf[sl][:, :].rearrange("p (k m) -> p k m", k=NKD)
        wtok[sl] = s_w[sl].dma("gpsimd", dst, src)

    load_wblk(0)
    load_wblk(1)
    for b in range(18):
        sl = b % 2
        p.wait("tensor", wtok[sl])
        p.wait("tensor", rmsh_all)
        if b < 8:
            for half in range(2):
                soff = half * 4096
                p.wait("scalar", stage_free[half])
                for mi in range(8):
                    m = half * 8 + mi
                    pb = bM[m % 2]
                    p.wait("tensor", pb_free[m % 2])
                    for k in range(NKD):
                        p.op("tensor", lambda h, m=m, k=k, sl=sl, pb=pb: h.matmul(
                            pb[:], rmsh[:, k * T + m * 128: k * T + (m + 1) * 128],
                            wbuf[sl][:, k * 512:(k + 1) * 512],
                            start=(k == 0), stop=(k == NKD - 1)))
                    mt = p.milestone("tensor")
                    p.wait("scalar", mt)
                    p.op("scalar", lambda h, mi=mi, pb=pb, soff=soff: h.copy(
                        stage[:, soff + mi * 512:soff + (mi + 1) * 512], pb[:]))
                    pb_free[m % 2] = p.milestone("scalar")
                p.wait("sync", pb_free[0])
                p.wait("sync", pb_free[1])
                dst = zscr[half * 1024:(half + 1) * 1024,
                           b * 512:(b + 1) * 512].rearrange("(m p) z -> p m z", p=128)
                ssrc = stage[:, soff:soff + 4096].rearrange("p (m z) -> p m z", m=8)
                stok = s_stage[half].dma("sync", dst, ssrc)
                stage_free[half] = stok
        else:
            xb = b - 8
            for mt_i in range(4):
                half = mt_i % 2
                soff = half * 4096
                p.wait("scalar", stage_free[half])
                for g in range(4):
                    idx = mt_i * 4 + g
                    pb = bM[idx % 2]
                    p.wait("tensor", pb_free[idx % 2])
                    for k in range(NKD):
                        p.op("tensor", lambda h, mt_i=mt_i, g=g, k=k, sl=sl, pb=pb:
                             h.matmul(pb[:],
                                      wbuf[sl][:, k * 512 + mt_i * 128:
                                               k * 512 + (mt_i + 1) * 128],
                                      rmsh[:, k * T + g * 512: k * T + (g + 1) * 512],
                                      start=(k == 0), stop=(k == NKD - 1)))
                    mt = p.milestone("tensor")
                    p.wait("scalar", mt)
                    p.op("scalar", lambda h, g=g, pb=pb, soff=soff: h.copy(
                        stage[:, soff + g * 512:soff + (g + 1) * 512], pb[:]))
                    pb_free[idx % 2] = p.milestone("scalar")
                p.wait("sync", pb_free[0])
                p.wait("sync", pb_free[1])
                r0 = xb * 512 + mt_i * 128
                dst = xbcscr[r0:r0 + 128, 3:3 + T]
                stok = s_stage[half].dma("sync", dst, stage[:, soff:soff + T])
                stage_free[half] = stok
        wfree[sl] = mt
        if b + 2 < 18:
            load_wblk(b + 2)

    wdt = c.sbuf("wdt", [128, NKD * 64], BF16)
    dtst = c.sbuf("dtst", [128, 16 * 64], F32)
    src = w_in[:, 9216:9280].rearrange("(k p) m -> p k m", p=128)
    wdst = wdt[:, :].rearrange("p (k m) -> p k m", k=NKD)
    dt_ld = s_misc.dma("gpsimd", wdst, src)
    p.wait("tensor", dt_ld)
    for m in range(16):
        pb = bM[m % 2]
        p.wait("tensor", pb_free[m % 2])
        for k in range(NKD):
            p.op("tensor", lambda h, m=m, k=k, pb=pb: h.matmul(
                pb[:, 0:64], rmsh[:, k * T + m * 128: k * T + (m + 1) * 128],
                wdt[:, k * 64:(k + 1) * 64],
                start=(k == 0), stop=(k == NKD - 1)))
        mt = p.milestone("tensor")
        p.wait("scalar", mt)
        p.op("scalar", lambda h, m=m, pb=pb: h.copy(
            dtst[:, m * 64:(m + 1) * 64], pb[:, 0:64]))
        pb_free[m % 2] = p.milestone("scalar")
    p.wait("sync", pb_free[0])
    p.wait("sync", pb_free[1])
    p.stream("dtout").dma("sync", dtscr.rearrange("(m p) hh -> p m hh", p=128),
          dtst[:, :].rearrange("p (m hh) -> p m hh", m=16))

    zhalo = c.sbuf("zhalo", [128, 40 * 3], BF16)
    htok = p.op("gpsimd", lambda h: h.memset(zhalo[:], 0.0), inc=1)
    p.wait("sync", htok)
    p.stream("halo").dma("sync", xbcscr[:, 0:3].rearrange("(ct p) t -> p ct t", p=128),
          zhalo[:, :].rearrange("p (ct t) -> p ct t", ct=40))



def emit_L2(c):
    p = c.p
    xbcscr = c.dram["xbcscr"]
    xbcp = c.dram["xbcp"]
    zscr = c.dram["zscr"]
    zsil = c.dram["zsil"]
    convw = c.dram["convw_sb"]
    convb = c.dram["convb_sb"]

    cw = c.sbuf("cw", [128, 160], F32)
    cb = c.sbuf("cb", [128, 40], F32)
    insl = [c.sbuf(f"cin{i}", [128, T + 3], BF16) for i in range(2)]
    outsl = [c.sbuf(f"cout{i}", [128, T], BF16) for i in range(2)]
    accs = [c.sbuf(f"cacc{i}", [128, T], F32) for i in range(2)]
    ctmp = c.sbuf("ctmp", [128, T], F32)
    zslb = [c.sbuf(f"zsl{i}", [128, 4096], BF16) for i in range(2)]

    s_cw = p.stream("cw")
    s_in = [p.stream("ci0"), p.stream("ci1")]
    s_out = [p.stream("co0"), p.stream("co1")]
    s_zi = [p.stream("zi0"), p.stream("zi1")]
    s_zo = [p.stream("zo0"), p.stream("zo1")]

    cw_tok = s_cw.dma("gpsimd", cw[:], convw[:, :])
    cb_tok = s_cw.dma("gpsimd", cb[:], convb[:, :])

    vtok = [None, None]
    stok = [None, None]
    sctok = [None, None]
    for ct in range(40):
        sl = ct % 2
        p.wait("gpsimd", stok[sl])
        p.wait("gpsimd", vtok[sl])
        ld = s_in[sl].dma("gpsimd", insl[sl][:],
                          xbcscr[ct * 128:(ct + 1) * 128, :])
        p.waits("vector", [ld, cw_tok, cb_tok])
        p.wait("vector", sctok[sl])
        acc = accs[sl]
        p.op("vector", lambda h, sl=sl, ct=ct, acc=acc: h.tensor_scalar(
            acc[:], insl[sl][:, 0:T], cw[:, ct * 4:ct * 4 + 1], None, AluOpType.mult))
        for j in range(1, 4):
            p.op("vector", lambda h, sl=sl, ct=ct, j=j: h.tensor_scalar(
                ctmp[:], insl[sl][:, j:j + T], cw[:, ct * 4 + j:ct * 4 + j + 1],
                None, AluOpType.mult))
            p.op("vector", lambda h, acc=acc: h.tensor_tensor(
                acc[:], acc[:], ctmp[:], AluOpType.add))
        vtok[sl] = p.milestone("vector")
        p.wait("scalar", vtok[sl])
        p.wait("scalar", stok[sl])
        p.op("scalar", lambda h, sl=sl, ct=ct, acc=acc: h.activation(
            outsl[sl][:], acc[:], AF.Silu, bias=cb[:, ct:ct + 1]))
        sctok[sl] = p.milestone("scalar")
        p.wait("sync", sctok[sl])
        stok[sl] = s_out[sl].dma("sync", xbcp[ct * 128:(ct + 1) * 128, :], outsl[sl][:])

    zvt = [None, None]
    zst = [None, None]
    for m in range(16):
        sl = m % 2
        p.wait("gpsimd", zst[sl])
        p.wait("gpsimd", zvt[sl])
        ld = s_zi[sl].dma("gpsimd", zslb[sl][:], zscr[m * 128:(m + 1) * 128, :])
        p.wait("scalar", ld)
        p.op("scalar", lambda h, sl=sl: h.activation(
            zslb[sl][:], zslb[sl][:], AF.Silu))
        zvt[sl] = p.milestone("scalar")
        p.wait("sync", zvt[sl])
        zst[sl] = s_zo[sl].dma("sync", zsil[m * 128:(m + 1) * 128, :], zslb[sl][:])



CH = 128
NC_ = T // CH
NH = 64
HD = 64


def emit_L3(c):
    p = c.p
    xbcp = c.dram["xbcp"]
    zsil = c.dram["zsil"]
    dtscr = c.dram["dtscr"]
    vscr = c.dram["vscr"]
    rstdscr = c.dram["rstdscr"]
    tmask = c.dram["tmask_f"]
    masklT = c.dram["masklT"]
    ident_bf = c.dram["ident_bf"]
    ident_f = c.dram["ident_f32"]
    rows = c.dram["hrows"]

    tmsk = c.sbuf("tmsk", [128, 128], F32)
    mskl = c.sbuf("mskl", [128, 128], F32)
    idb = c.sbuf("idb", [128, 128], BF16)
    idf = c.sbuf("idf", [128, 128], F32)
    hrows = c.sbuf("hrows_sb", [1, 192], F32)
    dtr = c.sbuf("dtr", [128, NC_ * 64], F32)
    dtsp = c.sbuf("dtsp", [128, NC_ * 64], F32)
    t1 = c.sbuf("t1g", [128, NC_ * 64], F32)
    t2 = c.sbuf("t2g", [128, NC_ * 64], F32)
    acum = c.sbuf("acum", [128, NC_ * 64], F32)
    nacum = c.sbuf("nacum", [128, NC_ * 64], F32)
    dste = c.sbuf("dste", [128, NC_ * 64], F32)
    outd = c.sbuf("outd", [128, NC_ * 64], F32)
    comb = c.sbuf("comb", [128, NC_ * 64], F32)
    ace = c.sbuf("ace", [1, NC_ * 64], F32)
    acer = c.sbuf("acer", [128, NC_ * 64], F32)
    cdh = c.sbuf("cdh", [128, NC_ * 64], F32)
    dtb_rep = c.sbuf("dtb_rep", [128, 64], F32)
    negA = c.sbuf("negA", [128, 64], F32)
    d_rep = c.sbuf("d_rep", [128, 64], F32)
    prev = c.sbuf("prev", [64, 4096], F32)
    prevbf = c.sbuf("prevbf", [64, 4096], BF16)
    rstdt = c.sbuf("rstdt", [128, NC_], F32)
    vsq = c.sbuf("vsq", [128, 1], F32)
    vpad = c.sbuf("vpad", [128, 1], F32)

    sxbc = [c.sbuf(f"sxbc{i}", [128, 4 * 128], BF16) for i in range(2)]
    sB = [c.sbuf(f"sB{i}", [64, 8 * 128], BF16) for i in range(2)]
    sC = [c.sbuf(f"sC{i}", [64, 64 * 128], BF16) for i in range(2)]
    szs = [c.sbuf(f"szs{i}", [128, 4096], BF16) for i in range(2)]
    vout = [c.sbuf(f"vout{i}", [128, 4096], BF16) for i in range(2)]
    xtm = c.sbuf("xtm", [128, 512], BF16)
    btm = c.sbuf("btm", [128, 512], BF16)
    xd8 = c.sbuf("xd8", [128, 512], BF16)
    dxd8 = c.sbuf("dxd8", [128, 512], BF16)
    dx8 = c.sbuf("dx8", [128, 512], BF16)
    yo8 = c.sbuf("yo8", [128, 512], F32)
    mrg = c.sbuf("mrg", [128, 512], F32)
    sq8 = c.sbuf("sq8", [128, 4096], BF16)
    lts = [c.sbuf(f"lts{i}", [128, 128], F32) for i in range(2)]
    ltb = [c.sbuf(f"ltb{i}", [128, 128], BF16) for i in range(2)]
    gl = c.sbuf("gl", [128, 8 * 128], BF16)

    b0 = c.psum("s_b0")
    b1 = c.psum("s_b1")
    bt = c.psum("s_bt", (128, 1024), BF16)
    bg = [c.psum("s_bg0"), c.psum("s_bg1")]
    bl = [c.psum("s_bl0"), c.psum("s_bl1")]
    bs = c.psum("s_bs")

    s_ld = p.stream("l3w")
    s_x = [p.stream("l3x0"), p.stream("l3x1")]
    s_z = [p.stream("l3z0"), p.stream("l3z1")]
    s_v = [p.stream("l3v0"), p.stream("l3v1")]

    for (dst, src) in ((tmsk, tmask), (mskl, masklT), (idb, ident_bf),
                       (idf, ident_f), (hrows, rows)):
        ldt = s_ld.dma("gpsimd", dst[:], src[:, :])
    ld_dt = s_ld.dma("gpsimd",
                     dtr[:, :].rearrange("p (m hh) -> p m hh", m=NC_),
                     dtscr.rearrange("(m p) hh -> p m hh", p=128))
    p.op("gpsimd", lambda h: h.memset(prev[:], 0.0))
    pz_tok = p.milestone("gpsimd")

    p.waits("tensor", [ldt, c.const_tok])
    for (row_i, dst) in ((0, dtb_rep), (1, negA), (2, d_rep)):
        p.op("tensor", lambda h, row_i=row_i: h.matmul(
            b0[:, 0:64], c.ones_k1f[:], hrows[0:1, row_i * 64:(row_i + 1) * 64],
            start=True, stop=True))
        mt = p.milestone("tensor")
        p.wait("scalar", mt)
        p.op("scalar", lambda h, dst=dst: h.copy(dst[:], b0[:, 0:64]))
        ev = p.milestone("scalar")
        p.wait("tensor", ev)
    p.op("scalar", lambda h: h.activation(negA[:], negA[:], AF.Exp))
    sc0 = p.milestone("scalar")
    p.wait("vector", sc0)
    p.op("vector", lambda h: h.tensor_scalar(negA[:], negA[:], -1.0, None, AluOpType.mult))
    p.wait("vector", ld_dt)
    dtbB = dtb_rep[:, :].unsqueeze(1).broadcast_to([128, NC_, 64])
    p.op("vector", lambda h: h.tensor_tensor(
        t1[:, :].rearrange("p (m hh) -> p m hh", m=NC_),
        dtr[:, :].rearrange("p (m hh) -> p m hh", m=NC_), dtbB, AluOpType.add))
    vt1 = p.milestone("vector")
    p.wait("scalar", vt1)
    p.op("scalar", lambda h: h.activation(t2[:], t1[:], AF.Abs))
    p.op("scalar", lambda h: h.activation(t2[:], t2[:], AF.Exp, scale=-1.0))
    p.op("scalar", lambda h: h.activation(
        t2[:], t2[:], AF.Ln, bias=c.cpool[0:128, 1:2]))
    sc1 = p.milestone("scalar")
    p.wait("vector", sc1)
    p.op("vector", lambda h: h.tensor_scalar(
        dtsp[:], t1[:], 0.0, None, AluOpType.max))
    p.op("vector", lambda h: h.tensor_tensor(dtsp[:], dtsp[:], t2[:], AluOpType.add))
    negAB = negA[:, :].unsqueeze(1).broadcast_to([128, NC_, 64])
    p.op("vector", lambda h: h.tensor_tensor(
        t1[:, :].rearrange("p (m hh) -> p m hh", m=NC_),
        dtsp[:, :].rearrange("p (m hh) -> p m hh", m=NC_), negAB, AluOpType.mult))
    da_tok = p.milestone("vector")
    acum_toks = []
    p.wait("tensor", da_tok)
    for cc in range(NC_):
        csl = slice(cc * 64, (cc + 1) * 64)
        p.op("tensor", lambda h, csl=csl: h.matmul(
            b0[:, 0:64], tmsk[:], t1[:, csl], start=True, stop=True))
        p.op("tensor", lambda h, csl=csl: h.matmul(
            b1[0:1, 0:64], c.ones_m1f[:], t1[:, csl], start=True, stop=True))
        mt = p.milestone("tensor")
        p.wait("scalar", mt)
        ta = p.op("scalar", lambda h, csl=csl: h.copy(acum[:, csl], b0[:, 0:64]), inc=1)
        ev = p.op("scalar", lambda h, csl=csl: h.copy(ace[0:1, csl], b1[0:1, 0:64]), inc=1)
        p.wait("tensor", ev)
        p.wait("gpsimd", ta)
        acum_toks.append(ta)
    evl = None
    for g2 in range(2):
        cs = slice(g2 * 512, (g2 + 1) * 512)
        p.op("tensor", lambda h, cs=cs: h.matmul(
            b0[:], c.ones_k1f[:], ace[0:1, cs], start=True, stop=True))
        mt = p.milestone("tensor")
        p.wait("scalar", mt)
        p.op("scalar", lambda h, cs=cs: h.copy(acer[:, cs], b0[:]))
        evl = p.milestone("scalar")
        p.wait("tensor", evl)
    p.wait("vector", evl)
    t_nacum = p.op("vector", lambda h: h.tensor_scalar(
        nacum[:], acum[:], -1.0, None, AluOpType.mult), inc=1)
    t_dsub = p.op("vector", lambda h: h.tensor_tensor(
        dste[:], acer[:], acum[:], AluOpType.subtract), inc=1)
    p.wait("scalar", t_dsub)
    t_dste = p.op("scalar", lambda h: h.activation(dste[:], dste[:], AF.Exp), inc=1)
    t_outd = p.op("scalar", lambda h: h.activation(outd[:], acum[:], AF.Exp), inc=1)
    t_cdh = p.op("scalar", lambda h: h.activation(cdh[:], acer[:], AF.Exp), inc=1)
    p.wait("vector", t_dste)
    pro_tok = p.op("vector", lambda h: h.tensor_tensor(
        comb[:], dtsp[:], dste[:], AluOpType.mult), inc=1)
    p.wait("gpsimd", t_nacum)
    p.wait("gpsimd", t_outd)
    p.wait("gpsimd", t_cdh)

    xtok = [None, None]
    ztok = [None, None]
    vtok = [None, None]
    lts_free = [None, None]
    ltb_free = [None, None]
    pY_free = None
    pYo_free = None
    bt_free = None
    bs_free = None
    bg_free = [None, None]
    bl_free = [None, None]
    for cc in range(NC_):
        sl = cc % 2
        p.wait("gpsimd", xtok[sl])
        ldx = s_x[sl].dma("gpsimd", sxbc[sl][:, :].rearrange(
            "p (ct t) -> p ct t", ct=4),
            xbcp[0:512, cc * 128:(cc + 1) * 128].rearrange("(ct p) t -> p ct t", p=128))
        ldb = s_x[sl].dma("gpsimd", sB[sl][:, :].rearrange("n (g t) -> n g t", g=8),
            xbcp[512:1024, cc * 128:(cc + 1) * 128].rearrange("(g n) t -> n g t", n=64))
        ldc = s_x[sl].dma("gpsimd", sC[sl][:, :].rearrange("n (hd t) -> n hd t", hd=64),
            xbcp[1024:5120, cc * 128:(cc + 1) * 128].rearrange("(hd n) t -> n hd t", n=64))
        p.wait("gpsimd", ztok[sl])
        ldz = s_z[sl].dma("gpsimd", szs[sl][:], zsil[cc * 128:(cc + 1) * 128, :])
        p.waits("tensor", [ldx, ldb, ldc, pz_tok, pro_tok, bt_free])
        p.waits("vector", [t_outd, t_cdh, pro_tok])
        p.waits("scalar", [t_nacum])
        for i in range(4):
            p.op("tensor", lambda h, sl=sl, i=i: h.transpose(
                bt[:, i * 128:(i + 1) * 128],
                sxbc[sl][:, i * 128:(i + 1) * 128], idb[:]))
        for g in range(8):
            p.op("tensor", lambda h, sl=sl, g=g: h.transpose(
                bt[:, 512 + g * 64:512 + (g + 1) * 64],
                sB[sl][:, g * 128:(g + 1) * 128], idb[0:64, 0:64]))
        tpt = p.milestone("tensor")
        p.wait("vector", tpt)
        for i in range(4):
            p.op("vector", lambda h, i=i: h.tensor_copy(
                xtm[:, i * 128:(i + 1) * 128], bt[:, i * 128:(i + 1) * 128]))
        btm_tok = p.op("vector", lambda h: h.tensor_copy(btm[:], bt[:, 512:1024]),
                       inc=1)
        bt_free = btm_tok
        p.op("vector", lambda h: h.tensor_copy(prevbf[:], prev[:]))
        pbf_tok = p.milestone("vector")

        vo = vout[sl]
        p.wait("vector", vtok[sl])
        p.wait("vector", pz_tok)
        for g in range(8):
            hs0 = cc * 64 + g * 8
            xg = xtm[:, g * 64:(g + 1) * 64]
            xgB = xg.unsqueeze(1).broadcast_to([128, 8, 64])
            dtB = dtsp[:, hs0:hs0 + 8].unsqueeze(2).broadcast_to([128, 8, 64])
            cbB = comb[:, hs0:hs0 + 8].unsqueeze(2).broadcast_to([128, 8, 64])
            ddB = d_rep[:, g * 8:g * 8 + 8].unsqueeze(2).broadcast_to([128, 8, 64])
            odB = outd[:, hs0:hs0 + 8].unsqueeze(2).broadcast_to([128, 8, 64])
            cdB64 = cdh[0:64, hs0:hs0 + 8].unsqueeze(2).broadcast_to([64, 8, 64])
            r3 = lambda ap: ap.rearrange("p (r i) -> p r i", r=8)
            xd8_tok = p.op("vector", lambda h, xgB=xgB, dtB=dtB: h.tensor_tensor(
                r3(xd8[:, :]), xgB, dtB, AluOpType.mult), inc=1)
            dxd_tok = p.op("vector", lambda h, xgB=xgB, cbB=cbB: h.tensor_tensor(
                r3(dxd8[:, :]), xgB, cbB, AluOpType.mult), inc=1)
            xd_tok = p.op("vector", lambda h, xgB=xgB, ddB=ddB: h.tensor_tensor(
                r3(dx8[:, :]), xgB, ddB, AluOpType.mult), inc=1)
            for hh in range(8):
                hd = g * 8 + hh
                bgs = bg[hh % 2]
                p.wait("tensor", bg_free[hh % 2])
                p.op("tensor", lambda h, sl=sl, g=g, hd=hd, bgs=bgs: h.matmul(
                    bgs[:, 0:128],
                    sB[sl][:, g * 128:(g + 1) * 128],
                    sC[sl][:, hd * 128:(hd + 1) * 128],
                    start=True, stop=True))
                gmt = p.milestone("tensor")
                lsl = hh % 2
                p.wait("gpsimd", lts_free[lsl])
                p.wait("gpsimd", pro_tok)
                col = cc * 64 + hd
                p.op("gpsimd", lambda h, lsl=lsl, col=col: h.tensor_scalar(
                    lts[lsl][:], mskl[:], acum[:, col:col + 1], 1.0,
                    AluOpType.add, AluOpType.mult))
                lt1 = p.milestone("gpsimd")
                p.waits("tensor", [lt1, bl_free[lsl]])
                p.op("tensor", lambda h, lsl=lsl: h.transpose(
                    bl[lsl][:, 0:128], lts[lsl][:], idf[:]))
                ltt = p.milestone("tensor")
                lts_free[lsl] = ltt
                p.wait("scalar", ltt)
                p.wait("scalar", ltb_free[lsl])
                p.op("scalar", lambda h, lsl=lsl, col=col: h.activation(
                    ltb[lsl][:], bl[lsl][:, 0:128], AF.Exp,
                    bias=nacum[:, col:col + 1]))
                lte = p.milestone("scalar")
                bl_free[lsl] = lte
                p.waits("vector", [gmt, lte])
                p.op("vector", lambda h, hh=hh, bgs=bgs, lsl=lsl: h.tensor_tensor(
                    gl[:, hh * 128:(hh + 1) * 128], bgs[:, 0:128], ltb[lsl][:],
                    AluOpType.mult))
                gltok = p.milestone("vector")
                bg_free[hh % 2] = gltok
                ltb_free[lsl] = gltok
            p.waits("tensor", [xd8_tok, dxd_tok, bs_free, btm_tok])
            p.op("tensor", lambda h, g=g: h.matmul(
                bs[0:64, :], btm[:, g * 64:(g + 1) * 64], dxd8[:],
                start=True, stop=True))
            smt = p.milestone("tensor")
            p.waits("tensor", [pY_free, pYo_free, pbf_tok, gltok, xd8_tok])
            for hh in range(8):
                hd = g * 8 + hh
                p.op("tensor", lambda h, hh=hh: h.matmul(
                    b0[:, hh * 64:(hh + 1) * 64], gl[:, hh * 128:(hh + 1) * 128],
                    xd8[:, hh * 64:(hh + 1) * 64], start=True, stop=True))
                p.op("tensor", lambda h, sl=sl, hd=hd, hh=hh: h.matmul(
                    b1[:, hh * 64:(hh + 1) * 64],
                    sC[sl][:, hd * 128:(hd + 1) * 128],
                    prevbf[0:64, hd * 64:(hd + 1) * 64], start=True, stop=True))
            ymt = p.milestone("tensor")
            p.waits("vector", [ymt, ldz])
            p.op("vector", lambda h, odB=odB: h.tensor_tensor(
                r3(yo8[:, :]), b1[:, :].rearrange("p (r i) -> p r i", r=8), odB,
                AluOpType.mult))
            p.op("vector", lambda h: h.tensor_tensor(
                mrg[:], b0[:], yo8[:], AluOpType.add))
            pY_free = p.milestone("vector")
            pYo_free = pY_free
            p.op("vector", lambda h: h.tensor_tensor(
                mrg[:], mrg[:], dx8[:], AluOpType.add))
            p.op("vector", lambda h, sl=sl, g=g, vo=vo: h.tensor_tensor(
                vo[:, g * 512:(g + 1) * 512], mrg[:],
                szs[sl][:, g * 512:(g + 1) * 512], AluOpType.mult))
            vmg = p.milestone("vector")
            p.op("vector", lambda h, g=g, cdB64=cdB64: h.tensor_tensor(
                prev[0:64, g * 512:(g + 1) * 512].rearrange("p (r i) -> p r i", r=8),
                prev[0:64, g * 512:(g + 1) * 512].rearrange("p (r i) -> p r i", r=8),
                cdB64, AluOpType.mult))
            p.wait("vector", smt)
            p.op("vector", lambda h, g=g: h.tensor_tensor(
                prev[:, g * 512:(g + 1) * 512], prev[:, g * 512:(g + 1) * 512],
                bs[0:64, :], AluOpType.add))
            bs_free = p.milestone("vector")
        p.op("vector", lambda h, vo=vo: h.tensor_tensor(
            sq8[:], vo[:], vo[:], AluOpType.mult))
        p.op("vector", lambda h: h.tensor_reduce(
            vsq[:], sq8[:], axis=AX.X, op=AluOpType.add))
        p.op("vector", lambda h: h.tensor_copy(vpad[:], vsq[:]))
        rt = p.milestone("vector")
        p.wait("scalar", rt)
        p.op("scalar", lambda h, cc=cc: h.activation(
            rstdt[:, cc:cc + 1], vsq[:], AF.Ln, bias=c.cpool[0:128, 0:1],
            scale=1.0 / 4096.0))
        p.op("scalar", lambda h, cc=cc: h.activation(
            rstdt[:, cc:cc + 1], rstdt[:, cc:cc + 1], AF.Exp, scale=-0.5))
        p.wait("sync", vmg)
        vtok[sl] = s_v[sl].dma("sync", vscr[cc * 128:(cc + 1) * 128, :], vo[:])
        xtok[sl] = ymt
        ztok[sl] = vmg
    rfin = p.milestone("scalar")
    p.waits("tensor", [rfin, bl_free[0]])
    p.op("tensor", lambda h: h.transpose(bl[0][0:16, 0:128], rstdt[:], idf[:]))
    rtt = p.milestone("tensor")
    rrow = c.sbuf("rrow", [16, 128], F32)
    p.wait("scalar", rtt)
    p.op("scalar", lambda h: h.copy(rrow[:], bl[0][0:16, 0:128]))
    rev = p.milestone("scalar")
    p.wait("sync", rev)
    p.stream("rstd").dma("sync", rstdscr.rearrange("(c p) -> c p", p=128), rrow[:])



def colstats(c, nk, get_tile, banks, sq_slots, want_mean, s1row, s2row, free_tok=None):
    p = c.p
    s1t = [banks[0][0:1, :], banks[0][64:65, :], banks[1][0:1, :], banks[1][64:65, :]]
    s2t = [banks[2][0:1, :], banks[2][64:65, :], banks[3][0:1, :], banks[3][64:65, :]]
    sq_tok = [None, None]
    mm_tok = None
    p.wait("tensor", free_tok)
    for k in range(nk):
        src, toks = get_tile(k)
        sl = k % 2
        sq = sq_slots[sl]
        p.waits("vector", toks)
        p.wait("vector", sq_tok[sl])
        p.op("vector", lambda h, sq=sq, src=src: h.tensor_tensor(
            sq[:], src, src, AluOpType.mult))
        vtok = p.milestone("vector")
        p.waits("tensor", toks)
        p.wait("tensor", vtok)
        for g in range(4):
            cs = slice(g * 512, (g + 1) * 512)
            if want_mean:
                p.op("tensor", lambda h, k=k, g=g, src=src, cs=cs: h.matmul(
                    s1t[g], c.ones_m1[:], src[:, cs],
                    start=(k == 0), stop=(k == nk - 1)))
            p.op("tensor", lambda h, k=k, g=g, sq=sq, cs=cs: h.matmul(
                s2t[g], c.ones_m1[:], sq[:, cs],
                start=(k == 0), stop=(k == nk - 1)))
        sq_tok[sl] = p.milestone("tensor")
    mm_tok = sq_tok[(nk - 1) % 2]
    p.wait("vector", mm_tok)
    for g in range(4):
        cs = slice(g * 512, (g + 1) * 512)
        if want_mean:
            p.op("vector", lambda h, g=g, cs=cs: h.tensor_copy(s1row[0:1, cs], s1t[g]))
        p.op("vector", lambda h, g=g, cs=cs: h.tensor_copy(s2row[0:1, cs], s2t[g]))
    ev = p.milestone("vector")
    return mm_tok, ev


def ln_mu(c, R1, row_bf, dd, ev_tok):
    p = c.p
    p.wait("vector", ev_tok)
    p.op("vector", lambda h: h.tensor_scalar(R1[:], R1[:], 1.0 / dd, None, AluOpType.mult))
    p.op("vector", lambda h: h.tensor_copy(row_bf[:], R1[:]))
    return p.milestone("vector")


def ln_r(c, R1, R2, row_bf, dd, mu_done):
    p = c.p
    p.wait("vector", mu_done)
    p.op("vector", lambda h: h.tensor_scalar(R2[:], R2[:], 1.0 / dd, None, AluOpType.mult))
    p.op("vector", lambda h: h.tensor_tensor(R1[:], R1[:], R1[:], AluOpType.mult))
    p.op("vector", lambda h: h.tensor_tensor(R2[:], R2[:], R1[:], AluOpType.subtract))
    rowv = p.milestone("vector")
    rs = rsqrt_row(c, rowv, R2[:], R2[:])
    p.wait("vector", rs)
    p.op("vector", lambda h: h.tensor_copy(row_bf[:], R2[:]))
    return p.milestone("vector")


def rms_r(c, R2, row_bf, dd, ev_tok):
    p = c.p
    p.wait("vector", ev_tok)
    p.op("vector", lambda h: h.tensor_scalar(R2[:], R2[:], 1.0 / dd, None, AluOpType.mult))
    rowv = p.milestone("vector")
    rs = rsqrt_row(c, rowv, R2[:], R2[:])
    p.wait("vector", rs)
    p.op("vector", lambda h: h.tensor_copy(row_bf[:], R2[:]))
    return p.milestone("vector")



def emit_L4(c):
    p = c.p
    vscr = c.dram["vscr"]
    rstdscr = c.dram["rstdscr"]
    hscr = c.dram["hscr"]
    xres_o = c.dram["xres_o"]
    w_out = c.dram["w_out"]
    normw = c.dram["normw_col"]
    ln2w = c.dram["ln2w_col"]
    gate_w = c.dram["gate_w"]
    up_w = c.dram["up_w"]
    down_w = c.dram["down_w"]
    h2scr = c.dram["h2scr"]
    xres1 = c.dram["xres1"]

    bS = [c.psum(f"l4S{i}") for i in range(4)]
    bB = c.psum("l4B")
    bM = [c.psum(f"l4M{i}") for i in range(2)]

    row_bf = c.sbuf("row_bf", [1, T], BF16)
    s1row = c.sbuf("s1row", [1, T], F32)
    s2row = c.sbuf("s2row", [1, T], F32)
    nw = c.sbuf("nw_sb", [128, 32], F32)
    l2w = c.sbuf("l2w_sb", [128, 16], F32)
    arena = c.sbuf("l4arena", [128, 65536], BF16)
    rb1 = arena[:, 61440:63488]
    mub = arena[:, 63488:65536]
    slots = [c.sbuf(f"l4s{i}", [128, T], BF16) for i in range(4)]
    wb = [c.sbuf(f"l4wb{i}", [128, 8192], BF16) for i in range(2)]
    tmpf = c.sbuf("l4tmp", [128, 512], F32)

    s_v = p.stream("l4v")
    s_w = [p.stream("l4w0"), p.stream("l4w1")]
    s_h = [p.stream("l4h0"), p.stream("l4h1")]
    s_o = [p.stream("l4o0"), p.stream("l4o1")]
    s_g = [p.stream("l4g0"), p.stream("l4g1")]
    s_d = [p.stream("l4d0"), p.stream("l4d1")]
    s_m = p.stream("l4misc")
    s_x5 = p.stream("l4x5")

    mt_ = s_m.dma("gpsimd", nw[:], normw[:, :])
    lt_ = s_m.dma("gpsimd", l2w[:], ln2w[:, :])
    rt_ = s_m.dma("gpsimd", s1row[0:1, :], rstdscr.rearrange("(a t) -> a t", a=1))
    p.wait("vector", rt_)
    p.op("vector", lambda h: h.tensor_copy(row_bf[:], s1row[0:1, :]))
    rbt = p.milestone("vector")
    b1 = bcast_rows(c, [(row_bf[:], rb1)], bB, rbt)

    vfm = arena[:, 0:32 * 1024]
    wtok = [None, None]
    wfree = [None, None]
    wnorm = [None, None]

    def load_wout(mt, half):
        sl = mt % 2
        p.wait("gpsimd", wfree[sl])
        src = w_out[:, mt * 128:(mt + 1) * 128].rearrange("(k p) m -> p k m", p=128)
        dst = wb[sl][:, 0:4096].rearrange("p (k m) -> p k m", k=32)
        tok = s_w[sl].dma("gpsimd", dst, src)
        p.waits("vector", [tok, mt_])
        p.wait("vector", wnorm[sl])
        for k in range(32):
            p.op("vector", lambda h, sl=sl, k=k: h.tensor_scalar(
                wb[sl][:, k * 128:(k + 1) * 128], wb[sl][:, k * 128:(k + 1) * 128],
                nw[:, k:k + 1], None, AluOpType.mult))
        wnorm[sl] = p.milestone("vector")
        wtok[sl] = wnorm[sl]

    pb_free = [None, None]
    hload = [None, None]
    ost = [None, None]
    hsl = slots[0:2]
    osl = slots[2:4]
    part1_mm = None
    vld_prev = None
    for half in range(2):
        toff = half * 1024
        vld = None
        p.wait("scalar", part1_mm)
        for kd in range(32):
            vld = s_v.dma("scalar", vfm[:, kd * 1024:(kd + 1) * 1024],
                          vscr[toff:toff + 1024, kd * 128:(kd + 1) * 128],
                          transpose=True)
        load_wout(0, half)
        load_wout(1, half)
        for mt in range(16):
            sl = mt % 2
            p.wait("gpsimd", hload[sl])
            hld = s_h[sl].dma("gpsimd", hsl[sl][:, 0:1024],
                              hscr[mt * 128:(mt + 1) * 128, toff:toff + 1024])
            p.waits("tensor", [wtok[sl], vld, b1])
            for g in range(2):
                pb = bM[g % 2]
                p.wait("tensor", pb_free[g % 2])
                for k in range(32):
                    p.op("tensor", lambda h, sl=sl, g=g, k=k, pb=pb: h.matmul(
                        pb[:], wb[sl][:, k * 128:(k + 1) * 128],
                        vfm[:, k * 1024 + g * 512: k * 1024 + (g + 1) * 512],
                        start=(k == 0), stop=(k == 31)))
                mmt = p.milestone("tensor")
                cs = slice(g * 512, (g + 1) * 512)
                p.waits("vector", [mmt, hld, ost[sl]])
                p.op("vector", lambda h, pb=pb, cs=cs, toff=toff: h.tensor_tensor(
                    tmpf[:], pb[:], rb1[:, toff + cs.start:toff + cs.stop],
                    AluOpType.mult))
                pb_free[g % 2] = p.milestone("vector")
                p.op("vector", lambda h, sl=sl, cs=cs: h.tensor_tensor(
                    hsl[sl][:, cs], tmpf[:], hsl[sl][:, cs], AluOpType.add))
            ovt = p.milestone("vector")
            hload[sl] = ovt
            p.wait("sync", ovt)
            ost[sl] = s_o[sl].dma("sync",
                                  h2scr[mt * 128:(mt + 1) * 128, toff:toff + 1024],
                                  hsl[sl][:, 0:1024])
            wfree[sl] = mmt
            part1_mm = mmt
            if mt + 2 < 16:
                load_wout(mt + 2, half)
    part1_ovt = hload[1]

    xs = slots[0:2]
    sqs = slots[2:4]
    ld_toks = {}
    p.wait("gpsimd", ost[0])
    p.wait("gpsimd", ost[1])

    def get_h2(k):
        sl = k % 2
        p.wait("gpsimd", ld_toks.get(k - 2))
        t = s_h[sl].dma("gpsimd", xs[sl][:], h2scr[k * 128:(k + 1) * 128, :])
        ld_toks[k] = t
        return xs[sl][:], [t, ost[sl]]

    mmq, evq = colstats(c, NKD, get_h2, bS, sqs, False, None, s2row)
    t3 = rms_r(c, s2row, row_bf, D, evq)
    b2 = bcast_rows(c, [(row_bf[:], rb1)], bB, t3, b1, out_free=part1_ovt)

    rmsh2 = arena[:, 0:16 * 1024]
    inter = arena[:, 16384:16384 + 45056]
    hacc = slots[0:2]
    silu_toks = {}
    down_mmt = None
    gtok = [None, None]
    gfree = [None, None]
    h2ld = [None, None]
    h2st = [None, None]
    apply_tok = None
    for half in range(2):
        toff = half * 1024
        ap_toks = {}
        for k in range(NKD):
            sl = k % 2
            p.wait("gpsimd", ap_toks.get(k - 2))
            if half == 0 and k < 2:
                p.wait("gpsimd", mmq)
            if half == 1 and k < 2:
                p.wait("gpsimd", h2st[0])
                p.wait("gpsimd", h2st[1])
            ldh = s_h[sl].dma("gpsimd", xs[sl][:, 0:1024],
                              h2scr[k * 128:(k + 1) * 128, toff:toff + 1024])
            p.waits("vector", [ldh, b2, lt_, part1_mm, down_mmt])
            p.op("vector", lambda h, sl=sl, k=k: h.tensor_scalar(
                sqs[sl][:, 0:1024], xs[sl][:, 0:1024], l2w[:, k:k + 1],
                None, AluOpType.mult))
            p.op("vector", lambda h, sl=sl, k=k, toff=toff: h.tensor_tensor(
                rmsh2[:, k * 1024:(k + 1) * 1024], sqs[sl][:, 0:1024],
                rb1[:, toff:toff + 1024], AluOpType.mult))
            ap_toks[k] = p.milestone("vector")
        rmsh2_all = ap_toks[NKD - 1]
        p.wait("scalar", down_mmt)
        for ib in range(11):
            for which in range(2):
                sl = (ib * 2 + which) % 2
                w = gate_w if which == 0 else up_w
                p.wait("gpsimd", gfree[sl])
                src = w[:, ib * 512:(ib + 1) * 512].rearrange("(k p) m -> p k m", p=128)
                dst = wb[sl][:, :].rearrange("p (k m) -> p k m", k=16)
                gtok[sl] = s_g[sl].dma("gpsimd", dst, src)
                p.waits("tensor", [gtok[sl], rmsh2_all])
                last_mmt = None
                for mt4 in range(4):
                    im = ib * 4 + mt4
                    for tg in range(2):
                        pb = bM[tg % 2]
                        p.wait("tensor", pb_free[tg % 2])
                        for k in range(NKD):
                            p.op("tensor", lambda h, sl=sl, mt4=mt4, tg=tg, k=k,
                                 pb=pb: h.matmul(
                                     pb[:], wb[sl][:, k * 512 + mt4 * 128:
                                                   k * 512 + (mt4 + 1) * 128],
                                     rmsh2[:, k * 1024 + tg * 512:
                                           k * 1024 + (tg + 1) * 512],
                                     start=(k == 0), stop=(k == NKD - 1)))
                        mmt = p.milestone("tensor")
                        last_mmt = mmt
                        ics = slice(16384 + im * 1024 + tg * 512,
                                    16384 + im * 1024 + (tg + 1) * 512)
                        if which == 0:
                            p.wait("scalar", mmt)
                            p.op("scalar", lambda h, pb=pb, ics=ics: h.activation(
                                arena[:, ics], pb[:], AF.Silu))
                            tk = p.milestone("scalar")
                            silu_toks[(im, tg)] = tk
                            pb_free[tg % 2] = tk
                        else:
                            p.waits("vector", [mmt, silu_toks[(im, tg)]])
                            p.op("vector", lambda h, pb=pb, ics=ics: h.tensor_tensor(
                                arena[:, ics], arena[:, ics], pb[:], AluOpType.mult))
                            pb_free[tg % 2] = p.milestone("vector")
                gfree[sl] = last_mmt
        inter_tok = pb_free[0]
        inter_tok2 = pb_free[1]
        for mt in range(16):
            sl = mt % 2
            p.wait("gpsimd", gfree[sl])
            src = down_w[:, mt * 128:(mt + 1) * 128].rearrange("(k p) m -> p k m", p=128)
            dstw = wb[sl][:, 0:44 * 128].rearrange("p (k m) -> p k m", k=44)
            dtok = s_d[sl].dma("gpsimd", dstw, src)
            p.wait("gpsimd", h2ld[sl])
            p.wait("gpsimd", rmsh2_all)
            hld = s_h[sl].dma("gpsimd", hacc[sl][:, 0:1024],
                              h2scr[mt * 128:(mt + 1) * 128, toff:toff + 1024])
            p.waits("tensor", [dtok, inter_tok, inter_tok2])
            for tg in range(2):
                pb = bM[tg % 2]
                p.wait("tensor", pb_free[tg % 2])
                for k in range(44):
                    p.op("tensor", lambda h, sl=sl, tg=tg, k=k, pb=pb: h.matmul(
                        pb[:], wb[sl][:, k * 128:(k + 1) * 128],
                        inter[:, k * 1024 + tg * 512: k * 1024 + (tg + 1) * 512],
                        start=(k == 0), stop=(k == 43)))
                mmt = p.milestone("tensor")
                p.waits("vector", [mmt, hld, h2st[sl]])
                cs = slice(tg * 512, (tg + 1) * 512)
                p.op("vector", lambda h, sl=sl, pb=pb, cs=cs: h.tensor_tensor(
                    hacc[sl][:, cs], hacc[sl][:, cs], pb[:], AluOpType.add))
                pb_free[tg % 2] = p.milestone("vector")
            hvt = pb_free[1]
            p.wait("sync", pb_free[0])
            p.wait("sync", pb_free[1])
            h2st[sl] = s_o[sl].dma(
                "sync", h2scr[mt * 128:(mt + 1) * 128, toff:toff + 1024],
                hacc[sl][:, 0:1024])
            h2ld[sl] = hvt
            gfree[sl] = mmt
            down_mmt = mmt

    p.wait("gpsimd", h2st[0])
    p.wait("gpsimd", h2st[1])
    ld5 = {}

    def get_h2b(k):
        sl = k % 2
        p.wait("gpsimd", ld5.get(k - 2, down_mmt))
        p.wait("gpsimd", h2ld[sl])
        t = s_h[sl].dma("gpsimd", xs[sl][:], h2scr[k * 128:(k + 1) * 128, :])
        ld5[k] = t
        return xs[sl][:], [t]

    mm5, ev5 = colstats(c, NKD, get_h2b, bS, sqs, True, s1row, s2row)
    t5 = ln_mu(c, s1row, row_bf, D, ev5)
    b5 = bcast_rows(c, [(row_bf[:], mub)], bB, t5, b2, out_free=down_mmt)
    t6 = ln_r(c, s1row, s2row, row_bf, D, b5)
    b6 = bcast_rows(c, [(row_bf[:], rb1)], bB, t6, b5, out_free=rmsh2_all)
    x1done = {}
    for k in range(NKD):
        sl = k % 2
        p.wait("gpsimd", x1done.get(k - 2, mm5))
        ldh = s_h[sl].dma("gpsimd", xs[sl][:], h2scr[k * 128:(k + 1) * 128, :])
        ldx = s_x5.dma("gpsimd", sqs[sl][:], xres_o[k * 128:(k + 1) * 128, :])
        p.waits("vector", [ldh, ldx, b6])
        p.op("vector", lambda h, sl=sl: h.tensor_tensor(
            xs[sl][:], xs[sl][:], mub[:], AluOpType.subtract))
        p.op("vector", lambda h, sl=sl: h.tensor_tensor(
            xs[sl][:], xs[sl][:], rb1[:], AluOpType.mult))
        p.op("vector", lambda h, sl=sl: h.tensor_tensor(
            xs[sl][:], xs[sl][:], sqs[sl][:], AluOpType.add))
        vt = p.milestone("vector")
        p.wait("sync", vt)
        x1done[k] = s_o[sl].dma("sync", xres1[k * 128:(k + 1) * 128, :], xs[sl][:])


def finish(c):
    c.p.finish_streams()


def build_L4():
    nc = make_nc()
    with contextlib.ExitStack() as es:
        c = Ctx(nc, es)
        c.dram_in("vscr", [T, 4096], BF16)
        c.dram_in("rstdscr", [T], F32)
        c.dram_in("hscr", [D, T], BF16)
        c.dram_in("xres_o", [D, T], BF16)
        c.dram_in("w_out", [4096, 2048], F32)
        c.dram_in("normw_col", [128, 32], F32)
        c.dram_in("ln2w_col", [128, 16], F32)
        c.dram_in("gate_w", [2048, 5632], F32)
        c.dram_in("up_w", [2048, 5632], F32)
        c.dram_in("down_w", [5632, 2048], F32)
        c.dram_out("h2scr", [D, T], BF16)
        c.dram_out("xres1", [D, T], BF16)
        emit_consts(c)
        emit_L4(c)
        finish(c)
        c.p.finalize()
    return nc




def emit_L5(c):
    p = c.p
    xres1 = c.dram["xres1"]
    w_gate = c.dram["w_gate"]
    w_up = c.dram["w_up"]
    w_down = c.dram["w_down"]
    sg = c.dram["sgate_col"]
    su = c.dram["sup_col"]
    sd = c.dram["sdown_col"]
    f2scr = c.dram["f2scr"]
    xres2 = c.dram["xres2"]

    bS = [c.psum(f"l5S{i}") for i in range(4)]
    bB = c.psum("l5B")
    bM = [c.psum(f"l5M{i}") for i in range(2)]

    row_bf = c.sbuf("row_bf", [1, T], BF16)
    s1row = c.sbuf("s1row", [1, T], F32)
    s2row = c.sbuf("s2row", [1, T], F32)
    mub = c.sbuf("mub", [128, T], BF16)
    rb = c.sbuf("rb", [128, T], BF16)
    sgc = c.sbuf("sgc", [128, 64], F32)
    suc = c.sbuf("suc", [128, 64], F32)
    sdc = c.sbuf("sdc", [128, 16], F32)
    lnx = c.sbuf("lnx", [128, NKD * T], BF16)
    interq = c.sbuf("interq", [128, 64 * 512], BF16)
    wbuf = [c.sbuf(f"l5w{i}", [128, 16 * 512], BF16) for i in range(2)]
    slots = [c.sbuf(f"l5s{i}", [128, T], BF16) for i in range(4)]
    upt = c.sbuf("l5up", [128, 512], BF16)

    s_x = [p.stream("l5x0"), p.stream("l5x1")]
    s_w = [p.stream("l5w0"), p.stream("l5w1")]
    s_o = [p.stream("l5o0"), p.stream("l5o1")]
    s_m = p.stream("l5m")

    t1 = s_m.dma("gpsimd", sgc[:], sg[:, :])
    t2 = s_m.dma("gpsimd", suc[:], su[:, :])
    t3 = s_m.dma("gpsimd", sdc[:], sd[:, :])

    xs = slots[0:2]
    sqs = slots[2:4]
    ldk = {}

    def get_x1(k):
        sl = k % 2
        p.wait("gpsimd", ldk.get(k - 2))
        t = s_x[sl].dma("gpsimd", xs[sl][:], xres1[k * 128:(k + 1) * 128, :])
        ldk[k] = t
        return xs[sl][:], [t]

    mm1, ev1 = colstats(c, NKD, get_x1, bS, sqs, True, s1row, s2row)
    tmu = ln_mu(c, s1row, row_bf, D, ev1)
    b1 = bcast_rows(c, [(row_bf[:], mub)], bB, tmu)
    tr = ln_r(c, s1row, s2row, row_bf, D, b1)
    b2 = bcast_rows(c, [(row_bf[:], rb)], bB, tr, b1)
    lna = {}
    for k in range(NKD):
        sl = k % 2
        p.wait("gpsimd", lna.get(k - 2, mm1))
        ld = s_x[sl].dma("gpsimd", xs[sl][:], xres1[k * 128:(k + 1) * 128, :])
        p.waits("vector", [ld, b2])
        p.op("vector", lambda h, sl=sl: h.tensor_tensor(
            sqs[sl][:], xs[sl][:], mub[:], AluOpType.subtract))
        p.op("vector", lambda h, sl=sl, k=k: h.tensor_tensor(
            lnx[:, k * T:(k + 1) * T], sqs[sl][:], rb[:], AluOpType.mult))
        lna[k] = p.milestone("vector")
    lnx_all = lna[NKD - 1]

    pb_free = [None, None]
    wtok = [None, None]
    wfree = [None, None]
    fst = [None, None]
    down_mmt = None
    silu_toks = {}
    for q in range(4):
        toff = q * 512
        p.wait("scalar", down_mmt)
        p.wait("vector", down_mmt)
        for ib in range(16):
            for which in range(2):
                sl = (ib * 2 + which) % 2
                w = w_gate if which == 0 else w_up
                p.wait("gpsimd", wfree[sl])
                src = w[:, ib * 512:(ib + 1) * 512].rearrange("(k p) m -> p k m", p=128)
                dst = wbuf[sl][:, :].rearrange("p (k m) -> p k m", k=16)
                wtok[sl] = s_w[sl].dma("gpsimd", dst, src)
                p.waits("tensor", [wtok[sl], lnx_all])
                last_mmt = None
                for mt4 in range(4):
                    im = ib * 4 + mt4
                    pb = bM[mt4 % 2]
                    p.wait("tensor", pb_free[mt4 % 2])
                    for k in range(NKD):
                        p.op("tensor", lambda h, sl=sl, mt4=mt4, k=k, pb=pb,
                             toff=toff: h.matmul(
                                 pb[:], wbuf[sl][:, k * 512 + mt4 * 128:
                                                 k * 512 + (mt4 + 1) * 128],
                                 lnx[:, k * T + toff: k * T + toff + 512],
                                 start=(k == 0), stop=(k == NKD - 1)))
                    mmt = p.milestone("tensor")
                    last_mmt = mmt
                    ics = slice(im * 512, (im + 1) * 512)
                    if which == 0:
                        p.waits("scalar", [mmt, t1])
                        p.op("scalar", lambda h, pb=pb, ics=ics, im=im: h.activation(
                            interq[:, ics], pb[:], AF.Silu,
                            scale=sgc[:, im:im + 1]))
                        tk = p.milestone("scalar")
                        silu_toks[im] = tk
                        pb_free[mt4 % 2] = tk
                    else:
                        p.waits("scalar", [mmt, t2])
                        p.op("scalar", lambda h, pb=pb, im=im: h.activation(
                            upt[:], pb[:], AF.Identity, scale=suc[:, im:im + 1]))
                        sct = p.milestone("scalar")
                        p.waits("vector", [sct, silu_toks[im]])
                        p.op("vector", lambda h, ics=ics: h.tensor_tensor(
                            interq[:, ics], interq[:, ics], upt[:], AluOpType.mult))
                        vt = p.milestone("vector")
                        pb_free[mt4 % 2] = vt
                        p.wait("scalar", vt)
                wfree[sl] = last_mmt
        inter_tok = pb_free[0]
        inter_tok2 = pb_free[1]
        for mt in range(16):
            sl = mt % 2
            p.wait("gpsimd", wfree[sl])
            src = w_down[:, mt * 128:(mt + 1) * 128].rearrange("(k p) m -> p k m", p=128)
            dst = wbuf[sl][:, 0:64 * 128].rearrange("p (k m) -> p k m", k=64)
            dtok = s_w[sl].dma("gpsimd", dst, src)
            p.waits("tensor", [dtok, inter_tok, inter_tok2])
            pb = bM[mt % 2]
            p.wait("tensor", pb_free[mt % 2])
            for k in range(64):
                p.op("tensor", lambda h, sl=sl, k=k, pb=pb: h.matmul(
                    pb[:], wbuf[sl][:, k * 128:(k + 1) * 128],
                    interq[:, k * 512:(k + 1) * 512],
                    start=(k == 0), stop=(k == 63)))
            mmt = p.milestone("tensor")
            p.waits("scalar", [mmt, t3, fst[sl]])
            p.op("scalar", lambda h, sl=sl, mt=mt, pb=pb: h.activation(
                xs[sl][:, 0:512], pb[:], AF.Identity, scale=sdc[:, mt:mt + 1]))
            sct = p.milestone("scalar")
            pb_free[mt % 2] = sct
            p.wait("sync", sct)
            fst[sl] = s_o[sl].dma("sync", f2scr[mt * 128:(mt + 1) * 128,
                                                toff:toff + 512], xs[sl][:, 0:512])
            wfree[sl] = mmt
            down_mmt = mmt

    p.wait("gpsimd", fst[0])
    p.wait("gpsimd", fst[1])
    ld5 = {}

    def get_f2(k):
        sl = k % 2
        p.wait("gpsimd", ld5.get(k - 2, down_mmt))
        t = s_x[sl].dma("gpsimd", xs[sl][:], f2scr[k * 128:(k + 1) * 128, :])
        ld5[k] = t
        return xs[sl][:], [t]

    mm5, ev5 = colstats(c, NKD, get_f2, bS, sqs, True, s1row, s2row)
    t5 = ln_mu(c, s1row, row_bf, D, ev5)
    b5 = bcast_rows(c, [(row_bf[:], mub)], bB, t5, b2, out_free=lnx_all)
    t6 = ln_r(c, s1row, s2row, row_bf, D, b5)
    b6 = bcast_rows(c, [(row_bf[:], rb)], bB, t6, b5, out_free=lnx_all)
    x2done = {}
    s_x2 = p.stream("l5x2")
    for k in range(NKD):
        sl = k % 2
        p.wait("gpsimd", x2done.get(k - 2, mm5))
        ldh = s_x[sl].dma("gpsimd", xs[sl][:], f2scr[k * 128:(k + 1) * 128, :])
        ldx = s_x2.dma("gpsimd", sqs[sl][:], xres1[k * 128:(k + 1) * 128, :])
        p.waits("vector", [ldh, ldx, b6])
        p.op("vector", lambda h, sl=sl: h.tensor_tensor(
            xs[sl][:], xs[sl][:], mub[:], AluOpType.subtract))
        p.op("vector", lambda h, sl=sl: h.tensor_tensor(
            xs[sl][:], xs[sl][:], rb[:], AluOpType.mult))
        p.op("vector", lambda h, sl=sl: h.tensor_tensor(
            xs[sl][:], xs[sl][:], sqs[sl][:], AluOpType.add))
        vt = p.milestone("vector")
        p.wait("sync", vt)
        x2done[k] = s_o[sl].dma("sync", xres2[k * 128:(k + 1) * 128, :], xs[sl][:])


def build_L5():
    nc = make_nc()
    with contextlib.ExitStack() as es:
        c = Ctx(nc, es)
        c.dram_in("xres1", [D, T], BF16)
        c.dram_in("w_gate", [2048, 8192], I8)
        c.dram_in("w_up", [2048, 8192], I8)
        c.dram_in("w_down", [8192, 2048], I8)
        c.dram_in("sgate_col", [128, 64], F32)
        c.dram_in("sup_col", [128, 64], F32)
        c.dram_in("sdown_col", [128, 16], F32)
        c.dram_out("f2scr", [D, T], BF16)
        c.dram_out("xres2", [D, T], BF16)
        emit_consts(c)
        emit_L5(c)
        finish(c)
        c.p.finalize()
    return nc




def emit_L6(c):
    p = c.p
    xres2 = c.dram["xres2"]
    wqT = c.dram["wqT"]
    wkT = c.dram["wkT"]
    wvT = c.dram["wvT"]
    woT = c.dram["woT"]
    lnxscr = c.dram["lnxscr"]
    qscr = c.dram["qscr"]
    kscr = c.dram["kscr"]
    vscr2 = c.dram["vscr2"]
    zascr = c.dram["zascr"]
    aoscr = c.dram["aoscr"]
    xres3 = c.dram["xres3"]
    caus = c.dram["causmask"]

    bS = [c.psum(f"l6S{i}") for i in range(4)]
    bB = c.psum("l6B")
    bM = [c.psum(f"l6M{i}") for i in range(2)]

    row_bf = c.sbuf("row_bf", [1, T], BF16)
    s1row = c.sbuf("s1row", [1, T], F32)
    s2row = c.sbuf("s2row", [1, T], F32)
    mub = c.sbuf("mub", [128, T], BF16)
    rb = c.sbuf("rb", [128, T], BF16)
    slots = [c.sbuf(f"l6s{i}", [128, T], BF16) for i in range(4)]
    arena = c.sbuf("l6arena", [128, 32768], BF16)
    qt = [c.sbuf(f"qt{i}", [128, 512], BF16) for i in range(2)]
    zh = c.sbuf("zh", [128, T], BF16)
    rrow = c.sbuf("rrow", [1, 512], F32)
    rbt = c.sbuf("rbt", [128, 512], BF16)
    cmask = c.sbuf("cmask", [128, 128], F32)
    stage = [c.sbuf(f"l6stage{i}", [128, 512], BF16) for i in range(2)]

    s_x = [p.stream("l6x0"), p.stream("l6x1")]
    s_w = p.stream("l6w")
    s_o = [p.stream("l6o0"), p.stream("l6o1")]
    s_q = [p.stream("l6q0"), p.stream("l6q1")]
    s_k = [p.stream("l6k0"), p.stream("l6k1")]
    s_v = [p.stream("l6v0"), p.stream("l6v1")]
    s_z = p.stream("l6z")
    s_m = p.stream("l6m")

    cm_tok = s_m.dma("gpsimd", cmask[:], caus[:, :])

    xs = slots[0:2]
    sqs = slots[2:4]
    ldk = {}

    def get_x2(k):
        sl = k % 2
        p.wait("gpsimd", ldk.get(k - 2))
        t = s_x[sl].dma("gpsimd", xs[sl][:], xres2[k * 128:(k + 1) * 128, :])
        ldk[k] = t
        return xs[sl][:], [t]

    mm1, ev1 = colstats(c, NKD, get_x2, bS, sqs, True, s1row, s2row)
    tmu = ln_mu(c, s1row, row_bf, D, ev1)
    b1 = bcast_rows(c, [(row_bf[:], mub)], bB, tmu)
    tr = ln_r(c, s1row, s2row, row_bf, D, b1)
    b2 = bcast_rows(c, [(row_bf[:], rb)], bB, tr, b1)
    lna = {}
    for k in range(NKD):
        sl = k % 2
        p.wait("gpsimd", lna.get(k - 2, mm1))
        ld = s_x[sl].dma("gpsimd", xs[sl][:], xres2[k * 128:(k + 1) * 128, :])
        p.waits("vector", [ld, b2])
        p.op("vector", lambda h, sl=sl: h.tensor_tensor(
            sqs[sl][:], xs[sl][:], mub[:], AluOpType.subtract))
        p.op("vector", lambda h, sl=sl: h.tensor_tensor(
            sqs[sl][:], sqs[sl][:], rb[:], AluOpType.mult))
        vt = p.milestone("vector")
        p.wait("sync", vt)
        lna[k] = s_o[sl].dma("sync", lnxscr[k * 128:(k + 1) * 128, :], sqs[sl][:])
    lnx_st = [lna[NKD - 2], lna[NKD - 1]]

    wbig = arena[:, 0:16384]
    lnb = [arena[:, 16384:24576], arena[:, 24576:32768]]
    pbf = [None, None]
    qst = [None, None]
    arena_user = None

    def proj_fm(wsrc, dst_hbm, prev_tok):
        last = prev_tok
        wtok = None
        for hhalf in range(2):
            p.wait("gpsimd", last)
            wtok = s_w.dma("gpsimd", wbig[:, :],
                           wsrc[:, hhalf * 16384:(hhalf + 1) * 16384])
            ln_tok = [last, last]
            for tg in range(4):
                sl = tg % 2
                p.wait("gpsimd", ln_tok[sl])
                p.waits("gpsimd", lnx_st)
                lt = s_q[sl].dma("gpsimd",
                                 lnb[sl].rearrange("p (k t) -> p k t", k=16),
                                 lnxscr[:, tg * 512:(tg + 1) * 512].rearrange(
                                     "(k p) t -> p k t", p=128))
                p.waits("tensor", [lt, wtok, prev_tok])
                for hh8 in range(8):
                    hh = hhalf * 8 + hh8
                    pb = bM[hh8 % 2]
                    p.wait("tensor", pbf[hh8 % 2])
                    for k in range(NKD):
                        p.op("tensor", lambda h, hh8=hh8, k=k, sl=sl, pb=pb: h.matmul(
                            pb[:], wbig[:, (hh8 * 16 + k) * 128:(hh8 * 16 + k + 1) * 128],
                            lnb[sl][:, k * 512:(k + 1) * 512],
                            start=(k == 0), stop=(k == NKD - 1)))
                    mmt = p.milestone("tensor")
                    ln_tok[sl] = mmt
                    last = mmt
                    p.waits("scalar", [mmt, qst[hh8 % 2]])
                    p.op("scalar", lambda h, pb=pb, hh8=hh8: h.copy(
                        stage[hh8 % 2][:], pb[:]))
                    sct = p.milestone("scalar")
                    pbf[hh8 % 2] = sct
                    p.wait("sync", sct)
                    qst[hh8 % 2] = s_o[hh8 % 2].dma(
                        "sync", dst_hbm[hh * 128:(hh + 1) * 128,
                                        tg * 512:(tg + 1) * 512], stage[hh8 % 2][:])
        return last, qst[0], qst[1]

    kdone, kst0, kst1 = proj_fm(wkT, kscr, None)
    qdone, qst0, qst1 = proj_fm(wqT, qscr, kdone)

    vlast = qdone
    for hhalf in range(2):
        p.wait("gpsimd", vlast)
        wtok = s_w.dma("gpsimd", wbig[:, :],
                       wvT[:, hhalf * 16384:(hhalf + 1) * 16384])
        lcb = [vlast, vlast]
        for tm in range(16):
            sl = tm % 2
            p.wait("gpsimd", lcb[sl])
            lt = s_q[sl].dma("gpsimd",
                             lnb[sl][:, 0:2048].rearrange("p (k t) -> p k t", k=16),
                             lnxscr[:, tm * 128:(tm + 1) * 128].rearrange(
                                 "(k p) t -> p k t", p=128))
            p.waits("tensor", [lt, wtok, qdone])
            for eg in range(2):
                pb = bM[eg % 2]
                p.wait("tensor", pbf[eg % 2])
                for k in range(NKD):
                    p.op("tensor", lambda h, eg=eg, k=k, sl=sl, pb=pb: h.matmul(
                        pb[:], lnb[sl][:, k * 128:(k + 1) * 128],
                        wbig[:, k * 1024 + eg * 512: k * 1024 + (eg + 1) * 512],
                        start=(k == 0), stop=(k == NKD - 1)))
                mmt = p.milestone("tensor")
                lcb[sl] = mmt
                vlast = mmt
                p.waits("scalar", [mmt, qst[eg % 2]])
                p.op("scalar", lambda h, pb=pb, eg=eg: h.copy(
                    stage[eg % 2][:], pb[:]))
                sct = p.milestone("scalar")
                pbf[eg % 2] = sct
                p.wait("sync", sct)
                qst[eg % 2] = s_o[eg % 2].dma(
                    "sync", vscr2[tm * 128:(tm + 1) * 128,
                                  hhalf * 1024 + eg * 512:
                                  hhalf * 1024 + (eg + 1) * 512], stage[eg % 2][:])
    vst = [qst[0], qst[1]]
    if getattr(c, "stop_after_v", False):
        return

    ptb = arena[:, 0:8192]
    kst = [arena[:, 8192:10240], arena[:, 10240:12288]]
    vstp = [arena[:, 12288:14336], arena[:, 14336:16384]]
    csb = bB
    bz = bS[0]
    brb = bS[1]
    qld = [None, None]
    kld = [vlast, vlast]
    vld = [vlast, vlast]
    ptb_free = {}
    zev = None
    zst = None
    att_last = None
    prev_rsc = None
    for hh in range(16):
        for tg in range(4):
            ntk = 4 * (tg + 1)
            sl = tg % 2
            p.wait("gpsimd", qld[sl])
            qtok = s_q[sl].dma("gpsimd", qt[sl][:],
                               qscr[hh * 128:(hh + 1) * 128, tg * 512:(tg + 1) * 512])
            p.wait("gpsimd", kld[sl])
            ktok = s_k[sl].dma(
                "gpsimd", kst[sl][:, 0:ntk * 128],
                kscr[hh * 128:(hh + 1) * 128, 0:ntk * 128])
            p.wait("gpsimd", vld[sl])
            vtok = s_v[sl].dma(
                "gpsimd", vstp[sl][:, 0:ntk * 128].rearrange("p (tk e) -> p tk e", tk=ntk),
                vscr2[0:ntk * 128, hh * 128:(hh + 1) * 128].rearrange(
                    "(tk p) e -> p tk e", p=128))
            p.waits("tensor", [qtok, ktok, vtok, qdone, kst0, kst1, vst[0], vst[1]])
            p.waits("tensor", [prev_rsc, zev])
            for tkT in range(ntk):
                pb = bM[tkT % 2]
                p.wait("tensor", pbf[tkT % 2])
                p.op("tensor", lambda h, tkT=tkT, sl=sl, pb=pb: h.matmul(
                    pb[:], kst[sl][:, tkT * 128:(tkT + 1) * 128],
                    qt[sl][:], start=True, stop=True))
                mmt = p.milestone("tensor")
                kld[sl] = mmt
                j = tkT - 4 * tg
                if 0 <= j < 4:
                    p.waits("vector", [mmt, cm_tok])
                    p.op("vector", lambda h, j=j, pb=pb: h.tensor_tensor(
                        pb[:, j * 128:(j + 1) * 128], pb[:, j * 128:(j + 1) * 128],
                        cmask[:], AluOpType.add))
                    mmt = p.milestone("vector")
                p.waits("scalar", [mmt, ptb_free.get(tkT)])
                p.op("scalar", lambda h, tkT=tkT, pb=pb: h.activation(
                    ptb[:, tkT * 512:(tkT + 1) * 512], pb[:], AF.Exp,
                    bias=c.cpool[0:128, 3:4], scale=0.08838834764831845))
                et = p.milestone("scalar")
                pbf[tkT % 2] = et
                if 0 < j < 4:
                    p.wait("vector", et)
                    p.op("vector", lambda h, tkT=tkT, j=j: h.memset(
                        ptb[:, tkT * 512:tkT * 512 + j * 128], 0.0))
                    et = p.milestone("vector")
                    pbf[tkT % 2] = et
                p.wait("tensor", et)
                p.op("tensor", lambda h, tkT=tkT, ntk=ntk: h.matmul(
                    csb[0:1, :], c.ones_m1[:], ptb[:, tkT * 512:(tkT + 1) * 512],
                    start=(tkT == 0), stop=(tkT == ntk - 1)))
                p.op("tensor", lambda h, sl=sl, tkT=tkT, ntk=ntk: h.matmul(
                    bz[:], vstp[sl][:, tkT * 128:(tkT + 1) * 128],
                    ptb[:, tkT * 512:(tkT + 1) * 512],
                    start=(tkT == 0), stop=(tkT == ntk - 1)))
                tmk = p.milestone("tensor")
                ptb_free[tkT] = tmk
                vld[sl] = tmk
            cs_mm = ptb_free[ntk - 1]
            att_last = cs_mm
            qld[sl] = kld[sl]
            p.wait("scalar", cs_mm)
            p.op("scalar", lambda h: h.activation(rrow[0:1, :], csb[0:1, :], AF.Ln))
            p.op("scalar", lambda h: h.activation(rrow[0:1, :], rrow[0:1, :],
                                                  AF.Exp, scale=-1.0))
            p.op("scalar", lambda h: h.copy(row_bf[0:1, 0:512], rrow[0:1, :]))
            rsc = p.milestone("scalar")
            prev_rsc = rsc
            p.waits("tensor", [rsc, zev])
            p.op("tensor", lambda h: h.matmul(
                brb[:], c.ones_k1[:], row_bf[0:1, 0:512], start=True, stop=True))
            rbm = p.milestone("tensor")
            p.wait("scalar", rbm)
            p.op("scalar", lambda h: h.copy(rbt[:], brb[:]))
            rbe = p.milestone("scalar")
            p.waits("vector", [rbe, zst])
            p.op("vector", lambda h, tg=tg: h.tensor_tensor(
                zh[:, tg * 512:(tg + 1) * 512], bz[:], rbt[:], AluOpType.mult))
            zev = p.milestone("vector")
        p.wait("sync", zev)
        zst = s_z.dma("sync", zascr[hh * 128:(hh + 1) * 128, :], zh[:])

    zfull = arena[:, 0:32768]
    zldt = {}
    for k in range(NKD):
        sl = k % 2
        p.wait("gpsimd", att_last)
        p.wait("gpsimd", zst)
        zldt[k] = s_q[sl].dma("gpsimd", zfull[:, k * T:(k + 1) * T],
                              zascr[k * 128:(k + 1) * 128, :])
    zall = zldt[NKD - 1]
    wot = c.sbuf("wot", [128, 2048], BF16)
    aost = [None, None]
    wo_free = None
    s_wo = p.stream("l6wo")
    for mt in range(16):
        sl = mt % 2
        p.wait("gpsimd", wo_free)
        wt = s_wo.dma("gpsimd",
                      wot[:, :].rearrange("p (k m) -> p k m", k=16),
                      woT[:, mt * 128:(mt + 1) * 128].rearrange("(k p) m -> p k m", p=128))
        p.waits("tensor", [wt, zall])
        for tg in range(4):
            pb = bM[tg % 2]
            p.wait("tensor", pbf[tg % 2])
            for k in range(NKD):
                p.op("tensor", lambda h, tg=tg, k=k, pb=pb: h.matmul(
                    pb[:], wot[:, k * 128:(k + 1) * 128],
                    zfull[:, k * T + tg * 512: k * T + (tg + 1) * 512],
                    start=(k == 0), stop=(k == NKD - 1)))
            mmt = p.milestone("tensor")
            wo_free = mmt
            p.waits("scalar", [mmt, aost[sl]])
            p.op("scalar", lambda h, sl=sl, tg=tg, pb=pb: h.copy(
                xs[sl][:, tg * 512:(tg + 1) * 512], pb[:]))
            pbf[tg % 2] = p.milestone("scalar")
        p.wait("sync", pbf[0])
        p.wait("sync", pbf[1])
        aost[sl] = s_o[sl].dma("sync", aoscr[mt * 128:(mt + 1) * 128, :], xs[sl][:])

    p.wait("gpsimd", aost[0])
    p.wait("gpsimd", aost[1])
    ld5 = {}

    def get_ao(k):
        sl = k % 2
        p.wait("gpsimd", ld5.get(k - 2, wo_free))
        t = s_x[sl].dma("gpsimd", xs[sl][:], aoscr[k * 128:(k + 1) * 128, :])
        ld5[k] = t
        return xs[sl][:], [t, aost[sl]]

    mm5, ev5 = colstats(c, NKD, get_ao, bS, sqs, True, s1row, s2row)
    t5 = ln_mu(c, s1row, row_bf, D, ev5)
    b5 = bcast_rows(c, [(row_bf[:], mub)], bB, t5, b2)
    t6 = ln_r(c, s1row, s2row, row_bf, D, b5)
    b6 = bcast_rows(c, [(row_bf[:], rb)], bB, t6, b5)
    x3done = {}
    s_x3 = p.stream("l6x3")
    for k in range(NKD):
        sl = k % 2
        p.wait("gpsimd", x3done.get(k - 2, mm5))
        ldh = s_x[sl].dma("gpsimd", xs[sl][:], aoscr[k * 128:(k + 1) * 128, :])
        ldx = s_x3.dma("gpsimd", sqs[sl][:], xres2[k * 128:(k + 1) * 128, :])
        p.waits("vector", [ldh, ldx, b6])
        p.op("vector", lambda h, sl=sl: h.tensor_tensor(
            xs[sl][:], xs[sl][:], mub[:], AluOpType.subtract))
        p.op("vector", lambda h, sl=sl: h.tensor_tensor(
            xs[sl][:], xs[sl][:], rb[:], AluOpType.mult))
        p.op("vector", lambda h, sl=sl: h.tensor_tensor(
            xs[sl][:], xs[sl][:], sqs[sl][:], AluOpType.add))
        vt = p.milestone("vector")
        p.wait("sync", vt)
        x3done[k] = s_o[sl].dma("sync", xres3[k * 128:(k + 1) * 128, :], xs[sl][:])


def build_L6(debug=False, stop_after_v=False):
    nc = make_nc()
    with contextlib.ExitStack() as es:
        c = Ctx(nc, es)
        c.stop_after_v = stop_after_v
        mk = c.dram_out if debug else c.dram_tmp
        c.dram_in("xres2", [D, T], BF16)
        c.dram_in("wqT", [128, 32768], F32)
        c.dram_in("wkT", [128, 32768], F32)
        c.dram_in("wvT", [128, 32768], F32)
        c.dram_in("woT", [2048, 2048], F32)
        c.dram_in("causmask", [128, 128], F32)
        mk("lnxscr", [D, T], BF16)
        mk("qscr", [2048, T], BF16)
        mk("kscr", [2048, T], BF16)
        mk("vscr2", [T, 2048], BF16)
        mk("zascr", [2048, T], BF16)
        c.dram_out("aoscr", [D, T], BF16)
        c.dram_out("xres3", [D, T], BF16)
        emit_consts(c)
        emit_L6(c)
        finish(c)
        c.p.finalize()
    return nc




def emit_L7(c):
    p = c.p
    xres3 = c.dram["xres3"]
    routerT = c.dram["routerT"]
    gate_w = c.dram["moe_gate"]
    up_w = c.dram["moe_up"]
    down_w = c.dram["moe_down"]
    sgw = c.dram["sgw"]
    suw = c.dram["suw"]
    sdw = c.dram["sdw"]
    sgb = c.dram["sgb_col"]
    sub = c.dram["sub_col"]
    sdb = c.dram["sdb_col"]
    sel = c.dram["sel8"]
    lnxscr = c.dram["lnx7scr"]
    xout = c.dram["xout"]

    bS = [c.psum(f"l7S{i}") for i in range(4)]
    bB = c.psum("l7B")
    bM = [c.psum(f"l7M{i}") for i in range(2)]

    row_bf = c.sbuf("row_bf", [1, T], BF16)
    s1row = c.sbuf("s1row", [1, T], F32)
    s2row = c.sbuf("s2row", [1, T], F32)
    mub = c.sbuf("mub", [128, T], BF16)
    rb = c.sbuf("rb", [128, T], BF16)
    slots = [c.sbuf(f"l7s{i}", [128, T], BF16) for i in range(4)]
    rwt = c.sbuf("rwt", [128, 16 * 8], BF16)
    logit = c.sbuf("logit", [128, 16 * 8], F32)
    comb = c.sbuf("comb", [128, 16 * 8], F32)
    v16a = c.sbuf("l7va", [128, 16], F32)
    v16b = c.sbuf("l7vb", [128, 16], F32)
    v16p = c.sbuf("l7vp", [128, 16], F32)
    vtmp8a = c.sbuf("l7ta", [128, 128], F32)
    vtmp8b = c.sbuf("l7tb", [128, 128], F32)
    sg1 = c.sbuf("l7sg1", [128, 128], F32)
    sg2 = c.sbuf("l7sg2", [128, 128], F32)
    px = c.sbuf("l7px", [128, 128], F32)
    combTb = c.sbuf("combTb", [8, T], BF16)
    selb = c.sbuf("selb", [8, 1024], BF16)
    ceb = c.sbuf("ceb", [128, 8 * 1024], BF16)
    idf7 = c.sbuf("idf7", [128, 128], F32)
    acc = c.sbuf("l7acc", [128, 16 * 1024], F32)
    inter = c.sbuf("l7inter", [128, 11 * 1024], BF16)
    h1t = c.sbuf("l7h1", [128, 1024], BF16)
    lnq = [c.sbuf(f"l7ln{i}", [128, 16 * 1024], BF16) for i in range(1)]
    wbuf = [c.sbuf(f"l7w{i}", [128, 22 * 128], BF16) for i in range(2)]
    sgbc = c.sbuf("sgbc", [128, 22], F32)
    subc = c.sbuf("subc", [128, 22], F32)
    sdbc = c.sbuf("sdbc", [128, 16], F32)

    s_x = [p.stream("l7x0"), p.stream("l7x1")]
    s_w = [p.stream("l7w0"), p.stream("l7w1")]
    s_o = [p.stream("l7o0"), p.stream("l7o1")]
    s_m = p.stream("l7m")
    s_ln = p.stream("l7ln")

    m1 = s_m.dma("gpsimd", rwt[:, :].rearrange("p (k e) -> p k e", k=16),
                 routerT.rearrange("(k p) e -> p k e", p=128))
    m2 = s_m.dma("gpsimd", selb[:], sel[:, :])
    m3 = s_m.dma("gpsimd", idf7[:], c.dram["ident_f32"][:, :])
    m4 = s_m.dma("gpsimd", sgbc[:], sgb[:, :])
    m5 = s_m.dma("gpsimd", subc[:], sub[:, :])
    m6 = s_m.dma("gpsimd", sdbc[:], sdb[:, :])

    xs = slots[0:2]
    sqs = slots[2:4]
    ldk = {}

    def get_x3(k):
        sl = k % 2
        p.wait("gpsimd", ldk.get(k - 2))
        t = s_x[sl].dma("gpsimd", xs[sl][:], xres3[k * 128:(k + 1) * 128, :])
        ldk[k] = t
        return xs[sl][:], [t]

    mm1, ev1 = colstats(c, NKD, get_x3, bS, sqs, True, s1row, s2row)
    tmu = ln_mu(c, s1row, row_bf, D, ev1)
    b1 = bcast_rows(c, [(row_bf[:], mub)], bB, tmu)
    tr = ln_r(c, s1row, s2row, row_bf, D, b1)
    b2 = bcast_rows(c, [(row_bf[:], rb)], bB, tr, b1)
    lna = {}
    for k in range(NKD):
        sl = k % 2
        p.wait("gpsimd", lna.get(k - 2, mm1))
        ld = s_x[sl].dma("gpsimd", xs[sl][:], xres3[k * 128:(k + 1) * 128, :])
        p.waits("vector", [ld, b2])
        p.op("vector", lambda h, sl=sl: h.tensor_tensor(
            sqs[sl][:], xs[sl][:], mub[:], AluOpType.subtract))
        p.op("vector", lambda h, sl=sl: h.tensor_tensor(
            sqs[sl][:], sqs[sl][:], rb[:], AluOpType.mult))
        vt = p.milestone("vector")
        p.wait("sync", vt)
        lna[k] = s_o[sl].dma("sync", lnxscr[k * 128:(k + 1) * 128, :], sqs[sl][:])
    lnx_st = [lna[NKD - 2], lna[NKD - 1]]

    lnb = wbuf[0]
    lnld = None
    cps = []
    for tm in range(16):
        p.waits("gpsimd", lnx_st)
        p.wait("gpsimd", cps[-1] if cps else None)
        lt = s_ln.dma("gpsimd", lnb[:, 0:2048].rearrange("p (k t) -> p k t", k=16),
                      lnxscr[:, tm * 128:(tm + 1) * 128].rearrange("(k p) t -> p k t", p=128))
        p.waits("tensor", [lt, m1])
        pb = bB
        for k in range(NKD):
            p.op("tensor", lambda h, tm=tm, k=k, pb=pb: h.matmul(
                pb[:, 0:8], lnb[:, k * 128:(k + 1) * 128],
                rwt[:, k * 8:(k + 1) * 8], start=(k == 0), stop=(k == NKD - 1)))
        mmt = p.milestone("tensor")
        p.wait("vector", mmt)
        p.op("vector", lambda h, tm=tm, pb=pb: h.tensor_copy(
            logit[:, tm * 8:(tm + 1) * 8], pb[:, 0:8]))
        cp = p.milestone("vector")
        cps.append(cp)
        p.wait("tensor", cp)
    negb = c.sbuf("l7negb", [128, 128], F32)
    p.op("gpsimd", lambda h: h.memset(negb[:], -1e9))
    nb_tok = p.milestone("gpsimd")
    NT16 = 16
    d1a = vtmp8a
    l2a = vtmp8b
    p.wait("vector", nb_tok)
    dts = []
    for tm in range(NT16):
        lg = logit[:, tm * 8:(tm + 1) * 8]
        p.op("vector", lambda h, tm=tm, lg=lg: h.tensor_reduce(
            v16a[:, tm:tm + 1], lg, axis=AX.X, op=AluOpType.max))
        p.op("vector", lambda h, tm=tm: h.tensor_copy(
            v16p[:, tm:tm + 1], v16a[:, tm:tm + 1]))
    for tm in range(NT16):
        dts.append(p.op("vector", lambda h, tm=tm: h.tensor_scalar(
            d1a[:, tm * 8:(tm + 1) * 8], logit[:, tm * 8:(tm + 1) * 8],
            v16a[:, tm:tm + 1], None, AluOpType.subtract), inc=1))
    p.waits("scalar", dts)
    t_s1 = p.op("scalar", lambda h: h.activation(sg1[:], d1a[:], AF.Sign), inc=1)
    t_px = p.op("scalar", lambda h: h.activation(px[:], d1a[:], AF.Exp), inc=1)
    p.wait("vector", t_s1)
    p.op("vector", lambda h: h.tensor_tensor(l2a[:], sg1[:], negb[:], AluOpType.mult))
    p.op("vector", lambda h: h.tensor_tensor(l2a[:], l2a[:], negb[:], AluOpType.add))
    p.op("vector", lambda h: h.tensor_tensor(l2a[:], l2a[:], logit[:, :], AluOpType.add))
    for tm in range(NT16):
        p.op("vector", lambda h, tm=tm: h.tensor_reduce(
            v16b[:, tm:tm + 1], l2a[:, tm * 8:(tm + 1) * 8], axis=AX.X,
            op=AluOpType.max))
        p.op("vector", lambda h, tm=tm: h.tensor_copy(
            v16p[:, tm:tm + 1], v16b[:, tm:tm + 1]))
    d2ts = []
    for tm in range(NT16):
        d2ts.append(p.op("vector", lambda h, tm=tm: h.tensor_scalar(
            l2a[:, tm * 8:(tm + 1) * 8], l2a[:, tm * 8:(tm + 1) * 8],
            v16b[:, tm:tm + 1], None, AluOpType.subtract), inc=1))
    t_mm = p.op("vector", lambda h: h.tensor_tensor(
        v16b[:], v16b[:], v16a[:], AluOpType.subtract), inc=1)
    p.waits("scalar", d2ts)
    p.wait("scalar", t_mm)
    t_s2 = p.op("scalar", lambda h: h.activation(sg2[:], l2a[:], AF.Sign), inc=1)
    p.op("scalar", lambda h: h.activation(v16b[:], v16b[:], AF.Exp))
    p.op("scalar", lambda h: h.add(v16b[:], v16b[:], 1.0))
    p.op("scalar", lambda h: h.activation(v16b[:], v16b[:], AF.Ln))
    t_rd = p.op("scalar", lambda h: h.activation(
        v16a[:], v16b[:], AF.Exp, scale=-1.0), inc=1)
    p.waits("vector", [t_s2, t_px])
    p.op("vector", lambda h: h.tensor_tensor(comb[:, :], sg1[:], sg2[:], AluOpType.add))
    p.op("vector", lambda h: h.tensor_scalar(
        comb[:, :], comb[:, :], 2.0, None, AluOpType.add))
    t_ip = p.op("vector", lambda h: h.tensor_tensor(
        comb[:, :], comb[:, :], px[:], AluOpType.mult), inc=1)
    p.waits("scalar", [t_ip, t_rd])
    tp = None
    for tm in range(NT16):
        tp = p.op("scalar", lambda h, tm=tm: h.activation(
            comb[:, tm * 8:(tm + 1) * 8], comb[:, tm * 8:(tm + 1) * 8],
            AF.Identity, scale=v16a[:, tm:tm + 1]), inc=1)
    if getattr(c, "dbg", False):
        p.wait("sync", tp)
        p.stream("dbgc").dma("sync", c.dram["dbg_comb"], comb[:, :])
        p.stream("dbgl").dma("sync", c.dram["dbg_logit"], logit[:, :])
        p.stream("dbgv").dma("sync", c.dram["dbg_v16"], v16a[:, :])
        p.stream("dbgs").dma("sync", c.dram["dbg_sg1"], sg2[:, :])
    p.waits("tensor", [tp, m3])
    for tm in range(16):
        p.op("tensor", lambda h, tm=tm: h.transpose(
            bM[tm % 2][0:8, 0:128], comb[:, tm * 8:(tm + 1) * 8], idf7[:]))
        tt = p.milestone("tensor")
        p.wait("vector", tt)
        p.op("vector", lambda h, tm=tm: h.tensor_copy(
            combTb[0:8, tm * 128:(tm + 1) * 128], bM[tm % 2][0:8, 0:128]))
        cv = p.milestone("vector")
        p.wait("tensor", cv)
    cbt = cv

    lnx_half = lnq[0]
    pbf = [None, None]
    wld = [None, None]
    acc_tok = None
    accbf_tok = None
    close_done = None
    xo_st = {}
    for half in range(2):
        toff = half * 1024
        p.wait("gpsimd", acc_tok)
        lnh = s_ln.dma("gpsimd",
                       lnx_half[:, :].rearrange("p (k t) -> p k t", k=16),
                       lnxscr[:, toff:toff + 1024].rearrange("(k p) t -> p k t", p=128))
        p.waits("tensor", [cbt, m2])
        cebt = None
        for e in range(8):
            for g in range(2):
                cs = slice(toff + g * 512, toff + (g + 1) * 512)
                p.wait("tensor", accbf_tok)
                p.op("tensor", lambda h, e=e, cs=cs: h.matmul(
                    bB[:], selb[:, e * 128:(e + 1) * 128], combTb[0:8, cs],
                    start=True, stop=True))
                mt = p.milestone("tensor")
                p.wait("scalar", mt)
                p.op("scalar", lambda h, e=e, g=g: h.copy(
                    ceb[:, e * 1024 + g * 512:e * 1024 + (g + 1) * 512], bB[:]))
                cebt = p.milestone("scalar")
                p.wait("tensor", cebt)
        p.wait("gpsimd", close_done)
        zz = p.op("gpsimd", lambda h: h.memset(acc[:], 0.0), inc=1)
        dn_last = None
        for e in getattr(c, "dbg_elist", range(10)):
            sp = e - 8
            nim = 11
            p.wait("scalar", dn_last)
            p.wait("vector", dn_last)
            for im in range(nim):
                for which in range(2):
                    sl = (im * 2 + which) % 2
                    if e < 8:
                        wsrc = (gate_w if which == 0 else up_w)[e][:, im * 128:(im + 1) * 128]
                    else:
                        imo = sp * 11 + im
                        wsrc = (sgw if which == 0 else suw)[:, imo * 128:(imo + 1) * 128]
                    p.wait("gpsimd", wld[sl])
                    wt = s_w[sl].dma("gpsimd",
                                     wbuf[sl][:, 0:2048].rearrange("p (k m) -> p k m", k=16),
                                     wsrc.rearrange("(k p) m -> p k m", p=128))
                    p.waits("tensor", [wt, lnh])
                    pb = bM[which]
                    for tg in range(2):
                        p.wait("tensor", pbf[which])
                        for k in range(NKD):
                            p.op("tensor", lambda h, k=k, sl=sl, tg=tg, pb=pb: h.matmul(
                                pb[:], wbuf[sl][:, k * 128:(k + 1) * 128],
                                lnx_half[:, k * 1024 + tg * 512: k * 1024 + (tg + 1) * 512],
                                start=(k == 0), stop=(k == NKD - 1)))
                        mmt = p.milestone("tensor")
                        wld[sl] = mmt
                        ics = slice(im * 1024 + tg * 512, im * 1024 + (tg + 1) * 512)
                        hcs = slice(tg * 512, (tg + 1) * 512)
                        if which == 0:
                            p.waits("scalar", [mmt, m4, pbf[0], pbf[1]])
                            if e < 8:
                                p.op("scalar", lambda h, pb=pb, hcs=hcs: h.activation(
                                    h1t[:, hcs], pb[:], AF.Silu))
                            else:
                                imo = sp * 11 + im
                                p.op("scalar", lambda h, pb=pb, hcs=hcs, imo=imo:
                                     h.activation(h1t[:, hcs], pb[:], AF.Silu,
                                                  bias=sgbc[:, imo:imo + 1]))
                            pbf[0] = p.milestone("scalar")
                        else:
                            p.waits("vector", [mmt, pbf[0], m5, cebt])
                            if e < 8:
                                p.op("vector", lambda h, pb=pb, ics=ics, hcs=hcs: h.tensor_tensor(
                                    inter[:, ics], h1t[:, hcs], pb[:], AluOpType.mult))
                                p.op("vector", lambda h, e=e, ics=ics, hcs=hcs:
                                     h.tensor_tensor(
                                         inter[:, ics], inter[:, ics],
                                         ceb[:, e * 1024 + hcs.start:
                                             e * 1024 + hcs.stop], AluOpType.mult))
                            else:
                                imo = sp * 11 + im
                                p.op("vector", lambda h, pb=pb, ics=ics, imo=imo:
                                     h.tensor_scalar(
                                         inter[:, ics], pb[:], subc[:, imo:imo + 1],
                                         None, AluOpType.add))
                                p.op("vector", lambda h, ics=ics, hcs=hcs: h.tensor_tensor(
                                    inter[:, ics], inter[:, ics], h1t[:, hcs],
                                    AluOpType.mult))
                            pbf[1] = p.milestone("vector")
            inter_tok = pbf[1]
            inter_tok0 = pbf[0]
            if getattr(c, "dbg", False) and half == 0 and e == 0:
                p.waits("sync", [inter_tok, inter_tok0])
                p.stream("dbgi").dma("sync", c.dram["dbg_inter"], inter[:, :])
            for mt in range(16):
                sl = mt % 2
                nk = 11
                if e < 8:
                    wsrc = down_w[e][:, mt * 128:(mt + 1) * 128]
                else:
                    wsrc = sdw[sp * 1408:(sp + 1) * 1408, mt * 128:(mt + 1) * 128]
                p.wait("gpsimd", wld[sl])
                wt = s_w[sl].dma("gpsimd",
                                 wbuf[sl][:, 0:nk * 128].rearrange("p (k m) -> p k m", k=nk),
                                 wsrc.rearrange("(k p) m -> p k m", p=128))
                p.waits("tensor", [wt, inter_tok, inter_tok0, zz])
                for tg in range(2):
                    pb = bM[tg]
                    p.wait("tensor", pbf[tg])
                    for k in range(nk):
                        p.op("tensor", lambda h, k=k, sl=sl, pb=pb, tg=tg, nk=nk:
                             h.matmul(
                                 pb[:], wbuf[sl][:, k * 128:(k + 1) * 128],
                                 inter[:, k * 1024 + tg * 512: k * 1024 + (tg + 1) * 512],
                                 start=(k == 0), stop=(k == nk - 1)))
                    mmt = p.milestone("tensor")
                    wld[sl] = mmt
                    dn_last = mmt
                    acs = slice(mt * 1024 + tg * 512, mt * 1024 + (tg + 1) * 512)
                    p.waits("vector", [mmt, zz, m6, accbf_tok])
                    if e != 8:
                        p.op("vector", lambda h, pb=pb, acs=acs: h.tensor_tensor(
                            acc[:, acs], acc[:, acs], pb[:], AluOpType.add))
                    else:
                        p.op("vector", lambda h, pb=pb, acs=acs, mt=mt: h.tensor_scalar(
                            h1t[:, 0:512], pb[:], sdbc[:, mt:mt + 1], None, AluOpType.add))
                        p.op("vector", lambda h, acs=acs: h.tensor_tensor(
                            acc[:, acs], acc[:, acs], h1t[:, 0:512], AluOpType.add))
                    pbf[tg] = p.milestone("vector")
        acc_done = pbf[1]
        acc_done0 = pbf[0]
        accbf_tok = acc_done
        acc_tok = acc_done
        if getattr(c, "dbg", False) and half == 0:
            p.waits("sync", [acc_done, acc_done0])
            p.stream("dbga").dma("sync", c.dram["dbg_acc"], acc[:, :])
        sq_tok = None
        s1h = s1row[0:1, 0:1024]
        s2h = s2row[0:1, 0:1024]
        s1t = [bS[0][0:1, :], bS[0][64:65, :]]
        s2t = [bS[2][0:1, :], bS[2][64:65, :]]
        cast_tok = {}
        for kk in range(16):
            sl = kk % 2
            p.waits("vector", [acc_done, acc_done0])
            p.wait("vector", sq_tok)
            p.op("vector", lambda h, kk=kk, sl=sl: h.tensor_copy(
                xs[sl][:, 0:1024], acc[:, kk * 1024:(kk + 1) * 1024]))
            p.op("vector", lambda h, kk=kk, sl=sl: h.tensor_tensor(
                sqs[sl][:, 0:1024], xs[sl][:, 0:1024], xs[sl][:, 0:1024],
                AluOpType.mult))
            vsq = p.milestone("vector")
            p.wait("tensor", vsq)
            for g in range(2):
                cs = slice(g * 512, (g + 1) * 512)
                p.op("tensor", lambda h, sl=sl, kk=kk, g=g, cs=cs: h.matmul(
                    s1t[g], c.ones_m1[:], xs[sl][:, cs],
                    start=(kk == 0), stop=(kk == 15)))
                p.op("tensor", lambda h, kk=kk, g=g, sl=sl, cs=cs: h.matmul(
                    s2t[g], c.ones_m1[:], sqs[sl][:, cs],
                    start=(kk == 0), stop=(kk == 15)))
            sq_tok = p.milestone("tensor")
        p.wait("vector", sq_tok)
        for g in range(2):
            cs = slice(g * 512, (g + 1) * 512)
            p.op("vector", lambda h, g=g, cs=cs: h.tensor_copy(s1h[0:1, cs], s1t[g]))
            p.op("vector", lambda h, g=g, cs=cs: h.tensor_copy(s2h[0:1, cs], s2t[g]))
        evh = p.milestone("vector")
        tmu2 = ln_mu(c, s1h, row_bf[0:1, 0:1024], D, evh)
        bmu = bcast_rows2(c, row_bf[0:1, 0:1024], mub[:, 0:1024], bB, tmu2)
        trh = ln_r(c, s1h, s2h, row_bf[0:1, 0:1024], D, bmu)
        brh = bcast_rows2(c, row_bf[0:1, 0:1024], rb[:, 0:1024], bB, trh, bmu)
        fo = c.sb.get("l7fo")
        if fo is None:
            fo = c.sbuf("l7fo", [128, 1024], F32)
        for kk in range(16):
            sl = kk % 2
            p.wait("gpsimd", xo_st.get((half, kk - 2), sq_tok))
            ldx = s_x[sl].dma("gpsimd", xs[sl][:, 0:1024],
                              xres3[kk * 128:(kk + 1) * 128, toff:toff + 1024])
            p.waits("vector", [ldx, brh])
            p.op("vector", lambda h, kk=kk, sl=sl: h.tensor_tensor(
                sqs[sl][:, 0:1024], acc[:, kk * 1024:(kk + 1) * 1024],
                mub[:, 0:1024], AluOpType.subtract))
            p.op("vector", lambda h, sl=sl: h.tensor_tensor(
                sqs[sl][:, 0:1024], sqs[sl][:, 0:1024], rb[:, 0:1024], AluOpType.mult))
            p.wait("vector", xo_st.get((half, kk - 1)))
            p.op("vector", lambda h, sl=sl, fo=fo: h.tensor_tensor(
                fo[:], sqs[sl][:, 0:1024], xs[sl][:, 0:1024], AluOpType.add))
            vt = p.milestone("vector")
            close_done = vt
            p.wait("sync", vt)
            xo_st[(half, kk)] = s_o[sl].dma(
                "sync", xout[kk * 128:(kk + 1) * 128, toff:toff + 1024], fo[:])


def bcast_rows2(c, row, out, bank, row_tok, free_tok=None):
    p = c.p
    W = row.shape[-1] if hasattr(row, 'shape') else 1024
    p.wait("tensor", row_tok)
    p.wait("tensor", free_tok)
    ev = None
    for g in range(2):
        cs = slice(g * 512, (g + 1) * 512)
        p.op("tensor", lambda h, cs=cs: h.matmul(
            bank[:], c.ones_k1[:], row[0:1, cs], start=True, stop=True))
        mt = p.milestone("tensor")
        p.wait("scalar", mt)
        p.op("scalar", lambda h, cs=cs: h.copy(out[:, cs], bank[:]))
        ev = p.milestone("scalar")
        p.wait("tensor", ev)
    return ev


def build_L7(debug=False, elist=None):
    nc = make_nc()
    with contextlib.ExitStack() as es:
        c = Ctx(nc, es)
        c.dbg = debug
        if elist is not None:
            c.dbg_elist = elist
        c.dram_in("xres3", [D, T], BF16)
        c.dram_in("routerT", [2048, 8], F32)
        c.dram_in("moe_gate", [8, 2048, 1408], F32)
        c.dram_in("moe_up", [8, 2048, 1408], F32)
        c.dram_in("moe_down", [8, 1408, 2048], F32)
        c.dram_in("sgw", [2048, 2816], F32)
        c.dram_in("suw", [2048, 2816], F32)
        c.dram_in("sdw", [2816, 2048], F32)
        c.dram_in("sgb_col", [128, 22], F32)
        c.dram_in("sub_col", [128, 22], F32)
        c.dram_in("sdb_col", [128, 16], F32)
        c.dram_in("sel8", [8, 1024], BF16)
        c.dram_in("ident_f32", [128, 128], F32)
        (c.dram_out if debug else c.dram_tmp)("lnx7scr", [D, T], BF16)
        if debug:
            c.dram_out("dbg_comb", [128, 128], F32)
            c.dram_out("dbg_logit", [128, 128], F32)
            c.dram_out("dbg_v16", [128, 16], F32)
            c.dram_out("dbg_sg1", [128, 128], F32)
            c.dram_out("dbg_acc", [128, 16 * 1024], F32)
            c.dram_out("dbg_inter", [128, 11 * 1024], BF16)
        c.dram_out("xout", [D, T], F32)
        emit_consts(c)
        emit_L7(c)
        finish(c)
        c.p.finalize()
    return nc


def build_L2():
    nc = make_nc()
    with contextlib.ExitStack() as es:
        c = Ctx(nc, es)
        c.dram_in("xbcscr", [5120, T + 3], BF16)
        c.dram_in("zscr", [T, 4096], BF16)
        c.dram_in("convw_sb", [128, 160], F32)
        c.dram_in("convb_sb", [128, 40], F32)
        c.dram_out("xbcp", [5120, T], BF16)
        c.dram_out("zsil", [T, 4096], BF16)
        emit_consts(c)
        emit_L2(c)
        finish(c)
        c.p.finalize()
    return nc


def build_L3():
    nc = make_nc()
    with contextlib.ExitStack() as es:
        c = Ctx(nc, es)
        c.dram_in("xbcp", [5120, T], BF16)
        c.dram_in("zsil", [T, 4096], BF16)
        c.dram_in("dtscr", [T, 64], F32)
        c.dram_in("tmask_f", [128, 128], F32)
        c.dram_in("masklT", [128, 128], F32)
        c.dram_in("ident_bf", [128, 128], BF16)
        c.dram_in("ident_f32", [128, 128], F32)
        c.dram_in("hrows", [1, 192], F32)
        c.dram_out("vscr", [T, 4096], BF16)
        c.dram_out("rstdscr", [T], F32)
        emit_consts(c)
        c.ones_m1f = c.sbuf("ones_m1f", [128, 1], F32)
        c.ones_k1f = c.sbuf("ones_k1f", [1, 128], F32)
        c.p.op("gpsimd", lambda h: h.memset(c.ones_m1f[:], 1.0))
        c.p.op("gpsimd", lambda h: h.memset(c.ones_k1f[:], 1.0))
        c.const_tok = c.p.milestone("gpsimd")
        emit_L3(c)
        finish(c)
        c.p.finalize()
    return nc


def make_nc():
    return bass.Bass("TRN2", target_bir_lowering=False,
                     detect_race_conditions=False)


def build_L1():
    nc = make_nc()
    with contextlib.ExitStack() as es:
        c = Ctx(nc, es)
        c.dram_in("xin", [D, T], F32)
        c.dram_in("w_in", [2048, 9280], F32)
        c.dram_in("ln1w_col", [128, 16], F32)
        c.dram_out("xres_o", [D, T], BF16)
        c.dram_out("hscr", [D, T], BF16)
        c.dram_out("zscr", [T, 4096], BF16)
        c.dram_out("xbcscr", [5120, T + 3], BF16)
        c.dram_out("dtscr", [T, 64], F32)
        emit_consts(c)
        emit_L1(c)
        finish(c)
        c.p.finalize()
    return nc


def build_fused(upto=7, phases=None, use_barrier=True, dbg_outs=False):
    nc = make_nc()
    with contextlib.ExitStack() as es:
        c = Ctx(nc, es)
        c.dram_in("xin", [D, T], F32)
        c.dram_in("w_in", [2048, 9280], F32)
        c.dram_in("ln1w_col", [128, 16], F32)
        c.dram_in("convw_sb", [128, 160], F32)
        c.dram_in("convb_sb", [128, 40], F32)
        c.dram_in("tmask_f", [128, 128], F32)
        c.dram_in("masklT", [128, 128], F32)
        c.dram_in("ident_bf", [128, 128], BF16)
        c.dram_in("ident_f32", [128, 128], F32)
        c.dram_in("hrows", [1, 192], F32)
        c.dram_in("w_out", [4096, 2048], F32)
        c.dram_in("normw_col", [128, 32], F32)
        c.dram_in("ln2w_col", [128, 16], F32)
        c.dram_in("gate_w", [2048, 5632], F32)
        c.dram_in("up_w", [2048, 5632], F32)
        c.dram_in("down_w", [5632, 2048], F32)
        c.dram_in("w_gate", [2048, 8192], I8)
        c.dram_in("w_up", [2048, 8192], I8)
        c.dram_in("w_down", [8192, 2048], I8)
        c.dram_in("sgate_col", [128, 64], F32)
        c.dram_in("sup_col", [128, 64], F32)
        c.dram_in("sdown_col", [128, 16], F32)
        c.dram_in("wqT", [128, 32768], F32)
        c.dram_in("wkT", [128, 32768], F32)
        c.dram_in("wvT", [128, 32768], F32)
        c.dram_in("woT", [2048, 2048], F32)
        c.dram_in("causmask", [128, 128], F32)
        c.dram_in("routerT", [2048, 8], F32)
        c.dram_in("moe_gate", [8, 2048, 1408], F32)
        c.dram_in("moe_up", [8, 2048, 1408], F32)
        c.dram_in("moe_down", [8, 1408, 2048], F32)
        c.dram_in("sgw", [2048, 2816], F32)
        c.dram_in("suw", [2048, 2816], F32)
        c.dram_in("sdw", [2816, 2048], F32)
        c.dram_in("sgb_col", [128, 22], F32)
        c.dram_in("sub_col", [128, 22], F32)
        c.dram_in("sdb_col", [128, 16], F32)
        c.dram_in("sel8", [8, 1024], BF16)
        mk = c.dram_out if dbg_outs else c.dram_tmp
        mk("xres_o", [D, T], BF16)
        mk("hscr", [D, T], BF16)
        mk("zscr", [T, 4096], BF16)
        mk("xbcscr", [5120, T + 3], BF16)
        mk("dtscr", [T, 64], F32)
        mk("xbcp", [5120, T], BF16)
        mk("zsil", [T, 4096], BF16)
        mk("vscr", [T, 4096], BF16)
        mk("rstdscr", [T], F32)
        mk("h2scr", [D, T], BF16)
        mk("xres1", [D, T], BF16)
        mk("f2scr", [D, T], BF16)
        mk("xres2", [D, T], BF16)
        mk("lnxscr", [D, T], BF16)
        mk("qscr", [2048, T], BF16)
        mk("kscr", [2048, T], BF16)
        mk("vscr2", [T, 2048], BF16)
        mk("zascr", [2048, T], BF16)
        mk("aoscr", [D, T], BF16)
        mk("xres3", [D, T], BF16)
        mk("lnx7scr", [D, T], BF16)
        c.dram_out("xout", [D, T], F32)
        emit_consts(c)
        c.ones_m1f = c.sbuf("ones_m1f", [128, 1], F32)
        c.ones_k1f = c.sbuf("ones_k1f", [1, 128], F32)
        c.p.op("gpsimd", lambda h: h.memset(c.ones_m1f[:], 1.0))
        c.p.op("gpsimd", lambda h: h.memset(c.ones_k1f[:], 1.0))
        c.const_tok = c.p.milestone("gpsimd")
        all_emits = [emit_L1, emit_L2, emit_L3, emit_L4, emit_L5, emit_L6,
                     emit_L7]
        emits = ([all_emits[i] for i in phases] if phases is not None
                 else all_emits[:upto])
        for i, emit in enumerate(emits):
            with contextlib.ExitStack() as pes:
                c.alloc_es = pes
                c.name_prefix = f"p{i + 1}_"
                emit(c)
            c.alloc_es = es
            if i < len(emits) - 1 and use_barrier:
                c.p.barrier(mode=use_barrier if isinstance(use_barrier, str)
                            else "full")
        finish(c)
        c.p.finalize()
    return nc


def _prep_shared(inputs):
    f32 = np.float32
    A = lambda x: np.ascontiguousarray(np.asarray(x, f32))
    d = {}
    d["w_in"] = A(inputs["L0_attn_w_in"])
    d["ln1w_col"] = A(np.asarray(inputs["L0_attn_ln1_w"], f32).reshape(16, 128).T)
    cw = np.asarray(inputs["L0_attn_conv_w"], f32)
    d["convw_sb"] = A(cw.reshape(40, 128, 4).transpose(1, 0, 2).reshape(128, 160))
    d["convb_sb"] = A(np.asarray(inputs["L0_attn_conv_b"], f32).reshape(40, 128).T)
    d["tmask_f"] = A(np.triu(np.ones((128, 128))))
    d["masklT"] = A(np.where(np.tril(np.ones((128, 128))) > 0, 0.0, -1e9))
    d["ident_bf"] = np.eye(128).astype(ml_dtypes.bfloat16)
    d["ident_f32"] = A(np.eye(128))
    d["w_gate"] = np.ascontiguousarray(np.asarray(inputs["L0_ffn_w_gate"], np.int8))
    d["w_up"] = np.ascontiguousarray(np.asarray(inputs["L0_ffn_w_up"], np.int8))
    d["w_down"] = np.ascontiguousarray(np.asarray(inputs["L0_ffn_w_down"], np.int8))
    d["sgate_col"] = A(np.asarray(inputs["L0_ffn_s_gate"], f32).reshape(64, 128).T)
    d["sup_col"] = A(np.asarray(inputs["L0_ffn_s_up"], f32).reshape(64, 128).T)
    d["sdown_col"] = A(np.asarray(inputs["L0_ffn_s_down"], f32).reshape(16, 128).T)
    def _qk_prep(w):
        wt = np.asarray(w, f32).transpose(0, 2, 1).reshape(16, 16, 128, 128)
        return A(wt.transpose(2, 0, 1, 3).reshape(128, 32768))

    def _v_prep(w):
        wt = np.asarray(w, f32).transpose(0, 2, 1).reshape(16, 16, 128, 128)
        wp = wt.transpose(2, 1, 0, 3).reshape(128, 16, 2, 8, 128)
        return A(wp.transpose(0, 2, 1, 3, 4).reshape(128, 32768))

    d["wqT"] = _qk_prep(inputs["L1_attn_W_Q"])
    d["wkT"] = _qk_prep(inputs["L1_attn_W_K"])
    d["wvT"] = _v_prep(inputs["L1_attn_W_V"])
    d["woT"] = A(np.asarray(inputs["L1_attn_W_O"], f32).transpose(0, 2, 1).reshape(2048, 2048))
    ii, jj = np.meshgrid(np.arange(128), np.arange(128), indexing="ij")
    d["causmask"] = A(np.where(ii <= jj, 0.0, -1e9))
    d["routerT"] = A(np.asarray(inputs["L1_ffn_router_w"], f32).T)
    d["moe_gate"] = A(inputs["L1_ffn_gate_w"])
    d["moe_up"] = A(inputs["L1_ffn_up_w"])
    d["moe_down"] = A(inputs["L1_ffn_down_w"])
    d["sgw"] = A(inputs["L1_ffn_s_gate_w"])
    d["suw"] = A(inputs["L1_ffn_s_up_w"])
    d["sdw"] = A(inputs["L1_ffn_s_down_w"])
    d["sgb_col"] = A(np.asarray(inputs["L1_ffn_s_gate_b"], f32).reshape(22, 128).T)
    d["sub_col"] = A(np.asarray(inputs["L1_ffn_s_up_b"], f32).reshape(22, 128).T)
    d["sdb_col"] = A(np.asarray(inputs["L1_ffn_s_down_b"], f32).reshape(16, 128).T)
    selm = np.zeros((8, 1024), np.float32)
    for e in range(8):
        selm[e, e * 128:(e + 1) * 128] = 1.0
    d["sel8"] = selm.astype(ml_dtypes.bfloat16)
    d["normw_col"] = A(np.asarray(inputs["L0_attn_norm_w"], f32).reshape(32, 128).T)
    d["ln2w_col"] = A(np.asarray(inputs["L0_attn_ln2_w"], f32).reshape(16, 128).T)
    d["w_out"] = A(inputs["L0_attn_w_out"])
    d["gate_w"] = A(inputs["L0_attn_gate_w"])
    d["up_w"] = A(inputs["L0_attn_up_w"])
    d["down_w"] = A(inputs["L0_attn_down_w"])
    d["hrows"] = A(np.concatenate([np.asarray(inputs["L0_attn_dt_bias"], f32),
                                   np.asarray(inputs["L0_attn_A_log"], f32),
                                   np.asarray(inputs["L0_attn_D"], f32)])[None, :])
    return d


def _launch(nc, shared, per_core, out_names):
    in_maps = []
    for cid in range(8):
        m = dict(shared)
        for k, v in per_core.items():
            m[k] = v[cid]
        in_maps.append(m)
    res = run_bass_kernel_spmd(nc, in_maps, core_ids=list(range(8)))
    return {k: [res.results[c][k] for c in range(8)] for k in out_names}


def kernel(**inputs):
    x = np.asarray(inputs["L0_attn_hidden_states"])
    shared = _prep_shared(inputs)
    xin = [np.ascontiguousarray(x[cid].T) for cid in range(8)]
    r = _launch(build_fused(), shared, {"xin": xin}, ["xout"])
    out = np.stack([np.ascontiguousarray(r["xout"][cid].T) for cid in range(8)])
    return out.astype(np.float32)
